# version 04
# baseline (speedup 1.0000x reference)
import math
from contextlib import ExitStack

import numpy as np
import ml_dtypes

import concourse.bass as bass
import concourse.mybir as mybir
from concourse.bass_utils import run_bass_kernel_spmd

F32 = mybir.dt.float32
BF16 = mybir.dt.bfloat16
I32 = mybir.dt.int32
U32 = mybir.dt.uint32
AF = mybir.ActivationFunctionType
ALU = mybir.AluOpType
AX = mybir.AxisListType
BF16NP = ml_dtypes.bfloat16

D = 1024
NT = 4096
NCHUNK = 8
NCORE = 8

ENGINES = ("sync", "gpsimd", "tensor", "vector", "scalar")


class Sem:
    def __init__(self, prog, name):
        self.h = prog.es.enter_context(prog.nc.semaphore(name))
        self.n = 0

    def inc(self, k=1):
        self.n += k
        return self.n


AUTO_SYNC_ENGINES = ("vector", "scalar", "gpsimd")


class Prog:

    def __init__(self, name, fused=False, final_outputs=()):
        self.name = name
        self.fused = fused
        self.final_outputs = set(final_outputs)
        self.nc = bass.Bass("TRN2", target_bir_lowering=False)
        self.es = ExitStack()
        self.phase_es = None
        self.phase_tag = ""
        self.streams = {e: [] for e in ENGINES}
        self.in_names = []
        self.out_names = []
        self._final_sem = None
        self._final_val = 0
        self.esem = {}
        self.registry = {}
        self.renames = [{}]
        self.all_sems = []

    def push(self, rename):
        self.renames.append(rename or {})

    def pop(self):
        self.renames.pop()

    def begin_phase(self, tag):
        self.phase_tag = tag
        if self.fused:
            self.phase_es = ExitStack()

    def end_phase(self):
        if self.fused:
            self.barrier()
            self.phase_es.close()
            self.phase_es = None
        self.phase_tag = ""

    def barrier(self):
        for eng in ENGINES:
            for s in self.all_sems:
                self.wait(eng, s, s.n)

    def din(self, name, shape, dtype):
        name = self.renames[-1].get(name, name)
        if name in self.registry:
            return self.registry[name]
        self.in_names.append(name)
        ap = self.nc.dram_tensor(name, list(shape), dtype, kind="ExternalInput").ap()
        self.registry[name] = ap
        return ap

    def dout(self, name, shape, dtype):
        name = self.renames[-1].get(name, name)
        if self.fused and name not in self.final_outputs:
            ap = self.nc.dram_tensor(name, list(shape), dtype, kind="Internal").ap()
        else:
            self.out_names.append(name)
            ap = self.nc.dram_tensor(name, list(shape), dtype, kind="ExternalOutput").ap()
        self.registry[name] = ap
        return ap

    def dint(self, name, shape, dtype):
        name = self.renames[-1].get(name, name)
        if name in self.registry:
            return self.registry[name]
        ap = self.nc.dram_tensor(name, list(shape), dtype, kind="Internal").ap()
        self.registry[name] = ap
        return ap

    def const(self, name, arr):
        return self.nc.inline_tensor(np.ascontiguousarray(arr),
                                     name=self.phase_tag + name).ap()

    def sb(self, name, shape, dtype):
        es = self.phase_es if self.phase_es is not None else self.es
        return es.enter_context(
            self.nc.sbuf_tensor(self.phase_tag + name, list(shape), dtype))

    def ps(self, name, shape, dtype=F32):
        es = self.phase_es if self.phase_es is not None else self.es
        return es.enter_context(
            self.nc.psum_tensor(self.phase_tag + name, list(shape), dtype))

    def sem(self, name):
        if name in self.esem:
            return self.esem[name]
        s = Sem(self, name)
        self.esem[name] = s
        self.all_sems.append(s)
        return s

    def on(self, eng, fn):
        self.streams[eng].append(fn)

    def op(self, eng, fn, sem=None, inc=1):
        if eng in AUTO_SYNC_ENGINES:
            s = self.sem("auto_" + eng)
            assert sem is None or sem is s, f"{eng} ops must use the stream sem"
            prev = s.n
            if prev > 0:
                self.on(eng, lambda e, s=s, prev=prev: e.wait_ge(s.h, prev))
            v = s.inc(1)
            self.on(eng, lambda e, fn=fn, s=s: fn(e).then_inc(s.h, 1))
            return v
        if sem is None:
            self.on(eng, fn)
            return None
        v = sem.inc(inc)
        self.on(eng, lambda e, fn=fn, sem=sem, inc=inc: fn(e).then_inc(sem.h, inc))
        return v

    def stream_sem(self, eng):
        return self.sem("auto_" + eng)

    def wait(self, eng, sem, val=None):
        v = sem.n if val is None else val
        if v > 0:
            self.on(eng, lambda e, sem=sem, v=v: e.wait_ge(sem.h, v))
        return v

    def dma(self, eng, out, in_, sem, slow=False, gate=True):
        if gate and sem.n > 0:
            self.wait(eng, sem, sem.n)
        v = sem.inc(16)
        kw = {"allow_slow_non_contiguous": True} if slow else {}
        self.on(eng, lambda e, out=out, in_=in_, sem=sem, kw=kw:
                e.dma_start(out=out, in_=in_, **kw).then_inc(sem.h, 16))
        return v

    def finish(self, sem, val):
        self._final_sem, self._final_val = sem, val

    def build(self):
        with self.nc.Block() as block:
            for eng in ENGINES:
                fns = self.streams[eng]
                if not fns and eng != "sync":
                    continue

                def body(e, fns=fns, eng=eng):
                    for fn in fns:
                        fn(e)
                    if eng == "sync" and self._final_sem is not None:
                        e.wait_ge(self._final_sem.h, self._final_val)

                getattr(block, eng)(body)
        self.es.close()
        return self.nc


_PROG_CACHE = {}


def run_prog(builder, in_maps, n_launch_counter=None):
    key = builder.__name__
    if key not in _PROG_CACHE:
        prog = builder()
        prog.build()
        _PROG_CACHE[key] = prog
    prog = _PROG_CACHE[key]
    res = run_bass_kernel_spmd(prog.nc, in_maps, core_ids=list(range(NCORE)))
    return res.results



def build_ph0(p=None, rename=None):
    standalone = p is None
    if standalone:
        p = Prog('ph0')
    p.push(rename)
    p.begin_phase('p0_')
    T = 512
    NTILE = NT // T
    x_fm = p.din("x_fm", (D, NT), F32)
    cenT = p.din("cenT", (D, 8), F32)
    ln1c = p.din("ln1c", (D, 1), F32)
    o_xn = p.dout("xn_fm", (D, NT), F32)
    o_xh1 = p.dout("xh1_bf", (D, NT), BF16)
    o_xhat = p.dout("xhat_bf", (D, NT), BF16)
    o_w0 = p.dout("w0_fm", (8, NT), F32)
    o_r1 = p.dout("r1_row", (1, NT), F32)

    idb = p.const("c_idb", np.eye(128, dtype=BF16NP))
    idf = p.const("c_idf", np.eye(128, dtype=np.float32))

    xs = [p.sb(f"x{i}", (128, NCHUNK * T), F32) for i in range(2)]
    xn = p.sb("xn", (128, NCHUNK * T), F32)
    sq = p.sb("sq", (128, NCHUNK * T), F32)
    xh_tmp = p.sb("xh_tmp", (128, T), F32)
    ones_f = p.sb("ones_f", (128, 1), F32)
    cvals = p.sb("cvals", (128, 2), F32)
    onesr_f = p.sb("onesr_f", (1, 128), F32)
    cen_sb = p.sb("cen_sb", (128, NCHUNK * 8), F32)
    ln1_sb = p.sb("ln1_sb", (128, NCHUNK), F32)
    id_bf = p.sb("id_bf", (128, 128), BF16)
    id_f = p.sb("id_f", (128, 128), F32)
    stat = p.sb("stat", (1, T * 8), F32)
    sc_sb = p.sb("sc_sb", (8, T), F32)
    sc_tok = p.sb("sc_tok", (128, 8 * 4), F32)
    mx8 = p.sb("mx8", (128, 8 * 4), F32)
    oh_tok = p.sb("oh_tok", (128, 8 * 4), F32)
    alpha = p.sb("alpha", (128, 4), F32)
    w0_sb = p.sb("w0_sb", (8, NT), F32)
    r1row = p.sb("r1row", (1, NT), F32)
    out_bf = p.sb("out_bf", (128, NCHUNK * T), BF16)
    out2_bf = p.sb("out2_bf", (128, NCHUNK * T), BF16)

    ps_mu = p.ps("ps_mu", (1, T))
    ps_sq = p.ps("ps_sq", (1, T))
    ps_b1 = p.ps("ps_b1", (128, T))
    ps_b2 = p.ps("ps_b2", (128, T))
    ps_sc = p.ps("ps_sc", (8, T))
    ps_tr = p.ps("ps_tr", (128, 128))
    ps_mu2 = p.ps("ps_mu2", (1, T))

    ld = p.sem("ld")
    lx = [p.sem("lx0"), p.sem("lx1")]
    pe = p.sem("pe")
    dve = p.stream_sem("vector")
    act = p.stream_sem("scalar")
    st = p.sem("st")
    sxn = p.sem("sxn")
    sxh = p.sem("sxh")
    sxt = p.sem("sxt")
    gp = p.stream_sem("gpsimd")

    p.dma("sync", cen_sb[:, 0:8 * NCHUNK].rearrange("p (c e) -> p c e", e=8),
          cenT.rearrange("(c p) e -> p c e", p=128), ld)
    p.dma("sync", ln1_sb[:], ln1c.rearrange("(c p) one -> p (c one)", p=128), ld, slow=True)
    p.dma("sync", id_bf[:], idb, ld)
    p.dma("sync", id_f[:], idf, ld)
    p.op("gpsimd", lambda e: e.memset(ones_f[:], 1.0))
    p.op("gpsimd", lambda e: e.memset(cvals[:, 0:1], 1e-5))
    p.op("gpsimd", lambda e: e.memset(cvals[:, 1:2], 1e-6))
    p.op("gpsimd", lambda e: e.memset(onesr_f[:], 1.0), sem=gp)
    v_pre = ld.n
    v_gp = gp.n

    eps_ln = 1e-5
    eps_rms = 1e-6
    p.wait("vector", ld, v_pre)
    p.wait("vector", gp, v_gp)

    compute_done = [0] * NTILE
    sq_guard = [0] * NTILE
    ps_tr_free = [0]

    for i in range(NTILE):
        xb = xs[i % 2]
        if i >= 2:
            p.wait("sync", dve, compute_done[i - 2])
        v_ld = p.dma("sync", xb[:].rearrange("p (c t) -> p c t", c=NCHUNK),
                     x_fm[:, i * T:(i + 1) * T].rearrange("(c p) t -> p c t", p=128),
                     lx[i % 2])

        p.wait("scalar", lx[i % 2], v_ld)
        if i > 0:
            p.wait("scalar", pe, sq_guard[i - 1])
        for c in range(NCHUNK):
            p.op("scalar", lambda e, c=c, xb=xb: e.activation(
                sq[:, c * T:(c + 1) * T], xb[:, c * T:(c + 1) * T], AF.Square), sem=act)
            v_sq = act.n
            p.wait("tensor", lx[i % 2], v_ld)
            p.wait("tensor", gp, v_gp)
            p.op("tensor", lambda e, c=c, xb=xb: e.matmul(
                ps_mu[:], ones_f[:], xb[:, c * T:(c + 1) * T],
                start=(c == 0), stop=(c == NCHUNK - 1)))
            p.wait("tensor", act, v_sq)
            p.op("tensor", lambda e, c=c: e.matmul(
                ps_sq[:], ones_f[:], sq[:, c * T:(c + 1) * T],
                start=(c == 0), stop=(c == NCHUNK - 1)),
                sem=pe, inc=1)
        v_stats = pe.n

        MU, RSTD, MR, R1, RSTD2, M2R2, TMP, TMP2 = range(8)

        def srow(k):
            return stat[:, k * T:(k + 1) * T]

        p.wait("vector", pe, v_stats)
        p.op("vector", lambda e: e.tensor_scalar_mul(srow(MU), ps_mu[:], 1.0 / D))
        p.op("vector", lambda e: e.tensor_scalar_mul(srow(TMP), ps_sq[:], 1.0 / D))
        p.op("vector", lambda e: e.tensor_mul(srow(TMP2), srow(MU), srow(MU)))
        p.op("vector", lambda e: e.tensor_sub(srow(TMP), srow(TMP), srow(TMP2)), sem=dve)
        v_var = dve.n
        p.wait("scalar", dve, v_var)
        p.wait("scalar", gp, v_gp)
        p.op("scalar", lambda e: e.activation(srow(RSTD), srow(TMP), AF.Ln, bias=cvals[0:1, 0:1]), sem=act)
        p.op("scalar", lambda e: e.activation(srow(RSTD), srow(RSTD), AF.Exp, scale=-0.5), sem=act)
        v_rstd = act.n
        p.wait("vector", act, v_rstd)
        p.op("vector", lambda e: e.tensor_mul(srow(MR), srow(MU), srow(RSTD)), sem=dve)
        v_mr = dve.n

        p.wait("tensor", dve, v_mr)
        p.op("tensor", lambda e: e.matmul(ps_b1[:], onesr_f[:], srow(RSTD), start=True, stop=True))
        p.op("tensor", lambda e: e.matmul(ps_b2[:], onesr_f[:], srow(MR), start=True, stop=True),
             sem=pe)
        v_bc = pe.n

        p.wait("vector", pe, v_bc)
        if i > 0:
            p.wait("vector", sxn, sxn.n)
        for c in range(NCHUNK):
            p.op("vector", lambda e, c=c, xb=xb: e.tensor_mul(
                xn[:, c * T:(c + 1) * T], xb[:, c * T:(c + 1) * T], ps_b1[:]))
            p.op("vector", lambda e, c=c: e.tensor_sub(
                xn[:, c * T:(c + 1) * T], xn[:, c * T:(c + 1) * T], ps_b2[:]), sem=dve)
        v_xn = dve.n

        p.wait("sync", dve, v_xn)
        p.dma("sync", o_xn[:, i * T:(i + 1) * T].rearrange("(c p) t -> p c t", p=128),
              xn[:].rearrange("p (c t) -> p c t", c=NCHUNK), sxn)

        p.wait("scalar", dve, v_xn)
        p.wait("tensor", dve, v_xn)
        for c in range(NCHUNK):
            p.op("scalar", lambda e, c=c: e.activation(
                sq[:, c * T:(c + 1) * T], xn[:, c * T:(c + 1) * T], AF.Square), sem=act)
            v_sq2 = act.n
            p.op("tensor", lambda e, c=c: e.matmul(
                ps_mu2[:], ones_f[:], xn[:, c * T:(c + 1) * T],
                start=(c == 0), stop=(c == NCHUNK - 1)))
            p.wait("tensor", act, v_sq2)
            p.op("tensor", lambda e, c=c: e.matmul(
                ps_sq[:], ones_f[:], sq[:, c * T:(c + 1) * T],
                start=(c == 0), stop=(c == NCHUNK - 1)), sem=pe)
        v_stats2 = pe.n
        sq_guard[i] = v_stats2

        p.wait("scalar", pe, v_stats2)
        p.op("scalar", lambda e: e.activation(srow(R1), ps_sq[:], AF.Ln,
                                              bias=cvals[0:1, 1:2], scale=1.0 / D), sem=act)
        p.op("scalar", lambda e: e.activation(srow(R1), srow(R1), AF.Exp, scale=-0.5), sem=act)
        v_r1 = act.n
        p.wait("vector", pe, v_stats2)
        p.op("vector", lambda e: e.tensor_scalar_mul(srow(MU), ps_mu2[:], 1.0 / D))
        p.op("vector", lambda e: e.tensor_scalar_mul(srow(TMP), ps_sq[:], 1.0 / D))
        p.op("vector", lambda e: e.tensor_mul(srow(TMP2), srow(MU), srow(MU)))
        p.op("vector", lambda e: e.tensor_sub(srow(TMP), srow(TMP), srow(TMP2)), sem=dve)
        v_var2 = dve.n
        p.wait("scalar", dve, v_var2)
        p.op("scalar", lambda e: e.activation(srow(RSTD2), srow(TMP), AF.Ln, bias=cvals[0:1, 0:1]), sem=act)
        p.op("scalar", lambda e: e.activation(srow(RSTD2), srow(RSTD2), AF.Exp, scale=-0.5),
             sem=act)
        v_rstd2 = act.n
        p.wait("vector", act, v_rstd2)
        p.op("vector", lambda e: e.tensor_mul(srow(M2R2), srow(MU), srow(RSTD2)), sem=dve)
        v_m2r2 = dve.n
        p.wait("vector", act, v_r1)
        p.op("vector", lambda e, i=i: e.tensor_copy(r1row[:, i * T:(i + 1) * T], srow(R1)),
             sem=dve)

        p.wait("tensor", act, v_r1)
        p.op("tensor", lambda e: e.matmul(ps_b1[:], onesr_f[:], srow(R1), start=True, stop=True),
             sem=pe)
        v_bcr1 = pe.n
        p.wait("vector", pe, v_bcr1)
        if i > 0:
            p.wait("vector", sxh, sxh.n)
        for c in range(NCHUNK):
            p.op("vector", lambda e, c=c: e.scalar_tensor_tensor(
                out_bf[:, c * T:(c + 1) * T], xn[:, c * T:(c + 1) * T],
                ln1_sb[:, c:c + 1], ps_b1[:], ALU.mult, ALU.mult), sem=dve)
        v_xh1 = dve.n
        p.wait("sync", dve, v_xh1)
        p.dma("sync", o_xh1[:, i * T:(i + 1) * T].rearrange("(c p) t -> p c t", p=128),
              out_bf[:].rearrange("p (c t) -> p c t", c=NCHUNK), sxh)

        p.wait("tensor", dve, max(v_m2r2, v_xh1))
        p.op("tensor", lambda e: e.matmul(ps_b1[:], onesr_f[:], srow(RSTD2), start=True, stop=True))
        p.op("tensor", lambda e: e.matmul(ps_b2[:], onesr_f[:], srow(M2R2), start=True, stop=True),
             sem=pe)
        v_bc2 = pe.n
        p.wait("vector", pe, v_bc2)
        if i > 0:
            p.wait("vector", sxt, sxt.n)
        for c in range(NCHUNK):
            p.op("vector", lambda e, c=c: e.tensor_mul(
                xh_tmp[:], xn[:, c * T:(c + 1) * T], ps_b1[:]))
            p.op("vector", lambda e, c=c: e.tensor_sub(
                out2_bf[:, c * T:(c + 1) * T], xh_tmp[:], ps_b2[:]), sem=dve)
        v_xhat = dve.n
        p.wait("sync", dve, v_xhat)
        p.dma("sync", o_xhat[:, i * T:(i + 1) * T].rearrange("(c p) t -> p c t", p=128),
              out2_bf[:].rearrange("p (c t) -> p c t", c=NCHUNK), sxt)

        p.wait("tensor", ld, v_pre)
        for c in range(NCHUNK):
            p.op("tensor", lambda e, c=c: e.matmul(
                ps_sc[:], cen_sb[:, c * 8:(c + 1) * 8], xn[:, c * T:(c + 1) * T],
                start=(c == 0), stop=(c == NCHUNK - 1)), sem=pe)
        v_sc = pe.n
        p.wait("vector", pe, v_sc)
        p.op("vector", lambda e: e.tensor_copy(sc_sb[:], ps_sc[:]), sem=dve)
        v_scsb = dve.n

        for s in range(4):
            p.wait("tensor", dve, max(v_scsb, ps_tr_free[0]))
            p.op("tensor", lambda e, s=s: e.transpose(
                ps_tr[:, 0:8], sc_sb[:, s * 128:(s + 1) * 128], id_f[0:8, 0:8]), sem=pe)
            v_tr = pe.n
            p.wait("vector", pe, v_tr)
            p.op("vector", lambda e, s=s: e.tensor_copy(sc_tok[:, s * 8:(s + 1) * 8],
                                                        ps_tr[:, 0:8]), sem=dve)
            v_tok = dve.n
            p.op("vector", lambda e, s=s: e.max(
                mx8[:, s * 8:(s + 1) * 8], sc_tok[:, s * 8:(s + 1) * 8]), sem=dve)
            v_mx = dve.n
            p.wait("scalar", dve, v_mx)
            p.op("scalar", lambda e, s=s: e.activation(
                alpha[:, s:s + 1], mx8[:, s * 8:s * 8 + 1], AF.Sigmoid), sem=act)
            v_al = act.n
            p.wait("vector", dve, v_mx)
            p.op("vector", lambda e, s=s: e.tensor_single_scalar(
                oh_tok[:, s * 8:(s + 1) * 8], sc_tok[:, s * 8:(s + 1) * 8],
                mx8[:, s * 8:s * 8 + 1], ALU.is_equal), sem=dve)
            v_oh = dve.n
            p.wait("vector", act, v_al)
            p.op("vector", lambda e, s=s: e.tensor_scalar_mul(
                oh_tok[:, s * 8:(s + 1) * 8], oh_tok[:, s * 8:(s + 1) * 8],
                alpha[:, s:s + 1]), sem=dve)
            v_w = dve.n
            p.wait("tensor", dve, v_w)
            p.op("tensor", lambda e, s=s: e.transpose(
                ps_tr[0:8, :], oh_tok[:, s * 8:(s + 1) * 8], id_f[:]),
                sem=pe)
            v_trb = pe.n
            p.wait("vector", pe, v_trb)
            v_w0c = p.op("vector", lambda e, s=s, i=i: e.tensor_copy(
                w0_sb[0:8, i * T + s * 128: i * T + (s + 1) * 128], ps_tr[0:8, :]), sem=dve)
            ps_tr_free[0] = v_w0c

        compute_done[i] = dve.n

    p.wait("sync", dve, compute_done[NTILE - 1])
    p.dma("sync", o_w0, w0_sb[:], st)
    p.dma("sync", o_r1, r1row[:], st)
    p.wait("sync", sxn, sxn.n)
    p.wait("sync", sxh, sxh.n)
    p.wait("sync", sxt, sxt.n)
    p.finish(st, st.n)
    p.end_phase()
    p.pop()
    return p



def build_qkv(p=None, rename=None):
    standalone = p is None
    if standalone:
        p = Prog('qkv')
    p.push(rename)
    p.begin_phase('qk_')
    T = 512
    NTILE = NT // T
    xh1 = p.din("xh1_bf", (D, NT), BF16)
    wq = p.din("wq", (D, D), F32)
    wk = p.din("wk", (D, D), F32)
    wv = p.din("wv", (D, D), F32)
    wbeta = p.din("wbeta", (D, 16), F32)
    bq_c = p.din("bq_c", (D, 1), F32)
    bk_c = p.din("bk_c", (D, 1), F32)
    bv_c = p.din("bv_c", (D, 1), F32)
    bbeta_c = p.din("bbeta_c", (16, 1), F32)
    o_q = p.dout("q_f", (D, NT), BF16)
    o_k = p.dout("kn_f", (D, NT), BF16)
    o_v = p.dout("v_f", (D, NT), BF16)
    o_bt = p.dout("beta_tok", (NT, 16), F32)

    e2 = np.zeros((128, 2), np.float32)
    e2[0:64, 0] = 1.0
    e2[64:128, 1] = 1.0
    c_e2 = p.const("c_e2", e2)
    c_e2t = p.const("c_e2t", np.ascontiguousarray(e2.T))
    c_idf = p.const("c_idf2", np.eye(128, dtype=np.float32))

    w_sb = {n: p.sb(f"w_{n}", (128, 8 * D), BF16) for n in ("q", "k", "v")}
    wb_sb = p.sb("wb_sb", (128, 8 * 16), BF16)
    bias_sb = p.sb("bias_sb", (128, 3 * 8), F32)
    bbeta_sb = p.sb("bbeta_sb", (16, 1), F32)
    e2_sb = p.sb("e2_sb", (128, 2), F32)
    e2t_sb = p.sb("e2t_sb", (2, 128), F32)
    idf_sb = p.sb("idf_sb", (128, 128), F32)
    xs = [p.sb(f"xh{i}", (128, NCHUNK * T), BF16) for i in range(2)]
    outq = p.sb("outq", (128, NCHUNK * T), BF16)
    outk = p.sb("outk", (128, NCHUNK * T), BF16)
    outv = p.sb("outv", (128, NCHUNK * T), BF16)
    kf32 = p.sb("kf32", (128, T), F32)
    ksq = p.sb("ksq", (128, T), F32)
    rn = p.sb("rn", (2, T), F32)
    beta_sb = p.sb("beta_sb", (16, T), F32)
    bt_tok = p.sb("bt_tok", (128, 16 * 4), F32)

    ps_a = p.ps("ps_a", (128, T))
    ps_b = p.ps("ps_b", (128, T))
    ps_n = p.ps("ps_n", (2, T))
    ps_bc = p.ps("ps_bc", (128, T))
    ps_beta = p.ps("ps_beta", (16, T))
    ps_tr = p.ps("ps_tr", (128, 16))

    ld = p.sem("ld")
    lw = p.sem("lw")
    lx = [p.sem("lx0"), p.sem("lx1")]
    pe = p.sem("pe")
    dve = p.stream_sem("vector")
    act = p.stream_sem("scalar")
    stq = p.sem("stq")
    stk = p.sem("stk")
    stv = p.sem("stv")
    stb = p.sem("stb")

    for n, w in (("q", wq), ("k", wk), ("v", wv)):
        for dc in range(NCHUNK):
            p.dma("gpsimd", w_sb[n][:, dc * D:(dc + 1) * D],
                  w[dc * 128:(dc + 1) * 128, :], lw, gate=False)
    for dc in range(NCHUNK):
        p.dma("gpsimd", wb_sb[:, dc * 16:(dc + 1) * 16],
              wbeta[dc * 128:(dc + 1) * 128, :], lw, gate=False)
    v_lw = lw.n
    p.dma("sync", bias_sb[:, 0:8], bq_c.rearrange("(c p) one -> p (c one)", p=128),
          ld, slow=True)
    p.dma("sync", bias_sb[:, 8:16], bk_c.rearrange("(c p) one -> p (c one)", p=128),
          ld, slow=True)
    p.dma("sync", bias_sb[:, 16:24], bv_c.rearrange("(c p) one -> p (c one)", p=128),
          ld, slow=True)
    p.dma("sync", bbeta_sb[:], bbeta_c, ld, slow=True)
    p.dma("sync", e2_sb[:], c_e2, ld)
    p.dma("sync", e2t_sb[:], c_e2t, ld)
    p.dma("sync", idf_sb[:], c_idf, ld)
    v_ld = ld.n

    comp_done = [0] * NTILE
    stq_done = stk_done = stv_done = 0
    pp = [ps_a, ps_b]
    psum_free = {0: None, 1: None}

    for i in range(NTILE):
        xb = xs[i % 2]
        if i >= 2:
            p.wait("sync", pe, comp_done[i - 2])
        v_lx = p.dma("sync", xb[:].rearrange("p (c t) -> p c t", c=NCHUNK),
                     xh1[:, i * T:(i + 1) * T].rearrange("(c p) t -> p c t", p=128),
                     lx[i % 2])
        p.wait("tensor", lx[i % 2], v_lx)
        if i == 0:
            p.wait("tensor", lw, v_lw)
            p.wait("scalar", ld, v_ld)
            p.wait("vector", ld, v_ld)

        nb = 0
        for n, outbuf, bias_i, stsem in (("q", outq, 0, stq), ("v", outv, 2, stv),
                                         ("k", outk, 1, stk)):
            if i > 0:
                p.wait("vector", stsem, stsem.n)
            for m in range(NCHUNK):
                bank = pp[nb]
                if psum_free[nb] is not None:
                    p.wait("tensor", {"dve": dve, "act": act}[psum_free[nb][0]],
                           psum_free[nb][1])
                for dc in range(NCHUNK):
                    p.op("tensor", lambda e, n=n, m=m, dc=dc, bank=bank, xb=xb: e.matmul(
                        bank[:], w_sb[n][:, (dc * 8 + m) * 128:(dc * 8 + m + 1) * 128],
                        xb[:, dc * T:(dc + 1) * T],
                        start=(dc == 0), stop=(dc == NCHUNK - 1)), sem=pe)
                v_mm = pe.n
                if n != "k":
                    p.wait("vector", pe, v_mm)
                    v_ev = p.op("vector", lambda e, m=m, bank=bank, outbuf=outbuf,
                                bias_i=bias_i: e.tensor_scalar_add(
                                    outbuf[:, m * T:(m + 1) * T], bank[:],
                                    bias_sb[:, bias_i * 8 + m:bias_i * 8 + m + 1]), sem=dve)
                    psum_free[nb] = ("dve", v_ev)
                else:
                    p.wait("vector", pe, v_mm)
                    v_kf = p.op("vector", lambda e, bank=bank, m=m: e.tensor_scalar_add(
                        kf32[:], bank[:], bias_sb[:, 8 + m:8 + m + 1]), sem=dve)
                    psum_free[nb] = ("dve", v_kf)
                    p.wait("scalar", dve, v_kf)
                    v_sq = p.op("scalar", lambda e: e.activation(
                        ksq[:], kf32[:], AF.Square), sem=act)
                    p.wait("tensor", act, v_sq)
                    p.op("tensor", lambda e: e.matmul(
                        ps_n[:], e2_sb[:], ksq[:], start=True, stop=True), sem=pe)
                    v_nrm = pe.n
                    p.wait("scalar", pe, v_nrm)
                    v_rt = p.op("scalar", lambda e: e.activation(
                        rn[:], ps_n[:], AF.Sqrt), sem=act)
                    p.wait("vector", act, v_rt)
                    p.op("vector", lambda e: e.tensor_scalar_add(rn[:], rn[:], 1e-6),
                         sem=dve)
                    v_rn = p.op("vector", lambda e: e.reciprocal(rn[:], rn[:]), sem=dve)
                    p.wait("tensor", dve, v_rn)
                    p.op("tensor", lambda e: e.matmul(
                        ps_bc[:], e2t_sb[:], rn[:], start=True, stop=True), sem=pe)
                    v_bc = pe.n
                    p.wait("vector", pe, v_bc)
                    p.op("vector", lambda e, m=m: e.tensor_mul(
                        outk[:, m * T:(m + 1) * T], kf32[:], ps_bc[:]), sem=dve)
                nb ^= 1

        for dc in range(NCHUNK):
            p.op("tensor", lambda e, dc=dc, xb=xb: e.matmul(
                ps_beta[:], wb_sb[:, dc * 16:(dc + 1) * 16], xb[:, dc * T:(dc + 1) * T],
                start=(dc == 0), stop=(dc == NCHUNK - 1)), sem=pe)
        v_bmm = pe.n
        p.wait("scalar", pe, v_bmm)
        if i > 0:
            p.wait("scalar", stb, stb.n)
        v_bsig = p.op("scalar", lambda e: e.activation(
            beta_sb[:], ps_beta[:], AF.Sigmoid, bias=bbeta_sb[:]), sem=act)
        p.wait("tensor", act, v_bsig)
        for s in range(4):
            if s > 0:
                p.wait("tensor", dve, dve.n)
            p.op("tensor", lambda e, s=s: e.transpose(
                ps_tr[:], beta_sb[:, s * 128:(s + 1) * 128], idf_sb[0:16, 0:16]), sem=pe)
            v_tr = pe.n
            p.wait("vector", pe, v_tr)
            p.op("vector", lambda e, s=s: e.tensor_copy(
                bt_tok[:, s * 16:(s + 1) * 16], ps_tr[:, 0:16]), sem=dve)
        v_bt = dve.n
        p.wait("tensor", dve, v_bt)

        p.wait("sync", act, act.n)
        p.wait("sync", dve, dve.n)
        p.dma("sync", o_q[:, i * T:(i + 1) * T].rearrange("(c p) t -> p c t", p=128),
              outq[:].rearrange("p (c t) -> p c t", c=NCHUNK), stq)
        p.dma("sync", o_v[:, i * T:(i + 1) * T].rearrange("(c p) t -> p c t", p=128),
              outv[:].rearrange("p (c t) -> p c t", c=NCHUNK), stv)
        p.dma("sync", o_k[:, i * T:(i + 1) * T].rearrange("(c p) t -> p c t", p=128),
              outk[:].rearrange("p (c t) -> p c t", c=NCHUNK), stk)
        p.dma("sync", o_bt[i * T:(i + 1) * T, :].rearrange("(s p) e -> p s e", p=128),
              bt_tok[:].rearrange("p (s e) -> p s e", e=16), stb)
        comp_done[i] = pe.n

    for sem in (stq, stk, stv, stb):
        p.wait("sync", sem, sem.n)
    p.finish(stb, stb.n)
    p.end_phase()
    p.pop()
    return p



def build_scan(name="scan", ci0=0, ci1=64, p=None, rename=None):
    import os
    STAGE = int(os.environ.get("SCAN_STAGE", "99"))
    standalone = p is None
    if standalone:
        p = Prog(name)
    p.push(rename)
    p.begin_phase("sc_")
    SEQ = 8192
    NH = 8
    HD = 64
    C = 128
    qs = p.din("qs", (NH * HD, SEQ), BF16)
    ks = p.din("ks", (NH * HD, SEQ), BF16)
    vs = p.din("vs", (NH * HD, SEQ), BF16)
    bt = p.din("bt", (SEQ, NH), F32)
    z_in = p.din("z_in", (NH * HD, HD), F32)
    o_out = p.dout("o_fm", (NH * HD, SEQ), BF16)
    z_out = p.dout("z_fin", (NH * HD, HD), F32)

    tri = np.tril(np.ones((128, 128), np.float32), -1).T
    trii = np.triu(np.ones((128, 128), np.float32))
    c_mu = p.const("c_maskU", tri)
    c_mi = p.const("c_maskLi", trii)
    c_idb = p.const("c_idb", np.eye(128, dtype=BF16NP))

    maskU = p.sb("maskU", (128, 128), F32)
    maskLi = p.sb("maskLi", (128, 128), F32)
    id_bf = p.sb("id_bf", (128, 128), BF16)
    kq = [p.sb(f"kq{j}", (64, NH * C), BF16) for j in range(2)]
    kk = [p.sb(f"kk{j}", (64, NH * C), BF16) for j in range(2)]
    kv = [p.sb(f"kv{j}", (64, NH * C), BF16) for j in range(2)]
    bcol = [p.sb(f"bcol{j}", (128, NH), F32) for j in range(2)]
    aT = p.sb("aT", (128, 7 * 128), BF16)
    aN = p.sb("aN", (128, 7 * 128), BF16)
    xw = [p.sb(f"xw{j}", (128, 128), BF16) for j in range(2)]
    kr_t = p.sb("kr_t", (128, 64), BF16)
    w_t = p.sb("w_t", (128, 64), BF16)
    u_t = p.sb("u_t", (128, 64), BF16)
    gt = p.sb("gt", (128, 128), BF16)
    qt_all = p.sb("qt_all", (64, NH * C), BF16)
    rt_all = p.sb("rt_all", (64, NH * HD), F32)
    pm_all = p.sb("pm_all", (64, NH * HD), F32)
    zc = [p.sb(f"z{j}", (64, NH * HD), F32) for j in range(2)]
    z_bf = p.sb("z_bf", (64, HD), BF16)
    ztmp = p.sb("ztmp", (64, HD), F32)
    o_intra = p.sb("o_intra", (128, NH * HD), F32)
    o_tok = p.sb("o_tok", (128, 128), BF16)
    o_stash = p.sb("o_stash", (128, 4 * 128), BF16)

    psA = p.ps("psA", (128, 512))
    psB = p.ps("psB", (128, 512))
    psC = p.ps("psC", (128, 512))
    psT = p.ps("psT", (128, 1024), BF16)
    psz = [p.ps(f"psz{j}", (128, 512)) for j in range(2)]

    ld = p.sem("ld")
    lin = [p.sem("lin0"), p.sem("lin1")]
    pe = p.sem("pe")
    dve = p.stream_sem("vector")
    act = p.stream_sem("scalar")
    sto = p.sem("sto")
    szf = p.sem("szf")

    p.dma("sync", maskU[:], c_mu, ld)
    p.dma("sync", maskLi[:], c_mi, ld)
    p.dma("sync", id_bf[:], c_idb, ld)
    p.dma("sync", zc[0][:].rearrange("d (h e) -> d h e", h=NH),
          z_in.rearrange("(h d) e -> d h e", d=64), ld)
    v_ld = ld.n
    p.wait("tensor", ld, v_ld)
    p.wait("vector", ld, v_ld)
    p.wait("scalar", ld, v_ld)

    NCH = ci1 - ci0
    comp_done = [0] * NCH

    for ic in range(NCH):
        i = ci0 + ic
        bb = ic % 2
        if ic >= 2:
            p.wait("sync", pe, comp_done[ic - 2])
        vq = p.dma("sync", kq[bb][:].rearrange("d (h t) -> d h t", h=NH),
                   qs[:, i * C:(i + 1) * C].rearrange("(h d) t -> d h t", d=64),
                   lin[bb], gate=False)
        vk = p.dma("sync", kk[bb][:].rearrange("d (h t) -> d h t", h=NH),
                   ks[:, i * C:(i + 1) * C].rearrange("(h d) t -> d h t", d=64),
                   lin[bb], gate=False)
        vv = p.dma("sync", kv[bb][:].rearrange("d (h t) -> d h t", h=NH),
                   vs[:, i * C:(i + 1) * C].rearrange("(h d) t -> d h t", d=64),
                   lin[bb], gate=False)
        vb = p.dma("sync", bcol[bb][:], bt[i * C:(i + 1) * C, :], lin[bb], gate=False)
        v_in = vb
        p.wait("tensor", lin[bb], v_in)
        p.wait("vector", lin[bb], v_in)

        zcur = zc[(ic + (ci0 % 2)) % 2]
        znxt = zc[(ic + 1 + (ci0 % 2)) % 2]

        for h in range(NH):
            if STAGE < 0:
                break
            q_t = kq[bb][:, h * C:(h + 1) * C]
            k_t = kk[bb][:, h * C:(h + 1) * C]
            v_t = kv[bb][:, h * C:(h + 1) * C]
            b_c = bcol[bb][:, h:h + 1]

            p.op("tensor", lambda e, k_t=k_t: e.matmul(
                psA[:, 0:128], k_t, k_t, start=True, stop=True), sem=pe)
            v_smat = pe.n
            if STAGE < 1:
                p.wait("tensor", dve, dve.n)
                p.wait("tensor", act, act.n)
                continue
            p.wait("vector", pe, v_smat)
            v_atT = p.op("vector", lambda e, b_c=b_c: e.scalar_tensor_tensor(
                aT[:, 0:128], maskU[:], b_c, psA[:, 0:128], ALU.mult, ALU.mult),
                sem=dve)
            if STAGE < 2:
                p.wait("tensor", dve, dve.n)
                p.wait("tensor", act, act.n)
                continue
            p.wait("tensor", dve, v_atT)
            p.op("tensor", lambda e: e.transpose(psT[:, 0:128], aT[:, 0:128], id_bf[:]),
                 sem=pe)
            v_trA = pe.n
            p.wait("scalar", pe, v_trA)
            v_a0 = p.op("scalar", lambda e: e.copy(aN[:, 0:128], psT[:, 0:128]), sem=act)
            if STAGE < 3:
                p.wait("tensor", dve, dve.n)
                p.wait("tensor", act, act.n)
                continue
            prev_evac = ("scalar", v_a0)
            NSQL = int(os.environ.get("SCAN_SQL", "6"))
            for l in range(NSQL):
                p.wait("tensor", act if prev_evac[0] == "scalar" else dve, prev_evac[1])
                p.op("tensor", lambda e, l=l: e.matmul(
                    psB[:, 0:128], aT[:, l * 128:(l + 1) * 128],
                    aN[:, l * 128:(l + 1) * 128], start=True, stop=True))
                p.op("tensor", lambda e, l=l: e.matmul(
                    psC[:, 0:128], aN[:, l * 128:(l + 1) * 128],
                    aT[:, l * 128:(l + 1) * 128], start=True, stop=True), sem=pe)
                v_sq = pe.n
                p.wait("vector", pe, v_sq)
                v_e1 = p.op("vector", lambda e, l=l: e.tensor_copy(
                    aN[:, (l + 1) * 128:(l + 2) * 128], psB[:, 0:128]), sem=dve)
                p.wait("scalar", pe, v_sq)
                v_e2 = p.op("scalar", lambda e, l=l: e.copy(
                    aT[:, (l + 1) * 128:(l + 2) * 128], psC[:, 0:128]), sem=act)
                prev_evac = ("scalar", v_e2)
                p.wait("tensor", dve, v_e1)
            p.wait("tensor", act, prev_evac[1])
            if NSQL < 6:
                p.wait("tensor", dve, dve.n)
                continue

            if STAGE < 4:
                p.wait("tensor", dve, dve.n)
                p.wait("tensor", act, act.n)
                continue
            p.op("tensor", lambda e, k_t=k_t: e.transpose(
                psT[:, 0:64], k_t, id_bf[0:64, 0:64]), sem=pe)
            v_tk = pe.n
            p.wait("vector", pe, v_tk)
            v_kr = p.op("vector", lambda e: e.tensor_copy(kr_t[:, 0:64], psT[:, 0:64]),
                        sem=dve)
            v_x0a = p.op("vector", lambda e: e.tensor_copy(xw[0][:, 0:64], psT[:, 0:64]),
                         sem=dve)
            p.wait("tensor", dve, v_x0a)
            p.op("tensor", lambda e, v_t=v_t: e.transpose(
                psT[:, 0:64], v_t, id_bf[0:64, 0:64]), sem=pe)
            v_tv = pe.n
            p.wait("vector", pe, v_tv)
            v_x0b = p.op("vector", lambda e: e.tensor_copy(xw[0][:, 64:128], psT[:, 0:64]),
                         sem=dve)
            p.wait("tensor", dve, v_x0b)

            if STAGE < 5:
                p.wait("tensor", dve, dve.n)
                p.wait("tensor", act, act.n)
                continue
            xi = 0
            for ai, sign in ((6, 1), (5, 1), (4, 1), (3, 1), (2, 1), (1, 1), (0, -1)):
                p.op("tensor", lambda e, ai=ai, xi=xi: e.matmul(
                    psB[:, 0:128], aT[:, ai * 128:(ai + 1) * 128], xw[xi][:],
                    start=True, stop=True), sem=pe)
                v_ap = pe.n
                p.wait("vector", pe, v_ap)
                if sign > 0:
                    v_x = p.op("vector", lambda e, xi=xi: e.tensor_add(
                        xw[1 - xi][:], xw[xi][:], psB[:, 0:128]), sem=dve)
                else:
                    v_x = p.op("vector", lambda e, xi=xi: e.tensor_sub(
                        xw[1 - xi][:], xw[xi][:], psB[:, 0:128]), sem=dve)
                p.wait("tensor", dve, v_x)
                xi = 1 - xi

            if STAGE < 6:
                p.wait("tensor", dve, dve.n)
                p.wait("tensor", act, act.n)
                continue
            v_w = p.op("vector", lambda e, xi=xi, b_c=b_c: e.tensor_scalar_mul(
                w_t[:, 0:64], xw[xi][:, 0:64], b_c), sem=dve)
            v_u = p.op("vector", lambda e, xi=xi, b_c=b_c: e.tensor_scalar_mul(
                u_t[:, 0:64], xw[xi][:, 64:128], b_c), sem=dve)
            p.wait("tensor", dve, v_u)

            if STAGE < 7:
                p.wait("tensor", dve, dve.n)
                p.wait("tensor", act, act.n)
                continue
            p.op("tensor", lambda e, k_t=k_t, q_t=q_t: e.matmul(
                psA[:, 0:128], k_t, q_t, start=True, stop=True), sem=pe)
            v_gm = pe.n
            p.wait("vector", pe, v_gm)
            v_gt = p.op("vector", lambda e: e.tensor_mul(
                gt[:], psA[:, 0:128], maskLi[:]), sem=dve)
            p.wait("tensor", dve, v_gt)

            if STAGE < 8:
                p.wait("tensor", dve, dve.n)
                p.wait("tensor", act, act.n)
                continue
            p.op("tensor", lambda e, h=h: e.matmul(
                psA[:, 0:64], gt[:], u_t[:, 0:64],
                start=True, stop=True), sem=pe)
            v_oi = pe.n
            p.wait("scalar", pe, v_oi)
            v_oie = p.op("scalar", lambda e, h=h: e.copy(
                o_intra[:, h * 64:(h + 1) * 64], psA[:, 0:64]), sem=act)
            p.wait("tensor", act, v_oie)
            p.op("tensor", lambda e: e.matmul(
                psB[0:64, 0:128], w_t[:, 0:64], gt[:], start=True, stop=True), sem=pe)
            v_qm = pe.n
            p.wait("vector", pe, v_qm)
            v_qt = p.op("vector", lambda e, h=h, q_t=q_t: e.tensor_sub(
                qt_all[:, h * C:(h + 1) * C], q_t, psB[0:64, 0:128]), sem=dve)
            p.op("tensor", lambda e: e.matmul(
                psA[0:64, 0:64], w_t[:, 0:64], kr_t[:, 0:64], start=True, stop=True),
                sem=pe)
            v_rt = pe.n
            p.wait("tensor", dve, v_qt)
            p.op("tensor", lambda e: e.matmul(
                psB[0:64, 0:64], kr_t[:, 0:64], u_t[:, 0:64], start=True, stop=True),
                sem=pe)
            v_pm = pe.n
            p.wait("scalar", pe, v_rt)
            v_rte = p.op("scalar", lambda e, h=h: e.copy(
                rt_all[:, h * 64:(h + 1) * 64], psA[0:64, 0:64]), sem=act)
            p.wait("scalar", pe, v_pm)
            v_pme = p.op("scalar", lambda e, h=h: e.copy(
                pm_all[:, h * 64:(h + 1) * 64], psB[0:64, 0:64]), sem=act)
            p.wait("tensor", act, v_pme)
            p.wait("tensor", dve, v_qt)

        if STAGE < 9:
            comp_done[ic] = pe.n
            continue
        if ic > 0:
            p.wait("vector", sto, sto.n)
        corr_free = [None, None]
        for h in range(NH):
            zslc = zcur[:, h * 64:(h + 1) * 64]
            bank = psz[h % 2]
            cbank = psA if h % 2 == 0 else psB
            v_zb = p.op("vector", lambda e, zslc=zslc: e.tensor_copy(
                z_bf[:, 0:64], zslc), sem=dve)
            p.wait("tensor", dve, v_zb)
            if corr_free[h % 2] is not None:
                p.wait("tensor", dve, corr_free[h % 2])
            p.op("tensor", lambda e, h=h, cbank=cbank: e.matmul(
                cbank[:, 0:64], qt_all[:, h * C:(h + 1) * C],
                z_bf[:, 0:64], start=True, stop=True), sem=pe)
            v_corr = pe.n
            p.op("tensor", lambda e, h=h, zslc=zslc, bank=bank: e.matmul(
                bank[0:64, 0:64], rt_all[:, h * 64:(h + 1) * 64], zslc,
                start=True, stop=True), sem=pe)
            v_zm = pe.n
            p.wait("vector", pe, v_corr)
            v_oh = p.op("vector", lambda e, h=h, cbank=cbank: e.tensor_add(
                o_tok[:, (h % 2) * 64:(h % 2) * 64 + 64],
                cbank[:, 0:64], o_intra[:, h * 64:(h + 1) * 64]), sem=dve)
            corr_free[h % 2] = v_oh
            if h % 2 == 1:
                p.wait("tensor", dve, v_oh)
                p.op("tensor", lambda e: e.transpose(
                    psT[:, 0:128], o_tok[:], id_bf[:]), sem=pe)
                v_otr = pe.n
                p.wait("vector", pe, v_otr)
                v_stash = p.op("vector", lambda e, h=h: e.tensor_copy(
                    o_stash[:, (h // 2) * 128:(h // 2 + 1) * 128],
                    psT[:, 0:128]), sem=dve)
                p.wait("tensor", dve, v_stash)
            p.wait("vector", pe, v_zm)
            p.op("vector", lambda e, zslc=zslc, bank=bank: e.scalar_tensor_tensor(
                ztmp[:, 0:64], zslc, 1.0, bank[0:64, 0:64], ALU.mult, ALU.subtract),
                sem=dve)
            v_zn = p.op("vector", lambda e, h=h, znxt=znxt: e.tensor_add(
                znxt[:, h * 64:(h + 1) * 64], ztmp[:, 0:64],
                pm_all[:, h * 64:(h + 1) * 64]), sem=dve)
            p.wait("tensor", dve, v_zn)
        v_oev = dve.n
        p.wait("sync", dve, v_oev)
        p.dma("sync", o_out[:, i * C:(i + 1) * C].rearrange("(hp d) t -> d hp t", d=128),
              o_stash[:].rearrange("d (hp t) -> d hp t", hp=4), sto)
        p.wait("tensor", dve, v_oev)
        comp_done[ic] = pe.n

    zfin = zc[(NCH + (ci0 % 2)) % 2]
    p.wait("sync", dve, dve.n)
    p.dma("sync", z_out.rearrange("(h d) e -> d h e", d=64),
          zfin[:].rearrange("d (h e) -> d h e", h=NH), szf)
    p.wait("sync", sto, sto.n)
    p.finish(szf, szf.n)
    p.end_phase()
    p.pop()
    return p



def build_attnout(p=None, rename=None):
    standalone = p is None
    if standalone:
        p = Prog('attnout')
    p.push(rename)
    p.begin_phase('ao_')
    T = 256
    NTILE = NT // T
    o_in = p.din("o_fm", (D, NT), BF16)
    xn_fm = p.din("xn_fm", (D, NT), F32)
    wo = p.din("wo", (D, D), F32)
    w1 = p.din("w1", (D, 4096), F32)
    w2 = p.din("w2", (4096, D), F32)
    bo_c = p.din("bo_c", (D, 1), F32)
    ln2_c = p.din("ln2_c", (D, 1), F32)
    b1_c = p.din("b1_c", (4096, 1), F32)
    b2_c = p.din("b2_c", (D, 1), F32)
    o_acc1 = p.dout("acc1_fm", (D, NT), F32)

    wo_sb = p.sb("wo_sb", (128, 8 * D), BF16)
    w1_sb = p.sb("w1_sb", (128, 8 * 4096), BF16)
    w2_sb = p.sb("w2_sb", (128, 32 * D), BF16)
    bcols = p.sb("bcols", (128, 8 + 8 + 8 + 32), F32)
    ones_f = p.sb("ones_f", (128, 1), F32)
    onesr_f = p.sb("onesr_f", (1, 128), F32)
    cvals = p.sb("cvals", (128, 1), F32)
    ob = p.sb("ob", (128, NCHUNK * T), BF16)
    xnb = p.sb("xnb", (128, NCHUNK * T), F32)
    y1 = p.sb("y1", (128, NCHUNK * T), F32)
    xh2 = p.sb("xh2", (128, NCHUNK * T), BF16)
    hb = p.sb("hb", (128, 32 * T), BF16)
    sq = p.sb("sq", (128, NCHUNK * T), F32)
    stat = p.sb("stat", (1, 2 * T), F32)

    ps_m = [p.ps("ps_m0", (128, 512)), p.ps("ps_m1", (128, 512))]
    ps_r = p.ps("ps_r", (1, 512))
    ps_bc = p.ps("ps_bc", (128, 512))

    ld = p.sem("ld")
    lw = p.sem("lw")
    li = p.sem("li")
    pe = p.sem("pe")
    dve = p.stream_sem("vector")
    act = p.stream_sem("scalar")
    gp = p.stream_sem("gpsimd")
    st = p.sem("st")

    for dc in range(NCHUNK):
        p.dma("gpsimd", wo_sb[:, dc * D:(dc + 1) * D],
              wo[dc * 128:(dc + 1) * 128, :], lw, gate=False)
        p.dma("gpsimd", w1_sb[:, dc * 4096:(dc + 1) * 4096],
              w1[dc * 128:(dc + 1) * 128, :], lw, gate=False)
    for fc in range(32):
        p.dma("gpsimd", w2_sb[:, fc * D:(fc + 1) * D],
              w2[fc * 128:(fc + 1) * 128, :], lw, gate=False)
    v_lw = lw.n
    p.dma("sync", bcols[:, 0:8], bo_c.rearrange("(c p) one -> p (c one)", p=128), ld, slow=True)
    p.dma("sync", bcols[:, 8:16], ln2_c.rearrange("(c p) one -> p (c one)", p=128), ld, slow=True)
    p.dma("sync", bcols[:, 16:24], b2_c.rearrange("(c p) one -> p (c one)", p=128), ld, slow=True)
    p.dma("sync", bcols[:, 24:56], b1_c.rearrange("(c p) one -> p (c one)", p=128), ld, slow=True)
    v_ld = ld.n
    p.op("gpsimd", lambda e: e.memset(ones_f[:], 1.0), sem=gp)
    p.op("gpsimd", lambda e: e.memset(onesr_f[:], 1.0), sem=gp)
    p.op("gpsimd", lambda e: e.memset(cvals[:, 0:1], 1e-6), sem=gp)
    v_gp = gp.n

    p.wait("vector", ld, v_ld)
    p.wait("scalar", ld, v_ld)
    p.wait("tensor", lw, v_lw)
    p.wait("tensor", gp, v_gp)
    p.wait("scalar", gp, v_gp)

    psum_free = [None, None]
    comp_done = [0] * NTILE

    def waitfree(nb):
        if psum_free[nb] is not None:
            eng, v = psum_free[nb]
            p.wait("tensor", {"dve": dve, "act": act}[eng], v)

    for i in range(NTILE):
        if i >= 1:
            p.wait("sync", dve, comp_done[i - 1])
            p.wait("vector", st, st.n)
        v_li = p.dma("sync", ob[:].rearrange("p (c t) -> p c t", c=NCHUNK),
                     o_in[:, i * T:(i + 1) * T].rearrange("(c p) t -> p c t", p=128), li)
        v_li = p.dma("sync", xnb[:].rearrange("p (c t) -> p c t", c=NCHUNK),
                     xn_fm[:, i * T:(i + 1) * T].rearrange("(c p) t -> p c t", p=128), li)
        p.wait("tensor", li, v_li)
        p.wait("vector", li, v_li)

        for mo in range(NCHUNK):
            nb = mo % 2
            bank = ps_m[nb]
            waitfree(nb)
            for dc in range(NCHUNK):
                p.op("tensor", lambda e, mo=mo, dc=dc, bank=bank: e.matmul(
                    bank[:, 0:T], wo_sb[:, (dc * 8 + mo) * 128:(dc * 8 + mo + 1) * 128],
                    ob[:, dc * T:(dc + 1) * T],
                    start=(dc == 0), stop=(dc == NCHUNK - 1)), sem=pe)
            v_mm = pe.n
            p.wait("vector", pe, v_mm)
            v_ev = p.op("vector", lambda e, mo=mo, bank=bank: e.scalar_tensor_tensor(
                y1[:, mo * T:(mo + 1) * T], xnb[:, mo * T:(mo + 1) * T],
                bcols[:, mo:mo + 1], bank[:, 0:T], ALU.add, ALU.add), sem=dve)
            psum_free[nb] = ("dve", v_ev)

        p.wait("scalar", dve, dve.n)
        for c in range(NCHUNK):
            p.op("scalar", lambda e, c=c: e.activation(
                sq[:, c * T:(c + 1) * T], y1[:, c * T:(c + 1) * T], AF.Square), sem=act)
            v_sq = act.n
            p.wait("tensor", act, v_sq)
            p.op("tensor", lambda e, c=c: e.matmul(
                ps_r[:, 0:T], ones_f[:], sq[:, c * T:(c + 1) * T],
                start=(c == 0), stop=(c == NCHUNK - 1)), sem=pe)
        v_st = pe.n
        p.wait("scalar", pe, v_st)
        p.op("scalar", lambda e: e.activation(stat[:, 0:T], ps_r[:, 0:T], AF.Ln,
                                              bias=cvals[0:1, 0:1], scale=1.0 / D), sem=act)
        v_r2 = p.op("scalar", lambda e: e.activation(stat[:, 0:T], stat[:, 0:T],
                                                     AF.Exp, scale=-0.5), sem=act)
        p.wait("tensor", act, v_r2)
        p.op("tensor", lambda e: e.matmul(ps_bc[:, 0:T], onesr_f[:], stat[:, 0:T],
                                          start=True, stop=True), sem=pe)
        v_bc = pe.n
        p.wait("vector", pe, v_bc)
        for c in range(NCHUNK):
            p.op("vector", lambda e, c=c: e.scalar_tensor_tensor(
                xh2[:, c * T:(c + 1) * T], y1[:, c * T:(c + 1) * T],
                bcols[:, 8 + c:8 + c + 1], ps_bc[:, 0:T], ALU.mult, ALU.mult), sem=dve)
        v_xh2 = dve.n
        p.wait("tensor", dve, v_xh2)

        for fc in range(32):
            nb = fc % 2
            bank = ps_m[nb]
            waitfree(nb)
            for dc in range(NCHUNK):
                p.op("tensor", lambda e, fc=fc, dc=dc, bank=bank: e.matmul(
                    bank[:, 0:T], w1_sb[:, dc * 4096 + fc * 128:dc * 4096 + (fc + 1) * 128],
                    xh2[:, dc * T:(dc + 1) * T],
                    start=(dc == 0), stop=(dc == NCHUNK - 1)), sem=pe)
            v_mm = pe.n
            p.wait("scalar", pe, v_mm)
            v_g = p.op("scalar", lambda e, fc=fc, bank=bank: e.activation(
                hb[:, fc * T:(fc + 1) * T], bank[:, 0:T], AF.Gelu_apprx_tanh,
                bias=bcols[:, 24 + fc:24 + fc + 1]), sem=act)
            psum_free[nb] = ("act", v_g)

        p.wait("tensor", act, act.n)
        for mo in range(NCHUNK):
            nb = mo % 2
            bank = ps_m[nb]
            waitfree(nb)
            for fc in range(32):
                p.op("tensor", lambda e, mo=mo, fc=fc, bank=bank: e.matmul(
                    bank[:, 0:T], w2_sb[:, fc * D + mo * 128:fc * D + (mo + 1) * 128],
                    hb[:, fc * T:(fc + 1) * T],
                    start=(fc == 0), stop=(fc == 31)), sem=pe)
            v_mm = pe.n
            p.wait("vector", pe, v_mm)
            v_a = p.op("vector", lambda e, mo=mo, bank=bank: e.scalar_tensor_tensor(
                y1[:, mo * T:(mo + 1) * T], y1[:, mo * T:(mo + 1) * T],
                bcols[:, 16 + mo:16 + mo + 1], bank[:, 0:T], ALU.add, ALU.add), sem=dve)
            psum_free[nb] = ("dve", v_a)

        v_done = dve.n
        p.wait("sync", dve, v_done)
        p.dma("sync", o_acc1[:, i * T:(i + 1) * T].rearrange("(c p) t -> p c t", p=128),
              y1[:].rearrange("p (c t) -> p c t", c=NCHUNK), st)
        comp_done[i] = v_done

    p.finish(st, st.n)
    p.end_phase()
    p.pop()
    return p



def build_x1init(p=None, rename=None):
    standalone = p is None
    if standalone:
        p = Prog('x1init')
    p.push(rename)
    p.begin_phase('xi_')
    T = 512
    NTILE = NT // T
    x_fm = p.din("x_fm", (D, NT), F32)
    acc1 = p.din("acc1_fm", (D, NT), F32)
    xn_fm = p.din("xn_fm", (D, NT), F32)
    o_acc = p.dout("x1acc", (D, NT), F32)
    a = p.sb("a", (128, NCHUNK * T), F32)
    b = p.sb("b", (128, NCHUNK * T), F32)
    c = p.sb("c", (128, NCHUNK * T), F32)
    li = p.sem("li")
    dve = p.stream_sem("vector")
    st = p.sem("st")
    for i in range(NTILE):
        if i > 0:
            p.wait("sync", dve, dve.n)
        for t, src in ((a, x_fm), (b, acc1), (c, xn_fm)):
            p.dma("sync", t[:].rearrange("p (c t) -> p c t", c=NCHUNK),
                  src[:, i * T:(i + 1) * T].rearrange("(c p) t -> p c t", p=128), li,
                  gate=False)
        v_li = p.wait("vector", li, li.n)
        if i > 0:
            p.wait("vector", st, st.n)
        p.op("vector", lambda e: e.tensor_add(a[:], a[:], b[:]), sem=dve)
        p.op("vector", lambda e: e.tensor_add(a[:], a[:], c[:]), sem=dve)
        p.wait("sync", dve, dve.n)
        p.dma("sync", o_acc[:, i * T:(i + 1) * T].rearrange("(c p) t -> p c t", p=128),
              a[:].rearrange("p (c t) -> p c t", c=NCHUNK), st)
    p.finish(st, st.n)
    p.end_phase()
    p.pop()
    return p



def build_moe(name, e0, e1, layer, p=None, rename=None):
    standalone = p is None
    if standalone:
        p = Prog(name)
    p.push(rename)
    p.begin_phase(name + "_")
    T = 512
    NTILE = NT // T
    F = 4096
    xhat = p.din("xhat_bf", (D, NT), BF16)
    acc_in = p.din("acc_in", (D, NT), F32)
    wm = p.din("wm_fm", (8, NT), F32)
    w1d = p.din("W1T", (8, D, F), F32)
    w2d = p.din("W2T", (8, F, D), F32)
    b1d = p.din("b1e", (8, F), F32)
    b2d = p.din("b2e", (8, D), F32)
    if layer == 0:
        gad = p.din("gammae", (8, D), F32)
        bed = p.din("betae", (8, D), F32)
    o_acc = p.dout("acc_out", (D, NT), F32)

    w1_sb = p.sb("w1_sb", (128, 8 * F), BF16)
    w2_sb = p.sb("w2_sb", (128, 32 * D), BF16)
    xh = p.sb("xh", (128, NCHUNK * T), BF16)
    accb = p.sb("accb", (128, NCHUNK * T), F32)
    hb = p.sb("hb", (128, 32 * T), BF16)
    w0row = p.sb("w0row", (1, NT), BF16)
    b1row = p.sb("b1row", (1, F), BF16)
    b2row = p.sb("b2row", (1, D), BF16)
    onesr_f = p.sb("onesr_f", (1, 128), F32)
    onesr_b = p.sb("onesr_b", (1, 128), BF16)
    if layer == 0:
        gcols = p.sb("gcols", (128, 8), F32)
        becols = p.sb("becols", (128, 8), BF16)

    ps_w = p.ps("ps_w", (128, 512))
    ps_m = [p.ps("ps_m0", (128, 512)), p.ps("ps_m1", (128, 512))]
    ps_b1 = p.ps("ps_b1", (1, 512))

    lw = p.sem("lw")
    lr = p.sem("lr")
    lrs = p.sem("lrs")
    la = p.sem("la")
    lx = p.sem("lx")
    pe = p.sem("pe")
    dve = p.stream_sem("vector")
    act = p.stream_sem("scalar")
    gp = p.stream_sem("gpsimd")
    sa = p.sem("sa")

    p.op("gpsimd", lambda e: e.memset(onesr_f[:], 1.0), sem=gp)
    p.op("gpsimd", lambda e: e.memset(onesr_b[:], 1.0), sem=gp)
    v_gp = gp.n
    p.wait("tensor", gp, v_gp)

    AFUN = AF.Relu if layer == 0 else AF.Gelu_apprx_tanh
    sval = [0] * NTILE
    comp = [0] * NTILE

    for ei, e in enumerate(range(e0, e1)):
        p.wait("gpsimd", pe, pe.n)
        for dc in range(NCHUNK):
            p.dma("gpsimd", w1_sb[:, dc * F:(dc + 1) * F],
                  w1d[e, dc * 128:(dc + 1) * 128, :], lw, gate=False)
        for fc in range(32):
            p.dma("gpsimd", w2_sb[:, fc * D:(fc + 1) * D],
                  w2d[e, fc * 128:(fc + 1) * 128, :], lw, gate=False)
        v_lw = lw.n
        p.dma("gpsimd", w0row[:], wm[e:e + 1, :], lr, gate=False)
        p.dma("gpsimd", b1row[:], b1d[e:e + 1, :], lr, gate=False)
        p.dma("gpsimd", b2row[:], b2d[e:e + 1, :], lr, gate=False)
        if layer == 0:
            p.dma("sync", gcols[:], gad[e:e + 1, :].rearrange(
                "one (c p) -> p (c one)", p=128), lrs, slow=True, gate=False)
            p.dma("gpsimd", becols[:], bed[e:e + 1, :].rearrange(
                "one (c p) -> p (c one)", p=128), lr, slow=True, gate=False)
        v_lr = lr.n
        p.wait("tensor", lw, v_lw)
        p.wait("tensor", lr, v_lr)
        p.wait("vector", lw, v_lw)
        p.wait("vector", lr, v_lr)
        p.wait("vector", lrs, lrs.n)
        p.wait("tensor", lrs, lrs.n)

        if layer == 0:
            for fg in range(8):
                p.wait("tensor", dve, dve.n)
                for dc in range(NCHUNK):
                    p.op("tensor", lambda e_, fg=fg, dc=dc: e_.matmul(
                        ps_b1[:, 0:512], becols[:, dc:dc + 1],
                        w1_sb[:, dc * F + fg * 512:dc * F + (fg + 1) * 512],
                        start=(dc == 0), stop=(dc == NCHUNK - 1)), sem=pe)
                v_mv = pe.n
                p.wait("vector", pe, v_mv)
                p.op("vector", lambda e_, fg=fg: e_.tensor_add(
                    b1row[:, fg * 512:(fg + 1) * 512], b1row[:, fg * 512:(fg + 1) * 512],
                    ps_b1[:, 0:512]), sem=dve)
            for dc in range(NCHUNK):
                p.op("vector", lambda e_, dc=dc: e_.tensor_scalar_mul(
                    w1_sb[:, dc * F:(dc + 1) * F], w1_sb[:, dc * F:(dc + 1) * F],
                    gcols[:, dc:dc + 1]), sem=dve)
            v_fold = dve.n
            p.wait("tensor", dve, v_fold)

        for i in range(NTILE):
            p.wait("sync", sa, sa.n)
            if ei > 0:
                p.wait("sync", pe, comp[i])
            v_lx = p.dma("sync", xh[:].rearrange("p (c t) -> p c t", c=NCHUNK),
                         xhat[:, i * T:(i + 1) * T].rearrange("(c p) t -> p c t", p=128),
                         lx, gate=True)
            src = acc_in if ei == 0 else o_acc
            v_la = p.dma("sync", accb[:].rearrange("p (c t) -> p c t", c=NCHUNK),
                         src[:, i * T:(i + 1) * T].rearrange("(c p) t -> p c t", p=128),
                         la, gate=True)
            p.wait("tensor", lx, v_lx)
            p.op("tensor", lambda e_, i=i: e_.matmul(
                ps_w[:, 0:T], onesr_b[:], w0row[:, i * T:(i + 1) * T],
                start=True, stop=True), sem=pe)
            v_w = pe.n
            p.wait("vector", pe, v_w)
            for c in range(NCHUNK):
                p.op("vector", lambda e_, c=c: e_.tensor_mul(
                    xh[:, c * T:(c + 1) * T], xh[:, c * T:(c + 1) * T], ps_w[:, 0:T]),
                    sem=dve)
            v_xe = dve.n
            p.wait("tensor", dve, v_xe)

            psum_free = [None, None]
            for fc in range(32):
                nb = fc % 2
                bank = ps_m[nb]
                if psum_free[nb] is not None:
                    p.wait("tensor", act, psum_free[nb])
                for dc in range(NCHUNK):
                    p.op("tensor", lambda e_, fc=fc, dc=dc, bank=bank: e_.matmul(
                        bank[:, 0:T], w1_sb[:, dc * F + fc * 128:dc * F + (fc + 1) * 128],
                        xh[:, dc * T:(dc + 1) * T],
                        start=(dc == 0), stop=False), sem=pe)
                p.op("tensor", lambda e_, fc=fc, i=i, bank=bank: e_.matmul(
                    bank[:, 0:T], b1row[:, fc * 128:(fc + 1) * 128],
                    w0row[:, i * T:(i + 1) * T], start=False, stop=True), sem=pe)
                v_mm = pe.n
                p.wait("scalar", pe, v_mm)
                v_h = p.op("scalar", lambda e_, fc=fc, bank=bank: e_.activation(
                    hb[:, fc * T:(fc + 1) * T], bank[:, 0:T], AFUN), sem=act)
                psum_free[nb] = v_h
            p.wait("tensor", act, act.n)

            psum_free = [None, None]
            for mo in range(NCHUNK):
                nb = mo % 2
                bank = ps_m[nb]
                if psum_free[nb] is not None:
                    p.wait("tensor", dve, psum_free[nb])
                for fc in range(32):
                    p.op("tensor", lambda e_, mo=mo, fc=fc, bank=bank: e_.matmul(
                        bank[:, 0:T], w2_sb[:, fc * D + mo * 128:fc * D + (mo + 1) * 128],
                        hb[:, fc * T:(fc + 1) * T],
                        start=(fc == 0), stop=False), sem=pe)
                p.op("tensor", lambda e_, mo=mo, i=i, bank=bank: e_.matmul(
                    bank[:, 0:T], b2row[:, mo * 128:(mo + 1) * 128],
                    w0row[:, i * T:(i + 1) * T], start=False, stop=True), sem=pe)
                v_mm = pe.n
                p.wait("vector", pe, v_mm)
                p.wait("vector", la, v_la)
                v_a = p.op("vector", lambda e_, mo=mo, bank=bank: e_.tensor_add(
                    accb[:, mo * T:(mo + 1) * T], accb[:, mo * T:(mo + 1) * T],
                    bank[:, 0:T]), sem=dve)
                psum_free[nb] = v_a
            v_done = dve.n
            p.wait("sync", dve, v_done)
            p.dma("sync", o_acc[:, i * T:(i + 1) * T].rearrange("(c p) t -> p c t", p=128),
                  accb[:].rearrange("p (c t) -> p c t", c=NCHUNK), sa, gate=True)
            sval[i] = sa.n
            comp[i] = pe.n

    p.finish(sa, sa.n)
    p.end_phase()
    p.pop()
    return p



def build_ph1(p=None, rename=None):
    standalone = p is None
    if standalone:
        p = Prog('ph1')
    p.push(rename)
    p.begin_phase('p1_')
    T = 512
    NTILE = NT // T
    x1 = p.din("x1_fm", (D, NT), F32)
    wg = p.din("Wg", (D, 8), F32)
    o_xn1 = p.dout("xn1_bf", (D, NT), BF16)
    o_g1 = p.dout("g1_row", (1, NT), F32)
    o_g2 = p.dout("g2_row", (1, NT), F32)
    o_pos = p.dout("pos_incl", (1, 2 * NT), F32)
    o_cnt = p.dout("counts", (8, 1), F32)
    o_ohE = p.dout("ohE_out", (8, 2 * NT), F32)
    o_gE = p.dout("gE_out", (1, 2 * NT), F32)

    idf = p.const("c_idf", np.eye(128, dtype=np.float32))

    xs = [p.sb("x0", (128, NCHUNK * T), F32)] * 2
    xn = p.sb("xn", (128, NCHUNK * T), F32)
    sq = p.sb("sq", (128, 2 * T), F32)
    out_bf = p.sb("out_bf", (128, NCHUNK * T), BF16)
    ones_f = p.sb("ones_f", (128, 1), F32)
    onesr_f = p.sb("onesr_f", (1, 128), F32)
    cvals = p.sb("cvals", (128, 1), F32)
    wg_sb = p.sb("wg_sb", (128, NCHUNK * 8), F32)
    id_f = p.sb("id_f", (128, 128), F32)
    stat = p.sb("stat", (1, 4 * T), F32)
    sc_sb = p.sb("sc_sb", (8, T), F32)
    sc_tok = p.sb("sc_tok", (128, 8), F32)
    mx8 = p.sb("mx8", (128, 8), F32)
    oh_tok = p.sb("oh_tok", (128, 16), F32)
    gg = p.sb("gg", (128, 4), F32)
    g1_sb = p.sb("g1_sb", (1, NT), F32)
    g2_sb = p.sb("g2_sb", (1, NT), F32)
    ohEc = p.sb("ohEc", (8, 2 * NT), F32)
    ohC = p.sb("ohC", (8, 2 * NT), F32)
    gEr = p.sb("gEr", (1, 2 * NT), F32)
    cnt_sb = p.sb("cnt_sb", (8, 1), F32)
    pos_sb = p.sb("pos_sb", (1, 512), F32)

    ps_mu = p.ps("ps_mu", (1, 512))
    ps_sq = p.ps("ps_sq", (1, 512))
    ps_b1 = p.ps("ps_b1", (128, 512))
    ps_b2 = p.ps("ps_b2", (128, 512))
    ps_sc = p.ps("ps_sc", (8, 512))
    ps_tr = p.ps("ps_tr", (128, 512))

    ld = p.sem("ld")
    lx = [p.sem("lx0"), p.sem("lx1")]
    pe = p.sem("pe")
    dve = p.stream_sem("vector")
    act = p.stream_sem("scalar")
    gp = p.stream_sem("gpsimd")
    st = p.sem("st")
    sxn = p.sem("sxn")

    p.dma("sync", wg_sb[:].rearrange("p (c e) -> p c e", e=8),
          wg.rearrange("(c p) e -> p c e", p=128), ld)
    p.dma("sync", id_f[:], idf, ld)
    v_ld = ld.n
    p.op("gpsimd", lambda e: e.memset(ones_f[:], 1.0), sem=gp)
    p.op("gpsimd", lambda e: e.memset(onesr_f[:], 1.0), sem=gp)
    p.op("gpsimd", lambda e: e.memset(cvals[:, 0:1], 1e-5), sem=gp)
    v_gp = gp.n
    p.wait("tensor", gp, v_gp)
    p.wait("tensor", ld, v_ld)
    p.wait("scalar", gp, v_gp)
    p.wait("vector", ld, v_ld)

    MU, RSTD, MR, TMP = range(4)

    def srow(k):
        return stat[:, k * T:(k + 1) * T]

    comp_done = [0] * NTILE
    sq_guard = [0] * NTILE
    ps_tr_free = [0]

    for i in range(NTILE):
        xb = xs[0]
        if i >= 1:
            p.wait("sync", dve, comp_done[i - 1])
        v_lx = p.dma("sync", xb[:].rearrange("p (c t) -> p c t", c=NCHUNK),
                     x1[:, i * T:(i + 1) * T].rearrange("(c p) t -> p c t", p=128),
                     lx[0])
        p.wait("scalar", lx[0], v_lx)
        p.wait("tensor", lx[0], v_lx)
        sq_cons = [0, 0]
        for c in range(NCHUNK):
            if c >= 2:
                p.wait("scalar", pe, sq_cons[c % 2])
            p.op("scalar", lambda e, c=c, xb=xb: e.activation(
                sq[:, (c % 2) * T:(c % 2 + 1) * T], xb[:, c * T:(c + 1) * T],
                AF.Square), sem=act)
            v_sq = act.n
            p.op("tensor", lambda e, c=c, xb=xb: e.matmul(
                ps_mu[:, 0:T], ones_f[:], xb[:, c * T:(c + 1) * T],
                start=(c == 0), stop=(c == NCHUNK - 1)))
            p.wait("tensor", act, v_sq)
            p.op("tensor", lambda e, c=c: e.matmul(
                ps_sq[:, 0:T], ones_f[:], sq[:, (c % 2) * T:(c % 2 + 1) * T],
                start=(c == 0), stop=(c == NCHUNK - 1)), sem=pe)
            sq_cons[c % 2] = pe.n
        v_stats = pe.n
        sq_guard[i] = v_stats
        p.wait("vector", pe, v_stats)
        p.op("vector", lambda e: e.tensor_scalar_mul(srow(MU), ps_mu[:, 0:T], 1.0 / D), sem=dve)
        p.op("vector", lambda e: e.tensor_scalar_mul(srow(TMP), ps_sq[:, 0:T], 1.0 / D), sem=dve)
        p.op("vector", lambda e: e.tensor_mul(srow(RSTD), srow(MU), srow(MU)), sem=dve)
        v_var = p.op("vector", lambda e: e.tensor_sub(srow(TMP), srow(TMP), srow(RSTD)),
                     sem=dve)
        p.wait("scalar", dve, v_var)
        p.op("scalar", lambda e: e.activation(srow(RSTD), srow(TMP), AF.Ln,
                                              bias=cvals[0:1, 0:1]), sem=act)
        v_rstd = p.op("scalar", lambda e: e.activation(srow(RSTD), srow(RSTD), AF.Exp,
                                                       scale=-0.5), sem=act)
        p.wait("vector", act, v_rstd)
        v_mr = p.op("vector", lambda e: e.tensor_mul(srow(MR), srow(MU), srow(RSTD)), sem=dve)
        p.wait("tensor", dve, v_mr)
        p.op("tensor", lambda e: e.matmul(ps_b1[:, 0:T], onesr_f[:], srow(RSTD),
                                          start=True, stop=True))
        p.op("tensor", lambda e: e.matmul(ps_b2[:, 0:T], onesr_f[:], srow(MR),
                                          start=True, stop=True), sem=pe)
        v_bc = pe.n
        p.wait("vector", pe, v_bc)
        if i > 0:
            p.wait("vector", sxn, sxn.n)
        for c in range(NCHUNK):
            p.op("vector", lambda e, c=c, xb=xb: e.tensor_mul(
                xn[:, c * T:(c + 1) * T], xb[:, c * T:(c + 1) * T], ps_b1[:, 0:T]), sem=dve)
            p.op("vector", lambda e, c=c: e.tensor_sub(
                xn[:, c * T:(c + 1) * T], xn[:, c * T:(c + 1) * T], ps_b2[:, 0:T]), sem=dve)
            p.op("vector", lambda e, c=c: e.tensor_copy(
                out_bf[:, c * T:(c + 1) * T], xn[:, c * T:(c + 1) * T]), sem=dve)
        v_xn = dve.n
        p.wait("sync", dve, v_xn)
        p.dma("sync", o_xn1[:, i * T:(i + 1) * T].rearrange("(c p) t -> p c t", p=128),
              out_bf[:].rearrange("p (c t) -> p c t", c=NCHUNK), sxn)

        p.wait("tensor", dve, v_xn)
        for c in range(NCHUNK):
            p.op("tensor", lambda e, c=c: e.matmul(
                ps_sc[:, 0:T], wg_sb[:, c * 8:(c + 1) * 8], xn[:, c * T:(c + 1) * T],
                start=(c == 0), stop=(c == NCHUNK - 1)), sem=pe)
        v_sc = pe.n
        p.wait("vector", pe, v_sc)
        v_scsb = p.op("vector", lambda e: e.tensor_copy(sc_sb[:], ps_sc[:, 0:T]), sem=dve)

        for s in range(4):
            p.wait("tensor", dve, max(v_scsb, ps_tr_free[0]))
            p.op("tensor", lambda e, s=s: e.transpose(
                ps_tr[:, 0:8], sc_sb[:, s * 128:(s + 1) * 128], id_f[0:8, 0:8]), sem=pe)
            v_tr = pe.n
            p.wait("vector", pe, v_tr)
            v_tok = p.op("vector", lambda e: e.tensor_copy(sc_tok[:], ps_tr[:, 0:8]), sem=dve)
            v_mx = p.op("vector", lambda e: e.max(mx8[:], sc_tok[:]), sem=dve)
            p.wait("vector", dve, v_mx)
            p.op("vector", lambda e: e.tensor_single_scalar(
                oh_tok[:, 0:8], sc_tok[:], mx8[:, 0:1], ALU.is_equal), sem=dve)
            v_oh = p.op("vector", lambda e: e.tensor_single_scalar(
                oh_tok[:, 8:16], sc_tok[:], mx8[:, 1:2], ALU.is_equal), sem=dve)
            p.op("vector", lambda e: e.tensor_sub(gg[:, 2:3], mx8[:, 0:1], mx8[:, 1:2]),
                 sem=dve)
            v_d = p.op("vector", lambda e: e.tensor_sub(gg[:, 3:4], mx8[:, 1:2], mx8[:, 0:1]),
                       sem=dve)
            p.wait("scalar", dve, v_d)
            p.op("scalar", lambda e: e.activation(gg[:, 0:1], gg[:, 2:3], AF.Sigmoid), sem=act)
            v_gs = p.op("scalar", lambda e: e.activation(gg[:, 1:2], gg[:, 3:4], AF.Sigmoid),
                        sem=act)
            p.wait("tensor", dve, v_oh)
            p.wait("tensor", act, v_gs)
            col0 = i * T + s * 128
            p.op("tensor", lambda e: e.transpose(ps_tr[0:8, 0:128], oh_tok[:, 0:8],
                                                 id_f[:]), sem=pe)
            v_tr1 = pe.n
            p.wait("vector", pe, v_tr1)
            v_c1 = p.op("vector", lambda e, col0=col0: e.tensor_copy(
                ohEc[0:8, 2 * col0:2 * col0 + 256].rearrange(
                    "p (t k) -> p t k", k=2)[:, :, 0], ps_tr[0:8, 0:128]), sem=dve)
            p.wait("tensor", dve, v_c1)
            p.op("tensor", lambda e: e.transpose(ps_tr[0:8, 0:128], oh_tok[:, 8:16],
                                                 id_f[:]), sem=pe)
            v_tr2 = pe.n
            p.wait("vector", pe, v_tr2)
            v_c2 = p.op("vector", lambda e, col0=col0: e.tensor_copy(
                ohEc[0:8, 2 * col0:2 * col0 + 256].rearrange(
                    "p (t k) -> p t k", k=2)[:, :, 1], ps_tr[0:8, 0:128]), sem=dve)
            p.wait("tensor", dve, v_c2)
            p.op("tensor", lambda e: e.transpose(ps_tr[0:1, 0:128], gg[:, 0:1], id_f[:]),
                 sem=pe)
            v_trg = pe.n
            p.wait("vector", pe, v_trg)
            v_g1 = p.op("vector", lambda e, col0=col0: e.tensor_copy(
                g1_sb[:, col0:col0 + 128], ps_tr[0:1, 0:128]), sem=dve)
            p.wait("tensor", dve, v_g1)
            p.op("tensor", lambda e: e.transpose(ps_tr[0:1, 0:128], gg[:, 1:2], id_f[:]),
                 sem=pe)
            v_trg2 = pe.n
            p.wait("vector", pe, v_trg2)
            v_last = p.op("vector", lambda e, col0=col0: e.tensor_copy(
                g2_sb[:, col0:col0 + 128], ps_tr[0:1, 0:128]), sem=dve)
            ps_tr_free[0] = v_last
        comp_done[i] = dve.n

    v_all = dve.n
    p.wait("vector", dve, v_all)
    p.op("vector", lambda e: e.tensor_copy(
        gEr[:].rearrange("p (t k) -> p t k", k=2)[:, :, 0], g1_sb[:]), sem=dve)
    p.op("vector", lambda e: e.tensor_copy(
        gEr[:].rearrange("p (t k) -> p t k", k=2)[:, :, 1], g2_sb[:]), sem=dve)
    v_scan = p.op("vector", lambda e: e.tensor_tensor_scan(
        ohC[:], ohEc[:], ohEc[:], 0.0, ALU.add, ALU.bypass), sem=dve)
    p.op("vector", lambda e: e.tensor_copy(cnt_sb[:], ohC[:, 2 * NT - 1:2 * NT]),
         sem=dve)
    v_prod = p.op("vector", lambda e: e.tensor_mul(ohC[:], ohEc[:], ohC[:]), sem=dve)
    p.wait("tensor", dve, v_prod)
    for g in range(2 * NT // 512):
        p.wait("tensor", st, st.n)
        p.op("tensor", lambda e, g=g: e.matmul(
            ps_mu[:, 0:512], ones_f[0:8, :], ohC[:, g * 512:(g + 1) * 512],
            start=True, stop=True), sem=pe)
        v_pm = pe.n
        p.wait("vector", pe, v_pm)
        v_pc = p.op("vector", lambda e, g=g: e.tensor_copy(
            pos_sb[:], ps_mu[:, 0:512]), sem=dve)
        p.wait("sync", dve, v_pc)
        p.dma("sync", o_pos[:, g * 512:(g + 1) * 512], pos_sb[:], st, gate=True)
    v_pos = dve.n
    p.wait("sync", dve, v_pos)
    p.dma("sync", o_g1, g1_sb[:], st, gate=False)
    p.dma("sync", o_g2, g2_sb[:], st, gate=False)
    p.dma("sync", o_cnt, cnt_sb[:], st, gate=True)
    p.dma("sync", o_ohE, ohEc[:], st, gate=True)
    p.dma("sync", o_gE, gEr[:], st, gate=True)
    p.wait("sync", sxn, sxn.n)
    p.finish(st, st.n)
    p.end_phase()
    p.pop()
    return p



CAP1 = 8602


def build_wm1(p=None, rename=None):
    standalone = p is None
    if standalone:
        p = Prog('wm1')
    p.push(rename)
    p.begin_phase('wm_')
    counts_all = p.din("counts_all", (8, 8), F32)
    selcol = p.din("selcol", (8, 1), F32)
    pos = p.din("pos_incl", (1, 2 * NT), F32)
    ohE_in = p.din("ohE_in", (8, 2 * NT), F32)
    gE_in = p.din("gE_in", (1, 2 * NT), F32)
    o_wm = p.dout("wm_fm", (8, NT), F32)

    tril8 = p.const("c_tril8", np.tril(np.ones((8, 8), np.float32), -1).T)

    ca_sb = p.sb("ca_sb", (8, 8), F32)
    sel_sb = p.sb("sel_sb", (8, 1), F32)
    tri_sb = p.sb("tri_sb", (8, 8), F32)
    pre_sb = p.sb("pre_sb", (8, 8), F32)
    offc_sb = p.sb("offc_sb", (8, 1), F32)
    ohE = p.sb("ohE", (8, 2 * NT), F32)
    pos_sb = p.sb("pos_sb", (1, 512), F32)
    offe_sb = p.sb("offe_sb", (1, 512), F32)
    wE_sb = p.sb("wE", (1, 2 * NT), F32)
    wtmp = p.sb("wtmp", (8, 512), F32)
    g_sb = p.sb("g_sb", (1, 2 * NT), F32)
    onesr = p.sb("onesr", (1, 8), F32)
    wm_sb = p.sb("wm_sb", (8, NT), F32)

    ps_a = p.ps("ps_a", (128, 512))
    ps_b = p.ps("ps_b", (128, 512))

    ld = p.sem("ld")
    lp = p.sem("lp")
    pe = p.sem("pe")
    dve = p.stream_sem("vector")
    gp = p.stream_sem("gpsimd")
    st = p.sem("st")

    p.dma("sync", ca_sb[:], counts_all, ld, gate=False)
    p.dma("sync", sel_sb[:], selcol, ld, gate=False)
    p.dma("sync", tri_sb[:], tril8, ld, gate=False)
    p.dma("sync", ohE[:], ohE_in, ld, gate=False)
    p.dma("sync", g_sb[:], gE_in, ld, gate=False)
    v_ld = ld.n
    p.op("gpsimd", lambda e: e.memset(onesr[:], 1.0), sem=gp)
    v_gp = gp.n
    p.wait("tensor", ld, v_ld)
    p.wait("tensor", gp, v_gp)
    p.wait("vector", ld, v_ld)

    p.op("tensor", lambda e: e.matmul(ps_a[0:8, 0:8], tri_sb[:], ca_sb[:],
                                      start=True, stop=True), sem=pe)
    v_pre = pe.n
    p.wait("vector", pe, v_pre)
    v_prc = p.op("vector", lambda e: e.tensor_copy(pre_sb[:], ps_a[0:8, 0:8]), sem=dve)
    p.wait("tensor", dve, v_prc)
    p.op("tensor", lambda e: e.matmul(ps_b[0:8, 0:1], pre_sb[:], sel_sb[:],
                                      start=True, stop=True), sem=pe)
    v_off = pe.n
    p.wait("vector", pe, v_off)
    v_ofc = p.op("vector", lambda e: e.tensor_copy(offc_sb[:], ps_b[0:8, 0:1]), sem=dve)


    pos_cons = [0]
    for g in range(2 * NT // 512):
        p.wait("tensor", dve, dve.n)
        p.op("tensor", lambda e, g=g: e.matmul(
            ps_a[0:1, 0:512], offc_sb[:], ohE[:, g * 512:(g + 1) * 512],
            start=True, stop=True), sem=pe)
        v_oe = pe.n
        p.wait("sync", dve, pos_cons[0])
        v_lp = p.dma("sync", pos_sb[:], pos[:, g * 512:(g + 1) * 512], lp, gate=True)
        p.wait("vector", pe, v_oe)
        p.wait("vector", lp, v_lp)
        v_pa = p.op("vector", lambda e: e.tensor_add(
            offe_sb[:], pos_sb[:], ps_a[0:1, 0:512]), sem=dve)
        pos_cons[0] = v_pa
        p.op("vector", lambda e: e.tensor_single_scalar(
            offe_sb[:], offe_sb[:], float(CAP1), ALU.is_le), sem=dve)
        v_w = p.op("vector", lambda e, g=g: e.tensor_mul(
            wE_sb[:, g * 512:(g + 1) * 512], g_sb[:, g * 512:(g + 1) * 512],
            offe_sb[:]), sem=dve)

    wEv = wE_sb[:].rearrange("p (t k) -> p t k", k=2)
    for g in range(NT // 512):
        sl = slice(g * 512, (g + 1) * 512)
        p.wait("tensor", dve, dve.n)
        p.op("tensor", lambda e, g=g, wEv=wEv: e.matmul(
            ps_a[0:8, 0:512], onesr[:], wEv[:, g * 512:(g + 1) * 512, 0],
            start=True, stop=True), sem=pe)
        v_b1 = pe.n
        p.op("tensor", lambda e, g=g, wEv=wEv: e.matmul(
            ps_b[0:8, 0:512], onesr[:], wEv[:, g * 512:(g + 1) * 512, 1],
            start=True, stop=True), sem=pe)
        v_b2 = pe.n
        oh1v = ohE[0:8, 2 * g * 512:2 * (g + 1) * 512].rearrange(
            "p (t k) -> p t k", k=2)[:, :, 0]
        oh2v = ohE[0:8, 2 * g * 512:2 * (g + 1) * 512].rearrange(
            "p (t k) -> p t k", k=2)[:, :, 1]
        p.wait("vector", pe, v_b1)
        p.op("vector", lambda e, sl=sl, oh1v=oh1v: e.tensor_mul(
            wm_sb[:, sl], oh1v, ps_a[0:8, 0:512]), sem=dve)
        p.wait("vector", pe, v_b2)
        p.op("vector", lambda e, sl=sl, oh2v=oh2v: e.tensor_mul(
            pos_sb[0:8, 0:512] if False else wtmp[:, 0:512], oh2v, ps_b[0:8, 0:512]),
            sem=dve)
        p.op("vector", lambda e, sl=sl: e.tensor_add(
            wm_sb[:, sl], wm_sb[:, sl], wtmp[:, 0:512]), sem=dve)
    v_wm = dve.n
    p.wait("sync", dve, v_wm)
    p.dma("sync", o_wm, wm_sb[:], st, gate=True)
    p.finish(st, st.n)
    p.end_phase()
    p.pop()
    return p



def _attn1_slots(qi):
    return [8, 9] + list(range(qi, qi + 10))


def _attn1_masked(qi):
    out = []
    for s, ck in enumerate(_attn1_slots(qi)):
        if s < 2 or ck <= 9 or ck in (qi, qi + 1, qi + 9):
            out.append(s)
    return out


def build_attn1(p=None, rename=None):
    import os
    ASTAGE = int(os.environ.get("ATT_STAGE", "99"))
    standalone = p is None
    if standalone:
        p = Prog('attn1')
    p.push(rename)
    p.begin_phase('a1_')
    KC = 5248
    NKT = 41
    NKV = 4
    xn1 = p.din("xn1_bf", (D, NT), BF16)
    x1 = p.din("x1_fm", (D, NT), F32)
    xprev = p.din("xprev_fm", (D, 1028), F32)
    Wq = p.din("Wq", (D, D), F32)
    Wk = p.din("Wk", (D, 512), F32)
    Wv = p.din("Wv", (D, 512), F32)
    Wo = p.din("Wo", (D, D), F32)
    hd3T = p.din("hd3T", (3, 8), F32)
    hscc = p.din("hscc", (1, 8), F32)
    cq2 = p.din("cosq2", (128, NT), F32)
    sq2 = p.din("sinq2", (128, NT), F32)
    ck2 = p.din("cosk2", (128, KC), F32)
    sk2 = p.din("sink2", (128, KC), F32)
    addm = p.din("addm", (32 * 12, 128 * 128), F32)
    q_scr = p.dint("q_scr", (D, NT), BF16)
    o_scr = p.dint("o_scr", (D, NT), BF16)
    o_x2a = p.dout("x2a_fm", (D, NT), F32)

    wsh = p.sb("wsh", (128, 8 * D), BF16)
    wk_sb = wsh
    wv_sb = None
    hd3_sb = p.sb("hd3_sb", (3, 8), BF16)
    hsc_sb = p.sb("hsc_sb", (1, 8), F32)
    kh = [p.sb(f"kh{j}", (128, KC), BF16) for j in range(NKV)]
    vt = [p.sb(f"vt{j}", (128, NKT * 132), BF16) for j in range(NKV)]
    xpn = p.sb("xpn", (128, NCHUNK * 1152), BF16)
    xlb = p.sb("xlb", (128, NCHUNK * 512), BF16)
    qh = p.sb("qh", (128, NT), BF16)
    ohd = p.sb("ohd", (128, NT), BF16)
    rc = p.sb("rc", (128, 512), BF16)
    rs = p.sb("rs", (128, 512), BF16)
    rt1 = p.sb("rt1", (128, 512), BF16)
    rt2 = p.sb("rt2", (128, 512), BF16)
    adm = [p.sb("adm0", (128, 12 * 128), F32)] * 2
    kpr = p.sb("kpr", (1, 1536), BF16)
    qpr = p.sb("qpr", (1, 128), BF16)
    mcol = p.sb("mcol", (128, 8), F32)
    ptok = p.sb("ptok", (128, 1536), BF16)
    stat = p.sb("stat", (1, 4 * 512), F32)
    ones_f = p.sb("ones_f", (128, 1), F32)
    onesr_f = p.sb("onesr_f", (1, 128), F32)
    cvals = p.sb("cvals", (128, 1), F32)
    id_bf = p.sb("id_bf", (128, 128), BF16)
    xa = p.sb("xa", (128, NCHUNK * 512), F32)
    xt2 = p.sb("xt2", (128, NCHUNK * 512), F32)
    ob2 = p.sb("ob2", (128, NCHUNK * 512), BF16)

    psS = [p.ps(f"psS{j}", (128, 512)) for j in range(4)]
    psB16 = [p.ps(f"psB16{j}", (128, 1024), BF16) for j in range(2)]
    psAV = p.ps("psAV", (128, 512))

    idb = p.const("c_idb", np.eye(128, dtype=BF16NP))

    ld = p.sem("ld")
    lw = p.sem("lw")
    lq = p.sem("lq")
    lm = [p.sem("lm0"), p.sem("lm1")]
    lx = p.sem("lx")
    lrt = p.sem("lrt")
    pe = p.sem("pe")
    dve = p.stream_sem("vector")
    act = p.stream_sem("scalar")
    gp = p.stream_sem("gpsimd")
    st = p.sem("st")
    so = p.sem("so")

    for dc in range(NCHUNK):
        p.dma("gpsimd", wsh[:, dc * 512:(dc + 1) * 512], Wk[dc * 128:(dc + 1) * 128, :],
              lw, gate=False)
        p.dma("gpsimd", wsh[:, 4096 + dc * 512:4096 + (dc + 1) * 512],
              Wv[dc * 128:(dc + 1) * 128, :], lw, gate=False)
    p.dma("gpsimd", hd3_sb[:], hd3T, lw, gate=False)
    v_lw = lw.n
    p.dma("sync", hsc_sb[:], hscc, ld, gate=False)
    p.dma("sync", id_bf[:], idb, ld, gate=False)
    v_ld = ld.n
    p.op("gpsimd", lambda e: e.memset(ones_f[:], 1.0), sem=gp)
    p.op("gpsimd", lambda e: e.memset(onesr_f[:], 1.0), sem=gp)
    p.op("gpsimd", lambda e: e.memset(cvals[:, 0:1], 1e-5), sem=gp)
    p.op("gpsimd", lambda e: e.memset(
        xpn[:].rearrange("p (c t) -> p c t", c=NCHUNK)[:, :, 1028:1152], 0.0), sem=gp)
    for j in range(NKV):
        p.op("gpsimd", lambda e, j=j: e.memset(
            vt[j][:].rearrange("p (k c) -> p k c", c=132)[:, :, 128:132], 0.0), sem=gp)
        p.op("gpsimd", lambda e, j=j: e.memset(
            vt[j][:].rearrange("p (k c) -> p k c", c=132)[:, :, 128:129], 1.0), sem=gp)
    v_gp = gp.n
    for eng in ("tensor", "vector", "scalar"):
        p.wait(eng, lw, v_lw)
        p.wait(eng, ld, v_ld)
        p.wait(eng, gp, v_gp)

    MU, RSTD, MR, TMP = range(4)
    LNG = [(0, 512), (512, 1024), (1024, 1028)]
    vt_cons = [0, 0]
    kp_cons = [0]
    exp_cons = [0]
    adm_cons = [0, 0]
    pt_cons = [0, 0]
    pt_eng = ["v", "v"]

    def srow(k, w):
        return stat[:, k * 512:k * 512 + w]

    for (g0, g1) in LNG:
        gw = g1 - g0
        p.wait("sync", dve, dve.n)
        v_lp = p.dma("sync", xa[:, 0:NCHUNK * gw].rearrange("p (c t) -> p c t", c=NCHUNK),
                     xprev[:, g0:g1].rearrange("(c p) t -> p c t", p=128), lx, gate=True)
        p.wait("tensor", lx, v_lp)
        p.wait("scalar", lx, v_lp)
        p.wait("scalar", dve, dve.n)
        p.wait("vector", lx, v_lp)
        sq_cons2 = [0, 0]
        for c in range(NCHUNK):
            xc = xa[:, c * gw: (c + 1) * gw]
            sl2 = (c % 2) * 512
            if c >= 2:
                p.wait("scalar", pe, sq_cons2[c % 2])
            p.op("scalar", lambda e, xc=xc, gw=gw, sl2=sl2: e.activation(
                xt2[:, sl2:sl2 + gw], xc, AF.Square), sem=act)
            v_sq = act.n
            p.op("tensor", lambda e, xc=xc, c=c, gw=gw: e.matmul(
                psS[0][0:1, 0:gw], ones_f[:], xc, start=(c == 0), stop=(c == NCHUNK - 1)))
            p.wait("tensor", act, v_sq)
            p.op("tensor", lambda e, c=c, gw=gw, sl2=sl2: e.matmul(
                psS[1][0:1, 0:gw], ones_f[:], xt2[:, sl2:sl2 + gw],
                start=(c == 0), stop=(c == NCHUNK - 1)), sem=pe)
            sq_cons2[c % 2] = pe.n
        v_stats = pe.n
        p.wait("vector", pe, v_stats)
        p.op("vector", lambda e, gw=gw: e.tensor_scalar_mul(
            srow(MU, gw), psS[0][0:1, 0:gw], 1.0 / D), sem=dve)
        p.op("vector", lambda e, gw=gw: e.tensor_scalar_mul(
            srow(TMP, gw), psS[1][0:1, 0:gw], 1.0 / D), sem=dve)
        p.op("vector", lambda e, gw=gw: e.tensor_mul(
            srow(RSTD, gw), srow(MU, gw), srow(MU, gw)), sem=dve)
        v_var = p.op("vector", lambda e, gw=gw: e.tensor_sub(
            srow(TMP, gw), srow(TMP, gw), srow(RSTD, gw)), sem=dve)
        p.wait("scalar", dve, v_var)
        p.op("scalar", lambda e, gw=gw: e.activation(
            srow(RSTD, gw), srow(TMP, gw), AF.Ln, bias=cvals[0:1, 0:1]), sem=act)
        v_rstd = p.op("scalar", lambda e, gw=gw: e.activation(
            srow(RSTD, gw), srow(RSTD, gw), AF.Exp, scale=-0.5), sem=act)
        p.wait("vector", act, v_rstd)
        v_mr = p.op("vector", lambda e, gw=gw: e.tensor_mul(
            srow(MR, gw), srow(MU, gw), srow(RSTD, gw)), sem=dve)
        p.wait("tensor", dve, v_mr)
        p.op("tensor", lambda e, gw=gw: e.matmul(
            psS[2][:, 0:gw], onesr_f[:], srow(RSTD, gw), start=True, stop=True))
        p.op("tensor", lambda e, gw=gw: e.matmul(
            psS[3][:, 0:gw], onesr_f[:], srow(MR, gw), start=True, stop=True), sem=pe)
        v_bc = pe.n
        p.wait("vector", pe, v_bc)
        for c in range(NCHUNK):
            xc = xa[:, c * gw: (c + 1) * gw]
            p.op("vector", lambda e, xc=xc, gw=gw: e.tensor_mul(
                xt2[:, 0:gw], xc, psS[2][:, 0:gw]), sem=dve)
            p.op("vector", lambda e, c=c, g0=g0, g1=g1, gw=gw: e.tensor_sub(
                xpn[:, c * 1152 + g0: c * 1152 + g1], xt2[:, 0:gw], psS[3][:, 0:gw]),
                sem=dve)
    v_xpn = dve.n
    p.wait("tensor", dve, v_xpn)

    groups = [("p", 0, 512, 0), ("p", 512, 512, 512), ("p", 1024, 128, 1024)]
    for gl in range(8):
        groups.append(("l", gl * 512, 512, 1152 + gl * 512))
    for (src, soff, gw, ooff) in groups:
        p.wait("gpsimd", dve, dve.n)
        vrt = p.dma("gpsimd", rc[:, 0:gw], ck2[:, ooff:ooff + gw], lrt, gate=False)
        vrt = p.dma("gpsimd", rs[:, 0:gw], sk2[:, ooff:ooff + gw], lrt, gate=False)
        if src == "l":
            p.wait("sync", pe, pe.n)
            vxl = p.dma("sync", xlb[:].rearrange("p (c t) -> p c t", c=NCHUNK),
                        xn1[:, soff:soff + gw].rearrange("(c p) t -> p c t", p=128),
                        lx, gate=True)
            p.wait("tensor", lx, vxl)
        p.wait("vector", lrt, lrt.n)
        for m in range(NKV):
            for dc in range(NCHUNK):
                rhs = (xpn[:, dc * 1152 + soff: dc * 1152 + soff + gw] if src == "p"
                       else xlb[:, dc * 512:dc * 512 + gw])
                p.op("tensor", lambda e, m=m, dc=dc, rhs=rhs, gw=gw: e.matmul(
                    psS[0][:, 0:gw], wsh[:, dc * 512 + m * 128:dc * 512 + (m + 1) * 128],
                    rhs, start=(dc == 0), stop=(dc == NCHUNK - 1)))
                p.op("tensor", lambda e, m=m, dc=dc, rhs=rhs, gw=gw: e.matmul(
                    psS[1][:, 0:gw],
                    wsh[:, 4096 + dc * 512 + m * 128:4096 + dc * 512 + (m + 1) * 128],
                    rhs, start=(dc == 0), stop=(dc == NCHUNK - 1)), sem=pe)
            v_mm = pe.n
            p.wait("vector", pe, v_mm)
            p.op("vector", lambda e, gw=gw: e.tensor_copy(rt1[:, 0:gw], psS[0][:, 0:gw]),
                 sem=dve)
            p.op("vector", lambda e, gw=gw: e.tensor_scalar_mul(
                rt2[0:64, 0:gw], rt1[64:128, 0:gw], -1.0), sem=dve)
            p.op("vector", lambda e, gw=gw: e.tensor_copy(
                rt2[64:128, 0:gw], rt1[0:64, 0:gw]), sem=dve)
            p.op("vector", lambda e, gw=gw: e.tensor_mul(
                rt1[:, 0:gw], rt1[:, 0:gw], rc[:, 0:gw]), sem=dve)
            p.op("vector", lambda e, gw=gw: e.tensor_mul(
                rt2[:, 0:gw], rt2[:, 0:gw], rs[:, 0:gw]), sem=dve)
            v_k = p.op("vector", lambda e, m=m, ooff=ooff, gw=gw: e.tensor_add(
                kh[m][:, ooff:ooff + gw], rt1[:, 0:gw], rt2[:, 0:gw]), sem=dve)
            p.wait("scalar", pe, v_mm)
            p.op("scalar", lambda e, gw=gw: e.activation(
                ob2[:, 0:gw], psS[1][:, 0:gw], AF.Copy), sem=act)
            v_vev = act.n
            p.wait("tensor", act, v_vev)
            p.wait("tensor", dve, v_k)
            for sc in range(gw // 128):
                ck = (ooff + sc * 128) // 128
                pb = psB16[sc % 2]
                if sc >= 2:
                    p.wait("tensor", dve, vt_cons[sc % 2])
                p.op("tensor", lambda e, sc=sc: e.transpose(
                    psB16[sc % 2][:, 0:128],
                    ob2[:, sc * 128:(sc + 1) * 128], id_bf[:]), sem=pe)
                v_tr = pe.n
                p.wait("vector", pe, v_tr)
                vt_cons[sc % 2] = p.op("vector", lambda e, m=m, ck=ck, sc=sc: e.tensor_copy(
                    vt[m][:, ck * 132:ck * 132 + 128], psB16[sc % 2][:, 0:128]), sem=dve)
            p.wait("tensor", dve, dve.n)
    v_kv = dve.n

    if ASTAGE < 1:
        p.wait("sync", dve, dve.n)
        p.finish(st, max(st.n, 16))
        return p
    p.wait("gpsimd", pe, pe.n)
    for dc in range(NCHUNK):
        p.dma("gpsimd", wsh[:, dc * D:(dc + 1) * D], Wq[dc * 128:(dc + 1) * 128, :],
              lw, gate=False)
    v_lw2 = lw.n
    p.wait("tensor", lw, v_lw2)
    for gq in range(8):
        ooff = gq * 512
        p.wait("gpsimd", dve, dve.n)
        p.dma("gpsimd", rc[:, 0:512], cq2[:, ooff:ooff + 512], lrt, gate=False)
        p.dma("gpsimd", rs[:, 0:512], sq2[:, ooff:ooff + 512], lrt, gate=False)
        p.wait("sync", pe, pe.n)
        vxl = p.dma("sync", xlb[:].rearrange("p (c t) -> p c t", c=NCHUNK),
                    xn1[:, ooff:ooff + 512].rearrange("(c p) t -> p c t", p=128),
                    lx, gate=True)
        p.wait("tensor", lx, vxl)
        p.wait("vector", lrt, lrt.n)
        for m in range(8):
            p.wait("tensor", dve, dve.n)
            for dc in range(NCHUNK):
                p.op("tensor", lambda e, m=m, dc=dc: e.matmul(
                    psS[0][:, 0:512], wsh[:, dc * D + m * 128:dc * D + (m + 1) * 128],
                    xlb[:, dc * 512:(dc + 1) * 512],
                    start=(dc == 0), stop=(dc == NCHUNK - 1)), sem=pe)
            v_mm = pe.n
            p.wait("vector", pe, v_mm)
            p.op("vector", lambda e: e.tensor_copy(rt1[:, 0:512], psS[0][:, 0:512]), sem=dve)
            p.op("vector", lambda e: e.tensor_scalar_mul(
                rt2[0:64, 0:512], rt1[64:128, 0:512], -1.0), sem=dve)
            p.op("vector", lambda e: e.tensor_copy(
                rt2[64:128, 0:512], rt1[0:64, 0:512]), sem=dve)
            p.op("vector", lambda e: e.tensor_mul(rt1[:, 0:512], rt1[:, 0:512],
                                                  rc[:, 0:512]), sem=dve)
            p.op("vector", lambda e: e.tensor_mul(rt2[:, 0:512], rt2[:, 0:512],
                                                  rs[:, 0:512]), sem=dve)
            v_q = p.op("vector", lambda e: e.tensor_add(rt1[:, 0:512], rt1[:, 0:512],
                                                        rt2[:, 0:512]), sem=dve)
            p.wait("sync", dve, v_q)
            p.dma("sync", q_scr[m * 128:(m + 1) * 128, ooff:ooff + 512], rt1[:, 0:512],
                  st, gate=True)
            p.wait("vector", st, st.n)
    p.wait("tensor", dve, dve.n)

    p.wait("gpsimd", pe, pe.n)
    for dc in range(NCHUNK):
        p.dma("gpsimd", wsh[:, dc * D:(dc + 1) * D], Wo[dc * 128:(dc + 1) * 128, :],
              lw, gate=False)
    v_lw3 = lw.n

    SC = 1.0 / math.sqrt(128.0)
    for h in range(8 if ASTAGE >= 2 else 0):
        kvh = h // 2
        vq = p.dma("sync", qh[:], q_scr[h * 128:(h + 1) * 128, :], lq, gate=True)
        p.wait("tensor", lq, vq)
        for qi in range(32 if ASTAGE >= 3 else 1):
            mb = 0
            p.wait("sync", dve, adm_cons[0])
            vm = 0
            for ms in _attn1_masked(qi):
                vm = p.dma("sync", adm[0][:, ms * 128:(ms + 1) * 128],
                           addm[qi * 12 + ms:qi * 12 + ms + 1, :].rearrange(
                               "one (q k) -> q (one k)", q=128), lm[0], gate=True)
            slots = _attn1_slots(qi)
            regs = [(1024, 256, 0), (qi * 128, 512, 256), (qi * 128 + 512, 512, 768),
                    (qi * 128 + 1024, 256, 1280)]
            for (koff, kw, coff) in regs:
                p.wait("tensor", dve, kp_cons[0])
                p.op("tensor", lambda e, koff=koff, kw=kw, kvh=kvh, h=h: e.matmul(
                    psAV[0:1, 0:kw], hd3_sb[:, h:h + 1], kh[kvh][0:3, koff:koff + kw],
                    start=True, stop=True), sem=pe)
                v_kp = pe.n
                p.wait("vector", pe, v_kp)
                kp_cons[0] = p.op("vector", lambda e, coff=coff, kw=kw, h=h:
                                  e.tensor_scalar(
                    kpr[:, coff:coff + kw], psAV[0:1, 0:kw],
                    hsc_sb[:, h:h + 1], 1.0 / SC,
                    ALU.mult, ALU.mult), sem=dve)
            p.wait("tensor", dve, kp_cons[0])
            p.op("tensor", lambda e, qi=qi, h=h: e.matmul(
                psAV[0:1, 0:128], hd3_sb[:, h:h + 1], qh[0:3, qi * 128:(qi + 1) * 128],
                start=True, stop=True), sem=pe)
            v_qp = pe.n
            p.wait("vector", pe, v_qp)
            v_qpr = p.op("vector", lambda e: e.tensor_copy(qpr[:], psAV[0:1, 0:128]),
                         sem=dve)
            p.wait("tensor", dve, v_qpr)

            q_t = qh[:, qi * 128:(qi + 1) * 128]
            p.wait("tensor", act, exp_cons[0])
            for ri, (koff, kw, coff) in enumerate(regs):
                p.op("tensor", lambda e, ri=ri, koff=koff, kw=kw, kvh=kvh, q_t=q_t:
                     e.matmul(psS[ri][:, 0:kw], q_t, kh[kvh][:, koff:koff + kw],
                              start=True, stop=False))
                p.op("tensor", lambda e, ri=ri, kw=kw, coff=coff: e.matmul(
                    psS[ri][:, 0:kw], qpr[:], kpr[:, coff:coff + kw],
                    start=False, stop=True), sem=pe)
            v_s = pe.n
            p.wait("vector", pe, v_s)
            p.wait("vector", lm[0], vm)
            for s in _attn1_masked(qi):
                coff = 256 + (s - 2) * 128 if s >= 2 else s * 128
                ri, off = (0, coff) if coff < 256 else \
                          (1, coff - 256) if coff < 768 else \
                          (2, coff - 768) if coff < 1280 else (3, coff - 1280)
                p.op("vector", lambda e, ri=ri, off=off, s=s: e.tensor_tensor(
                    psS[ri][:, off:off + 128], psS[ri][:, off:off + 128],
                    adm[0][:, s * 128:(s + 1) * 128], ALU.add), sem=dve)
            for ri, (koff, kw, coff) in enumerate(regs):
                p.op("vector", lambda e, ri=ri, kw=kw: e.reduce_max(
                    mcol[:, ri:ri + 1], psS[ri][:, 0:kw], axis=AX.X), sem=dve)
            p.op("vector", lambda e: e.tensor_max(mcol[:, 4:5], mcol[:, 0:1],
                                                  mcol[:, 1:2]), sem=dve)
            p.op("vector", lambda e: e.tensor_max(mcol[:, 5:6], mcol[:, 2:3],
                                                  mcol[:, 3:4]), sem=dve)
            p.op("vector", lambda e: e.tensor_max(mcol[:, 6:7], mcol[:, 4:5],
                                                  mcol[:, 5:6]), sem=dve)
            v_m = p.op("vector", lambda e: e.tensor_scalar_mul(
                mcol[:, 7:8], mcol[:, 6:7], -1.0), sem=dve)
            adm_cons[mb] = v_m
            p.wait("scalar", dve, v_m)
            for ri, (koff, kw, coff) in enumerate(regs):
                p.op("scalar", lambda e, ri=ri, kw=kw, coff=coff: e.activation(
                    ptok[:, coff:coff + kw], psS[ri][:, 0:kw], AF.Exp,
                    bias=mcol[:, 7:8]), sem=act)
            v_p = act.n
            exp_cons[0] = v_p
            p.wait("tensor", act, v_p)
            for sc2 in range(12):
                pb2 = sc2 % 2
                if sc2 >= 2:
                    p.wait("tensor", dve if pt_eng[pb2] == "v" else act, pt_cons[pb2])
                p.op("tensor", lambda e, sc2=sc2, pb2=pb2: e.transpose(
                    psB16[pb2][:, 0:128], ptok[:, sc2 * 128:(sc2 + 1) * 128], id_bf[:]),
                    sem=pe)
                v_tr = pe.n
                if sc2 % 2 == 0:
                    p.wait("vector", pe, v_tr)
                    pt_cons[pb2] = p.op("vector", lambda e, sc2=sc2, pb2=pb2: e.tensor_copy(
                        ptok[:, sc2 * 128:(sc2 + 1) * 128], psB16[pb2][:, 0:128]), sem=dve)
                    pt_eng[pb2] = "v"
                else:
                    p.wait("scalar", pe, v_tr)
                    pt_cons[pb2] = p.op("scalar", lambda e, sc2=sc2, pb2=pb2: e.copy(
                        ptok[:, sc2 * 128:(sc2 + 1) * 128], psB16[pb2][:, 0:128]), sem=act)
                    pt_eng[pb2] = "s"
            p.wait("tensor", dve, dve.n)
            p.wait("tensor", act, act.n)
            p.wait("tensor", dve, dve.n)
            slots_ck = [8, 9] + list(range(qi, qi + 10))
            for i2, ck in enumerate(slots_ck):
                p.op("tensor", lambda e, i2=i2, ck=ck, kvh=kvh: e.matmul(
                    psAV[:, 0:132], ptok[:, i2 * 128:(i2 + 1) * 128],
                    vt[kvh][:, ck * 132:(ck + 1) * 132],
                    start=(i2 == 0), stop=(i2 == 11)), sem=pe)
            v_av = pe.n
            p.wait("vector", pe, v_av)
            p.op("vector", lambda e: e.reciprocal(mcol[:, 0:1], psAV[:, 128:129]), sem=dve)
            v_o = p.op("vector", lambda e: e.tensor_scalar_mul(
                ptok[:, 0:128], psAV[:, 0:128], mcol[:, 0:1]), sem=dve)
            p.wait("tensor", dve, v_o)
            p.op("tensor", lambda e: e.transpose(psB16[0][:, 0:128], ptok[:, 0:128],
                                                 id_bf[:]), sem=pe)
            v_tro = pe.n
            p.wait("vector", pe, v_tro)
            v_oc = p.op("vector", lambda e, qi=qi: e.tensor_copy(
                ohd[:, qi * 128:(qi + 1) * 128], psB16[0][:, 0:128]), sem=dve)
            pt_cons[0] = v_oc
            pt_eng[0] = "v"
            kp_cons[0] = v_oc
        p.wait("sync", dve, dve.n)
        p.dma("sync", o_scr[h * 128:(h + 1) * 128, :], ohd[:], so, gate=True)
        p.wait("vector", so, so.n)

    p.wait("tensor", lw, v_lw3)
    p.wait("sync", so, so.n)
    for i in range(NCHUNK):
        T2 = 512
        p.wait("sync", dve, dve.n)
        vxo = p.dma("sync", ob2[:].rearrange("p (c t) -> p c t", c=NCHUNK),
                    o_scr[:, i * T2:(i + 1) * T2].rearrange("(c p) t -> p c t", p=128),
                    lq, gate=True)
        vx1 = p.dma("sync", xa[:].rearrange("p (c t) -> p c t", c=NCHUNK),
                    x1[:, i * T2:(i + 1) * T2].rearrange("(c p) t -> p c t", p=128),
                    lx, gate=True)
        p.wait("tensor", lq, vxo)
        psum_free2 = [0, 0]
        for mo in range(NCHUNK):
            nb = mo % 2
            if psum_free2[nb]:
                p.wait("tensor", dve, psum_free2[nb])
            for dc in range(NCHUNK):
                p.op("tensor", lambda e, mo=mo, dc=dc, nb=nb: e.matmul(
                    psS[nb][:, 0:T2], wsh[:, (dc * 8 + mo) * 128:(dc * 8 + mo + 1) * 128],
                    ob2[:, dc * T2:(dc + 1) * T2],
                    start=(dc == 0), stop=(dc == NCHUNK - 1)), sem=pe)
            v_mm = pe.n
            p.wait("vector", pe, v_mm)
            p.wait("vector", lx, vx1)
            if i > 0 or mo > 1:
                p.wait("vector", st, st.n)
            v_e = p.op("vector", lambda e, mo=mo, nb=nb: e.tensor_add(
                xt2[:, mo * T2:(mo + 1) * T2], xa[:, mo * T2:(mo + 1) * T2],
                psS[nb][:, 0:T2]), sem=dve)
            psum_free2[nb] = v_e
        v_done = dve.n
        p.wait("sync", dve, v_done)
        p.dma("sync", o_x2a[:, i * T2:(i + 1) * T2].rearrange("(c p) t -> p c t", p=128),
              xt2[:].rearrange("p (c t) -> p c t", c=NCHUNK), st, gate=True)

    p.finish(st, st.n)
    p.end_phase()
    p.pop()
    return p



def _attn1_host_consts(core):
    h = core % 2
    p0 = h * NT
    kpos = np.full(5248, -1e9)
    if h == 1:
        kpos[0:1024] = np.arange(3072, 4096)
        kpos[1024:1028] = np.arange(4)
    kpos[1152:5248] = p0 + np.arange(NT)
    qpos = p0 + np.arange(NT).astype(np.float64)
    inv = 1.0 / (10000.0 ** (np.arange(0, 128, 2, dtype=np.float32) / 128))

    def tables(pos):
        pc = pos.astype(np.float32)[None, :] * inv[:, None]
        return np.cos(pc).astype(np.float32), np.sin(pc).astype(np.float32)

    SCALE = np.float32(1.0 / np.sqrt(128.0))
    cq, sq = tables(qpos)
    kposc = np.where(kpos < -1e8, 0, kpos)
    ck_, sk_ = tables(kposc)
    out = {
        "cosq2": np.ascontiguousarray(np.concatenate([cq, cq], 0) * SCALE),
        "sinq2": np.ascontiguousarray(np.concatenate([sq, sq], 0) * SCALE),
        "cosk2": np.ascontiguousarray(np.concatenate([ck_, ck_], 0)),
        "sink2": np.ascontiguousarray(np.concatenate([sk_, sk_], 0)),
    }
    addm = np.zeros((32 * 12, 128 * 128), np.float32)
    for qi in range(32):
        for s, ckk in enumerate(_attn1_slots(qi)):
            pk = kpos[ckk * 128:(ckk + 1) * 128]
            pq = qpos[qi * 128:(qi + 1) * 128]
            causal = pk[None, :] <= pq[:, None]
            sink = (pk[None, :] >= 0) & (pk[None, :] < 4)
            window = pk[None, :] >= pq[:, None] - 1023
            valid = pk[None, :] > -1e8
            if s < 2:
                allowed = causal & sink & valid
            else:
                allowed = causal & window & (~sink) & valid
            addm[qi * 12 + s] = np.where(allowed, 0.0, -1e30).reshape(-1)
            if s not in _attn1_masked(qi) and not allowed.all():
                raise AssertionError(f"mask coverage bug qi={qi} s={s}")
    out["addm"] = addm
    return out



def _col(a):
    return np.ascontiguousarray(np.asarray(a, np.float32).reshape(-1, 1))



def _attn1_host_consts(core):
    h = core % 2
    p0 = h * NT
    kpos = np.full(5248, -1e9)
    if h == 1:
        kpos[0:1024] = np.arange(3072, 4096)
        kpos[1024:1028] = np.arange(4)
    kpos[1152:5248] = p0 + np.arange(NT)
    qpos = p0 + np.arange(NT).astype(np.float64)
    inv = 1.0 / (10000.0 ** (np.arange(0, 128, 2, dtype=np.float32) / 128))

    def tables(pos):
        pc = pos.astype(np.float32)[None, :] * inv[:, None]
        return np.cos(pc).astype(np.float32), np.sin(pc).astype(np.float32)

    SCALE = np.float32(1.0 / np.sqrt(128.0))
    cq, sq = tables(qpos)
    kposc = np.where(kpos < -1e8, 0, kpos)
    ck_, sk_ = tables(kposc)
    out = {
        "cosq2": np.ascontiguousarray(np.concatenate([cq, cq], 0) * SCALE),
        "sinq2": np.ascontiguousarray(np.concatenate([sq, sq], 0) * SCALE),
        "cosk2": np.ascontiguousarray(np.concatenate([ck_, ck_], 0)),
        "sink2": np.ascontiguousarray(np.concatenate([sk_, sk_], 0)),
    }
    addm = np.zeros((32 * 12, 128 * 128), np.float32)
    for qi in range(32):
        for s, ckk in enumerate(_attn1_slots(qi)):
            pk = kpos[ckk * 128:(ckk + 1) * 128]
            pq = qpos[qi * 128:(qi + 1) * 128]
            causal = pk[None, :] <= pq[:, None]
            sink = (pk[None, :] >= 0) & (pk[None, :] < 4)
            window = pk[None, :] >= pq[:, None] - 1023
            valid = pk[None, :] > -1e8
            if s < 2:
                allowed = causal & sink & valid
            else:
                allowed = causal & window & (~sink) & valid
            addm[qi * 12 + s] = np.where(allowed, 0.0, -1e30).reshape(-1)
            if s not in _attn1_masked(qi) and not allowed.all():
                raise AssertionError(f"mask coverage bug qi={qi} s={s}")
    out["addm"] = addm
    return out



def _col(a):
    return np.ascontiguousarray(np.asarray(a, np.float32).reshape(-1, 1))


def kernel(**inputs):
    import functools

    def f(k):
        return np.ascontiguousarray(np.asarray(inputs[k], np.float32))

    x = f("L0_attn_x")
    x_fm = [np.ascontiguousarray(x[c // 2, (c % 2) * NT:(c % 2 + 1) * NT, :].T)
            for c in range(NCORE)]

    cenT = np.ascontiguousarray(f("L0_ffn_centroids").T)
    ln1c = _col(inputs["L0_attn_ln1_scale"])
    r0 = run_prog(build_ph0, [{"x_fm": x_fm[c], "cenT": cenT, "ln1c": ln1c}
                              for c in range(NCORE)])

    qkv_common = {
        "wq": f("L0_attn_wq"), "wk": f("L0_attn_wk"), "wv": f("L0_attn_wv"),
        "wbeta": f("L0_attn_wbeta"), "bq_c": _col(inputs["L0_attn_bq"]),
        "bk_c": _col(inputs["L0_attn_bk"]), "bv_c": _col(inputs["L0_attn_bv"]),
        "bbeta_c": _col(inputs["L0_attn_bbeta"]),
    }
    r1 = run_prog(build_qkv, [{"xh1_bf": r0[c]["xh1_bf"], **qkv_common}
                              for c in range(NCORE)])

    def scat(key, sc):
        b, j = sc // 2, sc % 2
        return np.ascontiguousarray(np.concatenate(
            [r1[2 * b][key][512 * j:512 * (j + 1), :],
             r1[2 * b + 1][key][512 * j:512 * (j + 1), :]], axis=1))

    qs = [scat("q_f", sc) for sc in range(NCORE)]
    ks = [scat("kn_f", sc) for sc in range(NCORE)]
    vs = [scat("v_f", sc) for sc in range(NCORE)]
    bt = [np.ascontiguousarray(np.concatenate(
        [r1[2 * (sc // 2)]["beta_tok"][:, 8 * (sc % 2):8 * (sc % 2) + 8],
         r1[2 * (sc // 2) + 1]["beta_tok"][:, 8 * (sc % 2):8 * (sc % 2) + 8]], axis=0))
        for sc in range(NCORE)]
    z = [np.zeros((512, 64), np.float32)] * NCORE
    o_parts = []
    for ci in range(0, 64, 16):
        bld = functools.partial(build_scan, f"scan{ci}", ci, ci + 16)
        bld.__name__ = f"scan{ci}"
        rs = run_prog(bld, [{"qs": qs[sc], "ks": ks[sc], "vs": vs[sc],
                             "bt": bt[sc], "z_in": z[sc]} for sc in range(NCORE)])
        z = [rs[sc]["z_fin"] for sc in range(NCORE)]
        o_parts.append([rs[sc]["o_fm"] for sc in range(NCORE)])
    o_scan = [sum(np.asarray(o_parts[k][sc], np.float32) for k in range(4))
              for sc in range(NCORE)]
    o_fm = [np.ascontiguousarray(np.concatenate(
        [o_scan[2 * (c // 2)][:, (c % 2) * NT:(c % 2 + 1) * NT],
         o_scan[2 * (c // 2) + 1][:, (c % 2) * NT:(c % 2 + 1) * NT]],
        axis=0).astype(BF16NP)) for c in range(NCORE)]

    ao_common = {
        "wo": f("L0_attn_wo"), "w1": f("L0_attn_w1"), "w2": f("L0_attn_w2"),
        "bo_c": _col(inputs["L0_attn_bo"]), "ln2_c": _col(inputs["L0_attn_ln2_scale"]),
        "b1_c": _col(inputs["L0_attn_b1"]), "b2_c": _col(inputs["L0_attn_b2"]),
    }
    r2 = run_prog(build_attnout, [{"o_fm": o_fm[c], "xn_fm": r0[c]["xn_fm"],
                                   **ao_common} for c in range(NCORE)])

    r3 = run_prog(build_x1init, [{"x_fm": x_fm[c], "acc1_fm": r2[c]["acc1_fm"],
                                  "xn_fm": r0[c]["xn_fm"]} for c in range(NCORE)])
    moe0_common = {
        "W1T": np.ascontiguousarray(f("L0_ffn_W1").transpose(0, 2, 1)),
        "W2T": np.ascontiguousarray(f("L0_ffn_W2").transpose(0, 2, 1)),
        "b1e": f("L0_ffn_b1"), "b2e": f("L0_ffn_b2"),
        "gammae": f("L0_ffn_gamma"), "betae": f("L0_ffn_beta"),
    }
    bld = functools.partial(build_moe, "moe0a", 0, 4, 0)
    bld.__name__ = "moe0a"
    r4 = run_prog(bld, [{"xhat_bf": r0[c]["xhat_bf"], "acc_in": r3[c]["x1acc"],
                         "wm_fm": r0[c]["w0_fm"], **moe0_common} for c in range(NCORE)])
    bld = functools.partial(build_moe, "moe0b", 4, 8, 0)
    bld.__name__ = "moe0b"
    r5 = run_prog(bld, [{"xhat_bf": r0[c]["xhat_bf"], "acc_in": r4[c]["acc_out"],
                         "wm_fm": r0[c]["w0_fm"], **moe0_common} for c in range(NCORE)])
    x1_fm = [r5[c]["acc_out"] for c in range(NCORE)]

    r6 = run_prog(build_ph1, [{"x1_fm": x1_fm[c], "Wg": f("L1_ffn_Wg")}
                              for c in range(NCORE)])
    counts_all = np.ascontiguousarray(
        np.stack([r6[c]["counts"][:, 0] for c in range(NCORE)], axis=0))
    wm_maps = []
    for c in range(NCORE):
        ohE = np.empty((8, 2 * NT), np.float32)
        ohE[:, 0::2] = r6[c]["oh1_fm"]
        ohE[:, 1::2] = r6[c]["oh2_fm"]
        gE = np.empty((1, 2 * NT), np.float32)
        gE[:, 0::2] = r6[c]["g1_row"]
        gE[:, 1::2] = r6[c]["g2_row"]
        sel = np.zeros((8, 1), np.float32)
        sel[c, 0] = 1.0
        wm_maps.append({"counts_all": counts_all, "selcol": sel,
                        "pos_incl": r6[c]["pos_incl"],
                        "oh1_fm": r6[c]["oh1_fm"], "oh2_fm": r6[c]["oh2_fm"],
                        "ohE_in": np.ascontiguousarray(ohE),
                        "gE_in": np.ascontiguousarray(gE)})
    r7 = run_prog(build_wm1, wm_maps)

    at_maps = []
    for c in range(NCORE):
        h = c % 2
        if h == 1:
            part = x1_fm[c - 1]
            xprev = np.ascontiguousarray(
                np.concatenate([part[:, 3072:4096], part[:, 0:4]], axis=1))
        else:
            xprev = np.zeros((D, 1028), np.float32)
        at_maps.append({
            "xn1_bf": r6[c]["xn1_bf"], "x1_fm": x1_fm[c], "xprev_fm": xprev,
            "Wq": f("L1_attn_Wq"), "Wk": f("L1_attn_Wk"), "Wv": f("L1_attn_Wv"),
            "Wo": f("L1_attn_Wo"),
            "hd3T": np.ascontiguousarray(f("L1_attn_head_dirs").T),
            "hscc": np.ascontiguousarray(f("L1_attn_head_scales").reshape(1, 8)),
            **_attn1_host_consts(c),
        })
    r8 = run_prog(build_attn1, at_maps)

    moe1_common = {
        "W1T": f("L1_ffn_W1"), "W2T": f("L1_ffn_W2"),
        "b1e": f("L1_ffn_b1"), "b2e": f("L1_ffn_b2"),
    }
    bld = functools.partial(build_moe, "moe1a", 0, 4, 1)
    bld.__name__ = "moe1a"
    r9 = run_prog(bld, [{"xhat_bf": r6[c]["xn1_bf"], "acc_in": r8[c]["x2a_fm"],
                         "wm_fm": r7[c]["wm_fm"], **moe1_common} for c in range(NCORE)])
    bld = functools.partial(build_moe, "moe1b", 4, 8, 1)
    bld.__name__ = "moe1b"
    r10 = run_prog(bld, [{"xhat_bf": r6[c]["xn1_bf"], "acc_in": r9[c]["acc_out"],
                          "wm_fm": r7[c]["wm_fm"], **moe1_common} for c in range(NCORE)])

    out = np.zeros((4, 8192, D), np.float32)
    for c in range(NCORE):
        out[c // 2, (c % 2) * NT:(c % 2 + 1) * NT, :] = r10[c]["acc_out"].T
    return out



def build_scan2(p=None, rename=None):
    standalone = p is None
    if standalone:
        p = Prog("scan2")
    p.push(rename)
    p.begin_phase("sc_")
    NH = 16
    C = 128
    NCH = 32
    qs = p.din("q_f", (D, NT), BF16)
    ks = p.din("kn_f", (D, NT), BF16)
    vs = p.din("v_f", (D, NT), BF16)
    bt = p.din("beta_tok", (NT, NH), F32)
    oddm = p.din("oddmask", (64, 1), F32)
    o_out = p.dout("o_fm", (D, NT), BF16)
    qt_scr = p.dint("qt_scr", (64, NCH * NH * C), BF16)
    rt_scr = p.dint("rt_scr", (64, NCH * NH * 64), F32)
    zfin = p.dint("zfin", (NH * 64, 64), F32)
    zpair = p.dint("zpair", (2 * NH * 64, 64), F32)

    tri = np.tril(np.ones((128, 128), np.float32), -1).T
    trii = np.triu(np.ones((128, 128), np.float32))
    c_mu = p.const("c_maskU", tri)
    c_mi = p.const("c_maskLi", trii)
    c_idb = p.const("c_idb", np.eye(128, dtype=BF16NP))

    maskU = p.sb("maskU", (128, 128), F32)
    maskLi = p.sb("maskLi", (128, 128), F32)
    id_bf = p.sb("id_bf", (128, 128), BF16)
    kq = [p.sb(f"kq{j}", (64, NH * C), BF16) for j in range(2)]
    kk = [p.sb(f"kk{j}", (64, NH * C), BF16) for j in range(2)]
    kv = [p.sb(f"kv{j}", (64, NH * C), BF16) for j in range(2)]
    bcol = [p.sb(f"bcol{j}", (128, NH), F32) for j in range(2)]
    aT = p.sb("aT", (128, 7 * 128), BF16)
    aN = p.sb("aN", (128, 7 * 128), BF16)
    xw = [p.sb(f"xw{j}", (128, 128), BF16) for j in range(2)]
    kr_t = p.sb("kr_t", (128, 64), BF16)
    w_t = p.sb("w_t", (128, 64), BF16)
    u_t = p.sb("u_t", (128, 64), BF16)
    gt = p.sb("gt", (128, 128), BF16)
    qt_all = p.sb("qt_all", (64, NH * C), BF16)
    rt_all = p.sb("rt_all", (64, NH * 64), F32)
    pm_all = p.sb("pm_all", (64, NH * 64), F32)
    zc = [p.sb(f"z{j}", (64, NH * 64), F32) for j in range(2)]
    z_bf = p.sb("z_bf", (64, 64), BF16)
    ztmp = p.sb("ztmp", (64, 64), F32)
    o_intra = p.sb("o_intra", (128, NH * 64), F32)
    o_tok = p.sb("o_tok", (128, 128), BF16)
    o_stash = p.sb("o_stash", (128, 8 * 128), BF16)
    dmask = p.sb("dmask", (64, 1), F32)

    psA = p.ps("psA", (128, 512))
    psB = p.ps("psB", (128, 512))
    psC = p.ps("psC", (128, 512))
    psT = p.ps("psT", (128, 1024), BF16)
    psz = [p.ps(f"psz{j}", (128, 512)) for j in range(2)]

    ld = p.sem("ld")
    lin = [p.sem("lin0"), p.sem("lin1")]
    pe = p.sem("pe")
    dve = p.stream_sem("vector")
    act = p.stream_sem("scalar")
    gp = p.stream_sem("gpsimd")
    sto = p.sem("sto")
    sstash = p.sem("sstash")
    cc = p.sem("cc")

    p.dma("sync", maskU[:], c_mu, ld, gate=False)
    p.dma("sync", maskLi[:], c_mi, ld, gate=False)
    p.dma("sync", id_bf[:], c_idb, ld, gate=False)
    p.dma("sync", dmask[:], oddm, ld, gate=False)
    v_ld = ld.n
    p.op("gpsimd", lambda e: e.memset(zc[0][:], 0.0), sem=gp)
    v_gp = gp.n
    p.wait("tensor", ld, v_ld)
    p.wait("vector", ld, v_ld)
    p.wait("vector", gp, v_gp)
    p.wait("scalar", ld, v_ld)

    comp_done = [0] * NCH

    for i in range(NCH):
        bb = i % 2
        if i >= 2:
            p.wait("sync", pe, comp_done[i - 2])
        p.dma("sync", kq[bb][:].rearrange("d (h t) -> d h t", h=NH),
              qs[:, i * C:(i + 1) * C].rearrange("(h d) t -> d h t", d=64),
              lin[bb], gate=False)
        p.dma("sync", kk[bb][:].rearrange("d (h t) -> d h t", h=NH),
              ks[:, i * C:(i + 1) * C].rearrange("(h d) t -> d h t", d=64),
              lin[bb], gate=False)
        p.dma("sync", kv[bb][:].rearrange("d (h t) -> d h t", h=NH),
              vs[:, i * C:(i + 1) * C].rearrange("(h d) t -> d h t", d=64),
              lin[bb], gate=False)
        vb = p.dma("sync", bcol[bb][:], bt[i * C:(i + 1) * C, :], lin[bb], gate=False)
        v_in = vb
        p.wait("tensor", lin[bb], v_in)
        p.wait("vector", lin[bb], v_in)
        p.wait("vector", sstash, sstash.n)
        p.wait("scalar", sstash, sstash.n)

        zcur = zc[i % 2]
        znxt = zc[(i + 1) % 2]

        for h in range(NH):
            q_t = kq[bb][:, h * C:(h + 1) * C]
            k_t = kk[bb][:, h * C:(h + 1) * C]
            v_t = kv[bb][:, h * C:(h + 1) * C]
            b_c = bcol[bb][:, h:h + 1]
            p.op("tensor", lambda e, k_t=k_t: e.matmul(
                psA[:, 0:128], k_t, k_t, start=True, stop=True), sem=pe)
            v_smat = pe.n
            p.wait("vector", pe, v_smat)
            v_atT = p.op("vector", lambda e, b_c=b_c: e.scalar_tensor_tensor(
                aT[:, 0:128], maskU[:], b_c, psA[:, 0:128], ALU.mult, ALU.mult),
                sem=dve)
            p.wait("tensor", dve, v_atT)
            p.op("tensor", lambda e: e.transpose(psT[:, 0:128], aT[:, 0:128], id_bf[:]),
                 sem=pe)
            v_trA = pe.n
            p.wait("scalar", pe, v_trA)
            v_a0 = p.op("scalar", lambda e: e.copy(aN[:, 0:128], psT[:, 0:128]), sem=act)
            prev_evac = ("scalar", v_a0)
            for l in range(6):
                p.wait("tensor", act if prev_evac[0] == "scalar" else dve, prev_evac[1])
                p.op("tensor", lambda e, l=l: e.matmul(
                    psB[:, 0:128], aT[:, l * 128:(l + 1) * 128],
                    aN[:, l * 128:(l + 1) * 128], start=True, stop=True))
                p.op("tensor", lambda e, l=l: e.matmul(
                    psC[:, 0:128], aN[:, l * 128:(l + 1) * 128],
                    aT[:, l * 128:(l + 1) * 128], start=True, stop=True), sem=pe)
                v_sq = pe.n
                p.wait("vector", pe, v_sq)
                v_e1 = p.op("vector", lambda e, l=l: e.tensor_copy(
                    aN[:, (l + 1) * 128:(l + 2) * 128], psB[:, 0:128]), sem=dve)
                p.wait("scalar", pe, v_sq)
                v_e2 = p.op("scalar", lambda e, l=l: e.copy(
                    aT[:, (l + 1) * 128:(l + 2) * 128], psC[:, 0:128]), sem=act)
                prev_evac = ("scalar", v_e2)
                p.wait("tensor", dve, v_e1)
            p.wait("tensor", act, prev_evac[1])
            p.op("tensor", lambda e, k_t=k_t: e.transpose(
                psT[:, 0:64], k_t, id_bf[0:64, 0:64]), sem=pe)
            v_tk = pe.n
            p.wait("vector", pe, v_tk)
            v_kr = p.op("vector", lambda e: e.tensor_copy(kr_t[:, 0:64], psT[:, 0:64]),
                        sem=dve)
            v_x0a = p.op("vector", lambda e: e.tensor_copy(xw[0][:, 0:64], psT[:, 0:64]),
                         sem=dve)
            p.wait("tensor", dve, v_x0a)
            p.op("tensor", lambda e, v_t=v_t: e.transpose(
                psT[:, 0:64], v_t, id_bf[0:64, 0:64]), sem=pe)
            v_tv = pe.n
            p.wait("vector", pe, v_tv)
            v_x0b = p.op("vector", lambda e: e.tensor_copy(xw[0][:, 64:128], psT[:, 0:64]),
                         sem=dve)
            p.wait("tensor", dve, v_x0b)
            xi = 0
            for ai, sign in ((6, 1), (5, 1), (4, 1), (3, 1), (2, 1), (1, 1), (0, -1)):
                p.op("tensor", lambda e, ai=ai, xi=xi: e.matmul(
                    psB[:, 0:128], aT[:, ai * 128:(ai + 1) * 128], xw[xi][:],
                    start=True, stop=True), sem=pe)
                v_ap = pe.n
                p.wait("vector", pe, v_ap)
                if sign > 0:
                    v_x = p.op("vector", lambda e, xi=xi: e.tensor_add(
                        xw[1 - xi][:], xw[xi][:], psB[:, 0:128]), sem=dve)
                else:
                    v_x = p.op("vector", lambda e, xi=xi: e.tensor_sub(
                        xw[1 - xi][:], xw[xi][:], psB[:, 0:128]), sem=dve)
                p.wait("tensor", dve, v_x)
                xi = 1 - xi
            v_w = p.op("vector", lambda e, xi=xi, b_c=b_c: e.tensor_scalar_mul(
                w_t[:, 0:64], xw[xi][:, 0:64], b_c), sem=dve)
            v_u = p.op("vector", lambda e, xi=xi, b_c=b_c: e.tensor_scalar_mul(
                u_t[:, 0:64], xw[xi][:, 64:128], b_c), sem=dve)
            p.wait("tensor", dve, v_u)
            p.op("tensor", lambda e, k_t=k_t, q_t=q_t: e.matmul(
                psA[:, 0:128], k_t, q_t, start=True, stop=True), sem=pe)
            v_gm = pe.n
            p.wait("vector", pe, v_gm)
            v_gt = p.op("vector", lambda e: e.tensor_mul(
                gt[:], psA[:, 0:128], maskLi[:]), sem=dve)
            p.wait("tensor", dve, v_gt)
            p.op("tensor", lambda e, h=h: e.matmul(
                psA[:, 0:64], gt[:], u_t[:, 0:64], start=True, stop=True), sem=pe)
            v_oi = pe.n
            p.wait("scalar", pe, v_oi)
            v_oie = p.op("scalar", lambda e, h=h: e.copy(
                o_intra[:, h * 64:(h + 1) * 64], psA[:, 0:64]), sem=act)
            p.wait("tensor", act, v_oie)
            p.op("tensor", lambda e: e.matmul(
                psB[0:64, 0:128], w_t[:, 0:64], gt[:], start=True, stop=True), sem=pe)
            v_qm = pe.n
            p.wait("vector", pe, v_qm)
            v_qt = p.op("vector", lambda e, h=h, q_t=q_t: e.tensor_sub(
                qt_all[:, h * C:(h + 1) * C], q_t, psB[0:64, 0:128]), sem=dve)
            p.op("tensor", lambda e: e.matmul(
                psA[0:64, 0:64], w_t[:, 0:64], kr_t[:, 0:64], start=True, stop=True),
                sem=pe)
            v_rt = pe.n
            p.wait("tensor", dve, v_qt)
            p.op("tensor", lambda e: e.matmul(
                psB[0:64, 0:64], kr_t[:, 0:64], u_t[:, 0:64], start=True, stop=True),
                sem=pe)
            v_pm = pe.n
            p.wait("scalar", pe, v_rt)
            v_rte = p.op("scalar", lambda e, h=h: e.copy(
                rt_all[:, h * 64:(h + 1) * 64], psA[0:64, 0:64]), sem=act)
            p.wait("scalar", pe, v_pm)
            v_pme = p.op("scalar", lambda e, h=h: e.copy(
                pm_all[:, h * 64:(h + 1) * 64], psB[0:64, 0:64]), sem=act)
            p.wait("tensor", act, v_pme)

        p.wait("sync", dve, dve.n)
        p.wait("sync", act, act.n)
        p.dma("sync", qt_scr[:, i * NH * C:(i + 1) * NH * C], qt_all[:], sstash,
              gate=True)
        p.dma("sync", rt_scr[:, i * NH * 64:(i + 1) * NH * 64], rt_all[:], sstash,
              gate=False)

        if i > 0:
            p.wait("vector", sto, sto.n)
        corr_free = [None, None]
        for h in range(NH):
            zslc = zcur[:, h * 64:(h + 1) * 64]
            bank = psz[h % 2]
            cbank = psA if h % 2 == 0 else psB
            v_zb = p.op("vector", lambda e, zslc=zslc: e.tensor_copy(
                z_bf[:, 0:64], zslc), sem=dve)
            p.wait("tensor", dve, v_zb)
            if corr_free[h % 2] is not None:
                p.wait("tensor", dve, corr_free[h % 2])
            p.op("tensor", lambda e, h=h, cbank=cbank: e.matmul(
                cbank[:, 0:64], qt_all[:, h * C:(h + 1) * C],
                z_bf[:, 0:64], start=True, stop=True), sem=pe)
            v_corr = pe.n
            p.op("tensor", lambda e, h=h, zslc=zslc, bank=bank: e.matmul(
                bank[0:64, 0:64], rt_all[:, h * 64:(h + 1) * 64], zslc,
                start=True, stop=True), sem=pe)
            v_zm = pe.n
            p.wait("vector", pe, v_corr)
            v_oh = p.op("vector", lambda e, h=h, cbank=cbank: e.tensor_add(
                o_tok[:, (h % 2) * 64:(h % 2) * 64 + 64],
                cbank[:, 0:64], o_intra[:, h * 64:(h + 1) * 64]), sem=dve)
            corr_free[h % 2] = v_oh
            if h % 2 == 1:
                p.wait("tensor", dve, v_oh)
                p.op("tensor", lambda e: e.transpose(
                    psT[:, 0:128], o_tok[:], id_bf[:]), sem=pe)
                v_otr = pe.n
                p.wait("vector", pe, v_otr)
                v_stash = p.op("vector", lambda e, h=h: e.tensor_copy(
                    o_stash[:, (h // 2) * 128:(h // 2 + 1) * 128],
                    psT[:, 0:128]), sem=dve)
                p.wait("tensor", dve, v_stash)
            p.wait("vector", pe, v_zm)
            p.op("vector", lambda e, zslc=zslc, bank=bank: e.scalar_tensor_tensor(
                ztmp[:, 0:64], zslc, 1.0, bank[0:64, 0:64], ALU.mult, ALU.subtract),
                sem=dve)
            v_zn = p.op("vector", lambda e, h=h, znxt=znxt: e.tensor_add(
                znxt[:, h * 64:(h + 1) * 64], ztmp[:, 0:64],
                pm_all[:, h * 64:(h + 1) * 64]), sem=dve)
            p.wait("tensor", dve, v_zn)
        v_oev = dve.n
        p.wait("sync", dve, v_oev)
        p.dma("sync", o_out[:, i * C:(i + 1) * C].rearrange("(hp d) t -> d hp t", d=128),
              o_stash[:].rearrange("d (hp t) -> d hp t", hp=8), sto)
        p.wait("tensor", dve, v_oev)
        comp_done[i] = pe.n

    zlast = zc[NCH % 2]
    p.wait("sync", dve, dve.n)
    p.dma("sync", zfin.rearrange("(h d) e -> d h e", d=64),
          zlast[:].rearrange("d (h e) -> d h e", h=NH), sstash, gate=True)
    p.wait("gpsimd", sstash, sstash.n)
    v_cc = cc.inc(1)
    p.on("gpsimd", lambda e: e.collective_compute(
        "AllGather", ALU.bypass,
        [[0, 1], [2, 3], [4, 5], [6, 7]],
        [zfin.opt()], [zpair.opt()]).then_inc(cc.h, 1))

    dcur = p.sb("dcur", (64, NH * 64), F32)
    dnxt = p.sb("dnxt", (64, NH * 64), F32)
    d_bf = p.sb("d_bf", (64, NH * 64), BF16)
    qt_l = p.sb("qt_l", (64, NH * C), BF16)
    rt_l = p.sb("rt_l", (64, NH * 64), F32)
    o_l = p.sb("o_l", (128, 8 * 128), BF16)
    lc = p.sem("lc")
    lo = p.sem("lo")
    p.wait("sync", cc, v_cc)
    v_zp = p.dma("sync", dcur[:].rearrange("d (h e) -> d h e", h=NH),
                 zpair[0:NH * 64, :].rearrange("(h d) e -> d h e", d=64), lc, gate=True)
    p.wait("vector", lc, v_zp)
    p.wait("vector", ld, v_ld)
    v_d0 = p.op("vector", lambda e: e.tensor_scalar_mul(
        dcur[:], dcur[:], dmask[:]), sem=dve)
    dd = [dcur, dnxt]
    for i in range(NCH):
        dc_, dn_ = dd[i % 2], dd[(i + 1) % 2]
        p.wait("sync", pe, pe.n)
        vq = p.dma("sync", qt_l[:], qt_scr[:, i * NH * C:(i + 1) * NH * C], lc, gate=True)
        vr = p.dma("sync", rt_l[:], rt_scr[:, i * NH * 64:(i + 1) * NH * 64], lc,
                   gate=False)
        p.wait("sync", sto, sto.n)
        p.wait("sync", dve, dve.n)
        vo = p.dma("sync", o_l[:].rearrange("d (hp t) -> d hp t", hp=8),
                   o_out[:, i * C:(i + 1) * C].rearrange("(hp d) t -> d hp t", d=128),
                   lo, gate=True)
        p.wait("vector", dve, dve.n)
        v_dbf = p.op("vector", lambda e, dc_=dc_: e.tensor_copy(d_bf[:], dc_[:]), sem=dve)
        p.wait("tensor", lc, vr)
        p.wait("tensor", dve, v_dbf)
        p.wait("vector", lo, vo)
        for hp in range(8):
            for sub in range(2):
                h = 2 * hp + sub
                bank = psz[sub]
                p.op("tensor", lambda e, h=h, sub=sub: e.matmul(
                    psA[:, 0:64] if sub == 0 else psB[:, 0:64],
                    qt_l[:, h * C:(h + 1) * C],
                    d_bf[:, h * 64:(h + 1) * 64], start=True, stop=True), sem=pe)
                v_do = pe.n
                p.op("tensor", lambda e, h=h, bank=bank, dc_=dc_: e.matmul(
                    bank[0:64, 0:64], rt_l[:, h * 64:(h + 1) * 64],
                    dc_[:, h * 64:(h + 1) * 64], start=True, stop=True), sem=pe)
                v_dz = pe.n
                p.wait("vector", pe, v_do)
                p.op("vector", lambda e, sub=sub: e.tensor_copy(
                    o_tok[:, sub * 64:sub * 64 + 64],
                    psA[:, 0:64] if sub == 0 else psB[:, 0:64]), sem=dve)
                p.wait("vector", pe, v_dz)
                v_dn = p.op("vector", lambda e, h=h, bank=bank, dc_=dc_, dn_=dn_:
                            e.scalar_tensor_tensor(
                    dn_[:, h * 64:(h + 1) * 64], dc_[:, h * 64:(h + 1) * 64], 1.0,
                    bank[0:64, 0:64], ALU.mult, ALU.subtract), sem=dve)
                p.wait("tensor", dve, v_dn)
            p.op("tensor", lambda e: e.transpose(psT[:, 0:128], o_tok[:], id_bf[:]),
                 sem=pe)
            v_tr = pe.n
            p.wait("vector", pe, v_tr)
            v_oadd = p.op("vector", lambda e, hp=hp: e.tensor_add(
                o_l[:, hp * 128:(hp + 1) * 128], o_l[:, hp * 128:(hp + 1) * 128],
                psT[:, 0:128]), sem=dve)
            p.wait("tensor", dve, v_oadd)
        v_corrd = dve.n
        p.wait("sync", dve, v_corrd)
        p.dma("sync", o_out[:, i * C:(i + 1) * C].rearrange("(hp d) t -> d hp t", d=128),
              o_l[:].rearrange("d (hp t) -> d hp t", hp=8), sto, gate=True)
    p.wait("sync", sto, sto.n)
    p.finish(sto, sto.n)
    p.end_phase()
    p.pop()
    return p



def build_fused():
    p = Prog("fused", fused=True, final_outputs=("out_fm",))
    build_ph0(p=p)
    build_qkv(p=p)
    build_scan2(p=p)
    build_attnout(p=p)
    build_x1init(p=p)
    build_moe("moe0a", 0, 4, 0, p=p,
              rename={"acc_in": "x1acc", "acc_out": "acc_ab", "wm_fm": "w0_fm"})
    build_moe("moe0b", 4, 8, 0, p=p,
              rename={"acc_in": "acc_ab", "acc_out": "x1_fm", "wm_fm": "w0_fm"})
    build_ph1(p=p)

    p.push(None)
    p.begin_phase("cc2_")
    counts = p.registry["counts"]
    cnt_ag = p.dint("cnt_ag", (64, 1), F32)
    xbnd_src = p.dint("xbnd_src", (D, 1028), F32)
    xbnd_pair = p.dint("xbnd_pair", (2 * D, 1028), F32)
    x1 = p.registry["x1_fm"]
    bs = p.sem("bs")
    cc = p.sem("cc")
    p.dma("sync", xbnd_src[:, 0:1024], x1[:, 3072:4096], bs, gate=True)
    p.dma("sync", xbnd_src[:, 1024:1028], x1[:, 0:4], bs, gate=False)
    p.wait("gpsimd", bs, bs.n)
    v1 = cc.inc(1)
    p.on("gpsimd", lambda e: e.collective_compute(
        "AllGather", ALU.bypass, [[0, 1], [2, 3], [4, 5], [6, 7]],
        [xbnd_src.opt()], [xbnd_pair.opt()]).then_inc(cc.h, 1))
    v2 = cc.inc(1)
    p.on("gpsimd", lambda e: e.collective_compute(
        "AllGather", ALU.bypass, [[0, 1, 2, 3, 4, 5, 6, 7]],
        [counts.opt()], [cnt_ag.opt()]).then_inc(cc.h, 1))
    p.end_phase()
    p.pop()
    p.registry["counts_all"] = cnt_ag.rearrange("(c e) one -> c (e one)", e=8)
    p.registry["xprev_fm"] = xbnd_pair[0:D, :]

    build_wm1(p=p, rename={"ohE_in": "ohE_out", "gE_in": "gE_out"})
    build_attn1(p=p)
    build_moe("moe1a", 0, 4, 1, p=p,
              rename={"xhat_bf": "xn1_bf", "acc_in": "x2a_fm", "acc_out": "acc2_ab",
                      "wm_fm": "wm_fm", "W1T": "W1L1", "W2T": "W2L1",
                      "b1e": "b1L1", "b2e": "b2L1"})
    build_moe("moe1b", 4, 8, 1, p=p,
              rename={"xhat_bf": "xn1_bf", "acc_in": "acc2_ab", "acc_out": "out_fm",
                      "wm_fm": "wm_fm", "W1T": "W1L1", "W2T": "W2L1",
                      "b1e": "b1L1", "b2e": "b2L1"})
    return p


def kernel(**inputs):
    def f(k):
        return np.ascontiguousarray(np.asarray(inputs[k], np.float32))

    x = f("L0_attn_x")
    common = {
        "cenT": np.ascontiguousarray(f("L0_ffn_centroids").T),
        "ln1c": _col(inputs["L0_attn_ln1_scale"]),
        "wq": f("L0_attn_wq"), "wk": f("L0_attn_wk"), "wv": f("L0_attn_wv"),
        "wbeta": f("L0_attn_wbeta"), "bq_c": _col(inputs["L0_attn_bq"]),
        "bk_c": _col(inputs["L0_attn_bk"]), "bv_c": _col(inputs["L0_attn_bv"]),
        "bbeta_c": _col(inputs["L0_attn_bbeta"]),
        "wo": f("L0_attn_wo"), "w1": f("L0_attn_w1"), "w2": f("L0_attn_w2"),
        "bo_c": _col(inputs["L0_attn_bo"]), "ln2_c": _col(inputs["L0_attn_ln2_scale"]),
        "b1_c": _col(inputs["L0_attn_b1"]), "b2_c": _col(inputs["L0_attn_b2"]),
        "W1T": np.ascontiguousarray(f("L0_ffn_W1").transpose(0, 2, 1)),
        "W2T": np.ascontiguousarray(f("L0_ffn_W2").transpose(0, 2, 1)),
        "b1e": f("L0_ffn_b1"), "b2e": f("L0_ffn_b2"),
        "gammae": f("L0_ffn_gamma"), "betae": f("L0_ffn_beta"),
        "Wg": f("L1_ffn_Wg"),
        "Wq": f("L1_attn_Wq"), "Wk": f("L1_attn_Wk"), "Wv": f("L1_attn_Wv"),
        "Wo": f("L1_attn_Wo"),
        "hd3T": np.ascontiguousarray(f("L1_attn_head_dirs").T),
        "hscc": np.ascontiguousarray(f("L1_attn_head_scales").reshape(1, 8)),
        "W1L1": f("L1_ffn_W1"), "W2L1": f("L1_ffn_W2"),
        "b1L1": f("L1_ffn_b1"), "b2L1": f("L1_ffn_b2"),
    }
    in_maps = []
    for c in range(NCORE):
        b, h = c // 2, c % 2
        sel = np.zeros((8, 1), np.float32)
        sel[c, 0] = 1.0
        m = {
            "x_fm": np.ascontiguousarray(x[b, h * NT:(h + 1) * NT, :].T),
            "oddmask": np.full((64, 1), float(h), np.float32),
            "selcol": sel,
            **_attn1_host_consts(c),
            **common,
        }
        in_maps.append(m)
    res = run_prog(build_fused, in_maps)
    out = np.zeros((4, 8192, D), np.float32)
    for c in range(NCORE):
        out[c // 2, (c % 2) * NT:(c % 2 + 1) * NT, :] = res[c]["out_fm"].T
    return out


# version 05 (final)
# speedup vs baseline: 1.2578x; 1.2578x over previous
import math
from contextlib import ExitStack

import numpy as np
import ml_dtypes

import concourse.bass as bass
import concourse.mybir as mybir
from concourse.bass_utils import run_bass_kernel_spmd

F32 = mybir.dt.float32
BF16 = mybir.dt.bfloat16
I32 = mybir.dt.int32
U32 = mybir.dt.uint32
AF = mybir.ActivationFunctionType
ALU = mybir.AluOpType
AX = mybir.AxisListType
BF16NP = ml_dtypes.bfloat16

D = 1024
NT = 4096
NCHUNK = 8
NCORE = 8

ENGINES = ("sync", "gpsimd", "tensor", "vector", "scalar")


class Sem:
    def __init__(self, prog, name):
        self.h = prog.es.enter_context(prog.nc.semaphore(name))
        self.n = 0

    def inc(self, k=1):
        self.n += k
        return self.n


AUTO_SYNC_ENGINES = ("vector", "scalar", "gpsimd")


class Prog:

    def __init__(self, name, fused=False, final_outputs=()):
        self.name = name
        self.fused = fused
        self.final_outputs = set(final_outputs)
        self.nc = bass.Bass("TRN2", target_bir_lowering=False)
        self.es = ExitStack()
        self.phase_es = None
        self.phase_tag = ""
        self.streams = {e: [] for e in ENGINES}
        self.in_names = []
        self.out_names = []
        self._final_sem = None
        self._final_val = 0
        self.esem = {}
        self.registry = {}
        self.renames = [{}]
        self.all_sems = []

    def push(self, rename):
        self.renames.append(rename or {})

    def pop(self):
        self.renames.pop()

    def begin_phase(self, tag):
        self.phase_tag = tag
        if self.fused:
            self.phase_es = ExitStack()

    def end_phase(self):
        if self.fused:
            self.barrier()
            self.phase_es.close()
            self.phase_es = None
        self.phase_tag = ""

    def barrier(self):
        for eng in ENGINES:
            for s in self.all_sems:
                self.wait(eng, s, s.n)

    def din(self, name, shape, dtype):
        name = self.renames[-1].get(name, name)
        if name in self.registry:
            return self.registry[name]
        self.in_names.append(name)
        ap = self.nc.dram_tensor(name, list(shape), dtype, kind="ExternalInput").ap()
        self.registry[name] = ap
        return ap

    def dout(self, name, shape, dtype):
        name = self.renames[-1].get(name, name)
        if self.fused and name not in self.final_outputs:
            ap = self.nc.dram_tensor(name, list(shape), dtype, kind="Internal").ap()
        else:
            self.out_names.append(name)
            ap = self.nc.dram_tensor(name, list(shape), dtype, kind="ExternalOutput").ap()
        self.registry[name] = ap
        return ap

    def dint(self, name, shape, dtype):
        name = self.renames[-1].get(name, name)
        if name in self.registry:
            return self.registry[name]
        ap = self.nc.dram_tensor(name, list(shape), dtype, kind="Internal").ap()
        self.registry[name] = ap
        return ap

    def const(self, name, arr):
        return self.nc.inline_tensor(np.ascontiguousarray(arr),
                                     name=self.phase_tag + name).ap()

    def sb(self, name, shape, dtype):
        es = self.phase_es if self.phase_es is not None else self.es
        return es.enter_context(
            self.nc.sbuf_tensor(self.phase_tag + name, list(shape), dtype))

    def ps(self, name, shape, dtype=F32):
        es = self.phase_es if self.phase_es is not None else self.es
        return es.enter_context(
            self.nc.psum_tensor(self.phase_tag + name, list(shape), dtype))

    def sem(self, name):
        if name in self.esem:
            return self.esem[name]
        s = Sem(self, name)
        self.esem[name] = s
        self.all_sems.append(s)
        return s

    def on(self, eng, fn):
        self.streams[eng].append(fn)

    def op(self, eng, fn, sem=None, inc=1):
        if eng in AUTO_SYNC_ENGINES:
            s = self.sem("auto_" + eng)
            assert sem is None or sem is s, f"{eng} ops must use the stream sem"
            prev = s.n
            if prev > 0:
                self.on(eng, lambda e, s=s, prev=prev: e.wait_ge(s.h, prev))
            v = s.inc(1)
            self.on(eng, lambda e, fn=fn, s=s: fn(e).then_inc(s.h, 1))
            return v
        if sem is None:
            self.on(eng, fn)
            return None
        v = sem.inc(inc)
        self.on(eng, lambda e, fn=fn, sem=sem, inc=inc: fn(e).then_inc(sem.h, inc))
        return v

    def stream_sem(self, eng):
        return self.sem("auto_" + eng)

    def wait(self, eng, sem, val=None):
        v = sem.n if val is None else val
        if v > 0:
            self.on(eng, lambda e, sem=sem, v=v: e.wait_ge(sem.h, v))
        return v

    def dma(self, eng, out, in_, sem, slow=False, gate=True):
        if gate and sem.n > 0:
            self.wait(eng, sem, sem.n)
        v = sem.inc(16)
        kw = {"allow_slow_non_contiguous": True} if slow else {}
        self.on(eng, lambda e, out=out, in_=in_, sem=sem, kw=kw:
                e.dma_start(out=out, in_=in_, **kw).then_inc(sem.h, 16))
        return v

    def finish(self, sem, val):
        self._final_sem, self._final_val = sem, val

    def build(self):
        with self.nc.Block() as block:
            for eng in ENGINES:
                fns = self.streams[eng]
                if not fns and eng != "sync":
                    continue

                def body(e, fns=fns, eng=eng):
                    for fn in fns:
                        fn(e)
                    if eng == "sync" and self._final_sem is not None:
                        e.wait_ge(self._final_sem.h, self._final_val)

                getattr(block, eng)(body)
        self.es.close()
        return self.nc


_PROG_CACHE = {}


def run_prog(builder, in_maps, n_launch_counter=None):
    key = builder.__name__
    if key not in _PROG_CACHE:
        prog = builder()
        prog.build()
        _PROG_CACHE[key] = prog
    prog = _PROG_CACHE[key]
    res = run_bass_kernel_spmd(prog.nc, in_maps, core_ids=list(range(NCORE)))
    return res.results



def build_ph0(p=None, rename=None):
    standalone = p is None
    if standalone:
        p = Prog('ph0')
    p.push(rename)
    p.begin_phase('p0_')
    T = 512
    NTILE = NT // T
    x_fm = p.din("x_fm", (D, NT), F32)
    cenT = p.din("cenT", (D, 8), F32)
    ln1c = p.din("ln1c", (D, 1), F32)
    o_xn = p.dout("xn_fm", (D, NT), F32)
    o_xh1 = p.dout("xh1_bf", (D, NT), BF16)
    o_xhat = p.dout("xhat_bf", (D, NT), BF16)
    o_w0 = p.dout("w0_fm", (8, NT), F32)
    o_r1 = p.dout("r1_row", (1, NT), F32)

    idb = p.const("c_idb", np.eye(128, dtype=BF16NP))
    idf = p.const("c_idf", np.eye(128, dtype=np.float32))

    xs = [p.sb(f"x{i}", (128, NCHUNK * T), F32) for i in range(2)]
    xn = p.sb("xn", (128, NCHUNK * T), F32)
    sq = p.sb("sq", (128, NCHUNK * T), F32)
    xh_tmp = p.sb("xh_tmp", (128, T), F32)
    ones_f = p.sb("ones_f", (128, 1), F32)
    cvals = p.sb("cvals", (128, 2), F32)
    onesr_f = p.sb("onesr_f", (1, 128), F32)
    cen_sb = p.sb("cen_sb", (128, NCHUNK * 8), F32)
    ln1_sb = p.sb("ln1_sb", (128, NCHUNK), F32)
    id_bf = p.sb("id_bf", (128, 128), BF16)
    id_f = p.sb("id_f", (128, 128), F32)
    stat = p.sb("stat", (1, T * 8), F32)
    sc_sb = p.sb("sc_sb", (8, T), F32)
    sc_tok = p.sb("sc_tok", (128, 8 * 4), F32)
    mx8 = p.sb("mx8", (128, 8 * 4), F32)
    oh_tok = p.sb("oh_tok", (128, 8 * 4), F32)
    alpha = p.sb("alpha", (128, 4), F32)
    w0_sb = p.sb("w0_sb", (8, NT), F32)
    r1row = p.sb("r1row", (1, NT), F32)
    out_bf = p.sb("out_bf", (128, NCHUNK * T), BF16)
    out2_bf = p.sb("out2_bf", (128, NCHUNK * T), BF16)

    ps_mu = p.ps("ps_mu", (1, T))
    ps_sq = p.ps("ps_sq", (1, T))
    ps_b1 = p.ps("ps_b1", (128, T))
    ps_b2 = p.ps("ps_b2", (128, T))
    ps_sc = p.ps("ps_sc", (8, T))
    ps_tr = p.ps("ps_tr", (128, 128))
    ps_mu2 = p.ps("ps_mu2", (1, T))

    ld = p.sem("ld")
    lx = [p.sem("lx0"), p.sem("lx1")]
    pe = p.sem("pe")
    dve = p.stream_sem("vector")
    act = p.stream_sem("scalar")
    st = p.sem("st")
    sxn = p.sem("sxn")
    sxh = p.sem("sxh")
    sxt = p.sem("sxt")
    gp = p.stream_sem("gpsimd")

    p.dma("sync", cen_sb[:, 0:8 * NCHUNK].rearrange("p (c e) -> p c e", e=8),
          cenT.rearrange("(c p) e -> p c e", p=128), ld)
    p.dma("sync", ln1_sb[:], ln1c.rearrange("(c p) one -> p (c one)", p=128), ld, slow=True)
    p.dma("sync", id_bf[:], idb, ld)
    p.dma("sync", id_f[:], idf, ld)
    p.op("gpsimd", lambda e: e.memset(ones_f[:], 1.0))
    p.op("gpsimd", lambda e: e.memset(cvals[:, 0:1], 1e-5))
    p.op("gpsimd", lambda e: e.memset(cvals[:, 1:2], 1e-6))
    p.op("gpsimd", lambda e: e.memset(onesr_f[:], 1.0), sem=gp)
    v_pre = ld.n
    v_gp = gp.n

    eps_ln = 1e-5
    eps_rms = 1e-6
    p.wait("vector", ld, v_pre)
    p.wait("vector", gp, v_gp)

    compute_done = [0] * NTILE
    sq_guard = [0] * NTILE
    ps_tr_free = [0]

    for i in range(NTILE):
        xb = xs[i % 2]
        if i >= 2:
            p.wait("sync", dve, compute_done[i - 2])
        v_ld = p.dma("sync", xb[:].rearrange("p (c t) -> p c t", c=NCHUNK),
                     x_fm[:, i * T:(i + 1) * T].rearrange("(c p) t -> p c t", p=128),
                     lx[i % 2])

        p.wait("scalar", lx[i % 2], v_ld)
        if i > 0:
            p.wait("scalar", pe, sq_guard[i - 1])
        for c in range(NCHUNK):
            p.op("scalar", lambda e, c=c, xb=xb: e.activation(
                sq[:, c * T:(c + 1) * T], xb[:, c * T:(c + 1) * T], AF.Square), sem=act)
            v_sq = act.n
            p.wait("tensor", lx[i % 2], v_ld)
            p.wait("tensor", gp, v_gp)
            p.op("tensor", lambda e, c=c, xb=xb: e.matmul(
                ps_mu[:], ones_f[:], xb[:, c * T:(c + 1) * T],
                start=(c == 0), stop=(c == NCHUNK - 1)))
            p.wait("tensor", act, v_sq)
            p.op("tensor", lambda e, c=c: e.matmul(
                ps_sq[:], ones_f[:], sq[:, c * T:(c + 1) * T],
                start=(c == 0), stop=(c == NCHUNK - 1)),
                sem=pe, inc=1)
        v_stats = pe.n

        MU, RSTD, MR, R1, RSTD2, M2R2, TMP, TMP2 = range(8)

        def srow(k):
            return stat[:, k * T:(k + 1) * T]

        p.wait("vector", pe, v_stats)
        p.op("vector", lambda e: e.tensor_scalar_mul(srow(MU), ps_mu[:], 1.0 / D))
        p.op("vector", lambda e: e.tensor_scalar_mul(srow(TMP), ps_sq[:], 1.0 / D))
        p.op("vector", lambda e: e.tensor_mul(srow(TMP2), srow(MU), srow(MU)))
        p.op("vector", lambda e: e.tensor_sub(srow(TMP), srow(TMP), srow(TMP2)), sem=dve)
        v_var = dve.n
        p.wait("scalar", dve, v_var)
        p.wait("scalar", gp, v_gp)
        p.op("scalar", lambda e: e.activation(srow(RSTD), srow(TMP), AF.Ln, bias=cvals[0:1, 0:1]), sem=act)
        p.op("scalar", lambda e: e.activation(srow(RSTD), srow(RSTD), AF.Exp, scale=-0.5), sem=act)
        v_rstd = act.n
        p.wait("vector", act, v_rstd)
        p.op("vector", lambda e: e.tensor_mul(srow(MR), srow(MU), srow(RSTD)), sem=dve)
        v_mr = dve.n

        p.wait("tensor", dve, v_mr)
        p.op("tensor", lambda e: e.matmul(ps_b1[:], onesr_f[:], srow(RSTD), start=True, stop=True))
        p.op("tensor", lambda e: e.matmul(ps_b2[:], onesr_f[:], srow(MR), start=True, stop=True),
             sem=pe)
        v_bc = pe.n

        p.wait("vector", pe, v_bc)
        if i > 0:
            p.wait("vector", sxn, sxn.n)
        for c in range(NCHUNK):
            p.op("vector", lambda e, c=c, xb=xb: e.tensor_mul(
                xn[:, c * T:(c + 1) * T], xb[:, c * T:(c + 1) * T], ps_b1[:]))
            p.op("vector", lambda e, c=c: e.tensor_sub(
                xn[:, c * T:(c + 1) * T], xn[:, c * T:(c + 1) * T], ps_b2[:]), sem=dve)
        v_xn = dve.n

        p.wait("sync", dve, v_xn)
        p.dma("sync", o_xn[:, i * T:(i + 1) * T].rearrange("(c p) t -> p c t", p=128),
              xn[:].rearrange("p (c t) -> p c t", c=NCHUNK), sxn)

        p.wait("scalar", dve, v_xn)
        p.wait("tensor", dve, v_xn)
        for c in range(NCHUNK):
            p.op("scalar", lambda e, c=c: e.activation(
                sq[:, c * T:(c + 1) * T], xn[:, c * T:(c + 1) * T], AF.Square), sem=act)
            v_sq2 = act.n
            p.op("tensor", lambda e, c=c: e.matmul(
                ps_mu2[:], ones_f[:], xn[:, c * T:(c + 1) * T],
                start=(c == 0), stop=(c == NCHUNK - 1)))
            p.wait("tensor", act, v_sq2)
            p.op("tensor", lambda e, c=c: e.matmul(
                ps_sq[:], ones_f[:], sq[:, c * T:(c + 1) * T],
                start=(c == 0), stop=(c == NCHUNK - 1)), sem=pe)
        v_stats2 = pe.n
        sq_guard[i] = v_stats2

        p.wait("scalar", pe, v_stats2)
        p.op("scalar", lambda e: e.activation(srow(R1), ps_sq[:], AF.Ln,
                                              bias=cvals[0:1, 1:2], scale=1.0 / D), sem=act)
        p.op("scalar", lambda e: e.activation(srow(R1), srow(R1), AF.Exp, scale=-0.5), sem=act)
        v_r1 = act.n
        p.wait("vector", pe, v_stats2)
        p.op("vector", lambda e: e.tensor_scalar_mul(srow(MU), ps_mu2[:], 1.0 / D))
        p.op("vector", lambda e: e.tensor_scalar_mul(srow(TMP), ps_sq[:], 1.0 / D))
        p.op("vector", lambda e: e.tensor_mul(srow(TMP2), srow(MU), srow(MU)))
        p.op("vector", lambda e: e.tensor_sub(srow(TMP), srow(TMP), srow(TMP2)), sem=dve)
        v_var2 = dve.n
        p.wait("scalar", dve, v_var2)
        p.op("scalar", lambda e: e.activation(srow(RSTD2), srow(TMP), AF.Ln, bias=cvals[0:1, 0:1]), sem=act)
        p.op("scalar", lambda e: e.activation(srow(RSTD2), srow(RSTD2), AF.Exp, scale=-0.5),
             sem=act)
        v_rstd2 = act.n
        p.wait("vector", act, v_rstd2)
        p.op("vector", lambda e: e.tensor_mul(srow(M2R2), srow(MU), srow(RSTD2)), sem=dve)
        v_m2r2 = dve.n
        p.wait("vector", act, v_r1)
        p.op("vector", lambda e, i=i: e.tensor_copy(r1row[:, i * T:(i + 1) * T], srow(R1)),
             sem=dve)

        p.wait("tensor", act, v_r1)
        p.op("tensor", lambda e: e.matmul(ps_b1[:], onesr_f[:], srow(R1), start=True, stop=True),
             sem=pe)
        v_bcr1 = pe.n
        p.wait("vector", pe, v_bcr1)
        if i > 0:
            p.wait("vector", sxh, sxh.n)
        for c in range(NCHUNK):
            p.op("vector", lambda e, c=c: e.scalar_tensor_tensor(
                out_bf[:, c * T:(c + 1) * T], xn[:, c * T:(c + 1) * T],
                ln1_sb[:, c:c + 1], ps_b1[:], ALU.mult, ALU.mult), sem=dve)
        v_xh1 = dve.n
        p.wait("sync", dve, v_xh1)
        p.dma("sync", o_xh1[:, i * T:(i + 1) * T].rearrange("(c p) t -> p c t", p=128),
              out_bf[:].rearrange("p (c t) -> p c t", c=NCHUNK), sxh)

        p.wait("tensor", dve, max(v_m2r2, v_xh1))
        p.op("tensor", lambda e: e.matmul(ps_b1[:], onesr_f[:], srow(RSTD2), start=True, stop=True))
        p.op("tensor", lambda e: e.matmul(ps_b2[:], onesr_f[:], srow(M2R2), start=True, stop=True),
             sem=pe)
        v_bc2 = pe.n
        p.wait("vector", pe, v_bc2)
        if i > 0:
            p.wait("vector", sxt, sxt.n)
        for c in range(NCHUNK):
            p.op("vector", lambda e, c=c: e.tensor_mul(
                xh_tmp[:], xn[:, c * T:(c + 1) * T], ps_b1[:]))
            p.op("vector", lambda e, c=c: e.tensor_sub(
                out2_bf[:, c * T:(c + 1) * T], xh_tmp[:], ps_b2[:]), sem=dve)
        v_xhat = dve.n
        p.wait("sync", dve, v_xhat)
        p.dma("sync", o_xhat[:, i * T:(i + 1) * T].rearrange("(c p) t -> p c t", p=128),
              out2_bf[:].rearrange("p (c t) -> p c t", c=NCHUNK), sxt)

        p.wait("tensor", ld, v_pre)
        for c in range(NCHUNK):
            p.op("tensor", lambda e, c=c: e.matmul(
                ps_sc[:], cen_sb[:, c * 8:(c + 1) * 8], xn[:, c * T:(c + 1) * T],
                start=(c == 0), stop=(c == NCHUNK - 1)), sem=pe)
        v_sc = pe.n
        p.wait("vector", pe, v_sc)
        p.op("vector", lambda e: e.tensor_copy(sc_sb[:], ps_sc[:]), sem=dve)
        v_scsb = dve.n

        for s in range(4):
            p.wait("tensor", dve, max(v_scsb, ps_tr_free[0]))
            p.op("tensor", lambda e, s=s: e.transpose(
                ps_tr[:, 0:8], sc_sb[:, s * 128:(s + 1) * 128], id_f[0:8, 0:8]), sem=pe)
            v_tr = pe.n
            p.wait("vector", pe, v_tr)
            p.op("vector", lambda e, s=s: e.tensor_copy(sc_tok[:, s * 8:(s + 1) * 8],
                                                        ps_tr[:, 0:8]), sem=dve)
            v_tok = dve.n
            p.op("vector", lambda e, s=s: e.max(
                mx8[:, s * 8:(s + 1) * 8], sc_tok[:, s * 8:(s + 1) * 8]), sem=dve)
            v_mx = dve.n
            p.wait("scalar", dve, v_mx)
            p.op("scalar", lambda e, s=s: e.activation(
                alpha[:, s:s + 1], mx8[:, s * 8:s * 8 + 1], AF.Sigmoid), sem=act)
            v_al = act.n
            p.wait("vector", dve, v_mx)
            p.op("vector", lambda e, s=s: e.tensor_single_scalar(
                oh_tok[:, s * 8:(s + 1) * 8], sc_tok[:, s * 8:(s + 1) * 8],
                mx8[:, s * 8:s * 8 + 1], ALU.is_equal), sem=dve)
            v_oh = dve.n
            p.wait("vector", act, v_al)
            p.op("vector", lambda e, s=s: e.tensor_scalar_mul(
                oh_tok[:, s * 8:(s + 1) * 8], oh_tok[:, s * 8:(s + 1) * 8],
                alpha[:, s:s + 1]), sem=dve)
            v_w = dve.n
            p.wait("tensor", dve, v_w)
            p.op("tensor", lambda e, s=s: e.transpose(
                ps_tr[0:8, :], oh_tok[:, s * 8:(s + 1) * 8], id_f[:]),
                sem=pe)
            v_trb = pe.n
            p.wait("vector", pe, v_trb)
            v_w0c = p.op("vector", lambda e, s=s, i=i: e.tensor_copy(
                w0_sb[0:8, i * T + s * 128: i * T + (s + 1) * 128], ps_tr[0:8, :]), sem=dve)
            ps_tr_free[0] = v_w0c

        compute_done[i] = dve.n

    p.wait("sync", dve, compute_done[NTILE - 1])
    p.dma("sync", o_w0, w0_sb[:], st)
    p.dma("sync", o_r1, r1row[:], st)
    p.wait("sync", sxn, sxn.n)
    p.wait("sync", sxh, sxh.n)
    p.wait("sync", sxt, sxt.n)
    p.finish(st, st.n)
    p.end_phase()
    p.pop()
    return p



def build_qkv(p=None, rename=None):
    standalone = p is None
    if standalone:
        p = Prog('qkv')
    p.push(rename)
    p.begin_phase('qk_')
    T = 512
    NTILE = NT // T
    xh1 = p.din("xh1_bf", (D, NT), BF16)
    wq = p.din("wq", (D, D), F32)
    wk = p.din("wk", (D, D), F32)
    wv = p.din("wv", (D, D), F32)
    wbeta = p.din("wbeta", (D, 16), F32)
    bq_c = p.din("bq_c", (D, 1), F32)
    bk_c = p.din("bk_c", (D, 1), F32)
    bv_c = p.din("bv_c", (D, 1), F32)
    bbeta_c = p.din("bbeta_c", (16, 1), F32)
    o_q = p.dout("q_f", (D, NT), BF16)
    o_k = p.dout("kn_f", (D, NT), BF16)
    o_v = p.dout("v_f", (D, NT), BF16)
    o_bt = p.dout("beta_tok", (NT, 16), F32)

    e2 = np.zeros((128, 2), np.float32)
    e2[0:64, 0] = 1.0
    e2[64:128, 1] = 1.0
    c_e2 = p.const("c_e2", e2)
    c_e2t = p.const("c_e2t", np.ascontiguousarray(e2.T))
    c_idf = p.const("c_idf2", np.eye(128, dtype=np.float32))

    w_sb = {n: p.sb(f"w_{n}", (128, 8 * D), BF16) for n in ("q", "k", "v")}
    wb_sb = p.sb("wb_sb", (128, 8 * 16), BF16)
    bias_sb = p.sb("bias_sb", (128, 3 * 8), F32)
    bbeta_sb = p.sb("bbeta_sb", (16, 1), F32)
    e2_sb = p.sb("e2_sb", (128, 2), F32)
    e2t_sb = p.sb("e2t_sb", (2, 128), F32)
    idf_sb = p.sb("idf_sb", (128, 128), F32)
    xs = [p.sb(f"xh{i}", (128, NCHUNK * T), BF16) for i in range(2)]
    outq = p.sb("outq", (128, NCHUNK * T), BF16)
    outk = p.sb("outk", (128, NCHUNK * T), BF16)
    outv = p.sb("outv", (128, NCHUNK * T), BF16)
    kf32 = p.sb("kf32", (128, T), F32)
    ksq = p.sb("ksq", (128, T), F32)
    rn = p.sb("rn", (2, T), F32)
    beta_sb = p.sb("beta_sb", (16, T), F32)
    bt_tok = p.sb("bt_tok", (128, 16 * 4), F32)

    ps_a = p.ps("ps_a", (128, T))
    ps_b = p.ps("ps_b", (128, T))
    ps_n = p.ps("ps_n", (2, T))
    ps_bc = p.ps("ps_bc", (128, T))
    ps_beta = p.ps("ps_beta", (16, T))
    ps_tr = p.ps("ps_tr", (128, 16))

    ld = p.sem("ld")
    lw = p.sem("lw")
    lx = [p.sem("lx0"), p.sem("lx1")]
    pe = p.sem("pe")
    dve = p.stream_sem("vector")
    act = p.stream_sem("scalar")
    stq = p.sem("stq")
    stk = p.sem("stk")
    stv = p.sem("stv")
    stb = p.sem("stb")

    for n, w in (("q", wq), ("k", wk), ("v", wv)):
        for dc in range(NCHUNK):
            p.dma("gpsimd", w_sb[n][:, dc * D:(dc + 1) * D],
                  w[dc * 128:(dc + 1) * 128, :], lw, gate=False)
    for dc in range(NCHUNK):
        p.dma("gpsimd", wb_sb[:, dc * 16:(dc + 1) * 16],
              wbeta[dc * 128:(dc + 1) * 128, :], lw, gate=False)
    v_lw = lw.n
    p.dma("sync", bias_sb[:, 0:8], bq_c.rearrange("(c p) one -> p (c one)", p=128),
          ld, slow=True)
    p.dma("sync", bias_sb[:, 8:16], bk_c.rearrange("(c p) one -> p (c one)", p=128),
          ld, slow=True)
    p.dma("sync", bias_sb[:, 16:24], bv_c.rearrange("(c p) one -> p (c one)", p=128),
          ld, slow=True)
    p.dma("sync", bbeta_sb[:], bbeta_c, ld, slow=True)
    p.dma("sync", e2_sb[:], c_e2, ld)
    p.dma("sync", e2t_sb[:], c_e2t, ld)
    p.dma("sync", idf_sb[:], c_idf, ld)
    v_ld = ld.n

    comp_done = [0] * NTILE
    stq_done = stk_done = stv_done = 0
    pp = [ps_a, ps_b]
    psum_free = {0: None, 1: None}

    for i in range(NTILE):
        xb = xs[i % 2]
        if i >= 2:
            p.wait("sync", pe, comp_done[i - 2])
        v_lx = p.dma("sync", xb[:].rearrange("p (c t) -> p c t", c=NCHUNK),
                     xh1[:, i * T:(i + 1) * T].rearrange("(c p) t -> p c t", p=128),
                     lx[i % 2])
        p.wait("tensor", lx[i % 2], v_lx)
        if i == 0:
            p.wait("tensor", lw, v_lw)
            p.wait("scalar", ld, v_ld)
            p.wait("vector", ld, v_ld)

        nb = 0
        for n, outbuf, bias_i, stsem in (("q", outq, 0, stq), ("v", outv, 2, stv),
                                         ("k", outk, 1, stk)):
            if i > 0:
                p.wait("vector", stsem, stsem.n)
            for m in range(NCHUNK):
                bank = pp[nb]
                if psum_free[nb] is not None:
                    p.wait("tensor", {"dve": dve, "act": act}[psum_free[nb][0]],
                           psum_free[nb][1])
                for dc in range(NCHUNK):
                    p.op("tensor", lambda e, n=n, m=m, dc=dc, bank=bank, xb=xb: e.matmul(
                        bank[:], w_sb[n][:, (dc * 8 + m) * 128:(dc * 8 + m + 1) * 128],
                        xb[:, dc * T:(dc + 1) * T],
                        start=(dc == 0), stop=(dc == NCHUNK - 1)), sem=pe)
                v_mm = pe.n
                if n != "k":
                    p.wait("vector", pe, v_mm)
                    v_ev = p.op("vector", lambda e, m=m, bank=bank, outbuf=outbuf,
                                bias_i=bias_i: e.tensor_scalar_add(
                                    outbuf[:, m * T:(m + 1) * T], bank[:],
                                    bias_sb[:, bias_i * 8 + m:bias_i * 8 + m + 1]), sem=dve)
                    psum_free[nb] = ("dve", v_ev)
                else:
                    p.wait("vector", pe, v_mm)
                    v_kf = p.op("vector", lambda e, bank=bank, m=m: e.tensor_scalar_add(
                        kf32[:], bank[:], bias_sb[:, 8 + m:8 + m + 1]), sem=dve)
                    psum_free[nb] = ("dve", v_kf)
                    p.wait("scalar", dve, v_kf)
                    v_sq = p.op("scalar", lambda e: e.activation(
                        ksq[:], kf32[:], AF.Square), sem=act)
                    p.wait("tensor", act, v_sq)
                    p.op("tensor", lambda e: e.matmul(
                        ps_n[:], e2_sb[:], ksq[:], start=True, stop=True), sem=pe)
                    v_nrm = pe.n
                    p.wait("scalar", pe, v_nrm)
                    v_rt = p.op("scalar", lambda e: e.activation(
                        rn[:], ps_n[:], AF.Sqrt), sem=act)
                    p.wait("vector", act, v_rt)
                    p.op("vector", lambda e: e.tensor_scalar_add(rn[:], rn[:], 1e-6),
                         sem=dve)
                    v_rn = p.op("vector", lambda e: e.reciprocal(rn[:], rn[:]), sem=dve)
                    p.wait("tensor", dve, v_rn)
                    p.op("tensor", lambda e: e.matmul(
                        ps_bc[:], e2t_sb[:], rn[:], start=True, stop=True), sem=pe)
                    v_bc = pe.n
                    p.wait("vector", pe, v_bc)
                    p.op("vector", lambda e, m=m: e.tensor_mul(
                        outk[:, m * T:(m + 1) * T], kf32[:], ps_bc[:]), sem=dve)
                nb ^= 1

        for dc in range(NCHUNK):
            p.op("tensor", lambda e, dc=dc, xb=xb: e.matmul(
                ps_beta[:], wb_sb[:, dc * 16:(dc + 1) * 16], xb[:, dc * T:(dc + 1) * T],
                start=(dc == 0), stop=(dc == NCHUNK - 1)), sem=pe)
        v_bmm = pe.n
        p.wait("scalar", pe, v_bmm)
        if i > 0:
            p.wait("scalar", stb, stb.n)
        v_bsig = p.op("scalar", lambda e: e.activation(
            beta_sb[:], ps_beta[:], AF.Sigmoid, bias=bbeta_sb[:]), sem=act)
        p.wait("tensor", act, v_bsig)
        for s in range(4):
            if s > 0:
                p.wait("tensor", dve, dve.n)
            p.op("tensor", lambda e, s=s: e.transpose(
                ps_tr[:], beta_sb[:, s * 128:(s + 1) * 128], idf_sb[0:16, 0:16]), sem=pe)
            v_tr = pe.n
            p.wait("vector", pe, v_tr)
            p.op("vector", lambda e, s=s: e.tensor_copy(
                bt_tok[:, s * 16:(s + 1) * 16], ps_tr[:, 0:16]), sem=dve)
        v_bt = dve.n
        p.wait("tensor", dve, v_bt)

        p.wait("sync", act, act.n)
        p.wait("sync", dve, dve.n)
        p.dma("sync", o_q[:, i * T:(i + 1) * T].rearrange("(c p) t -> p c t", p=128),
              outq[:].rearrange("p (c t) -> p c t", c=NCHUNK), stq)
        p.dma("sync", o_v[:, i * T:(i + 1) * T].rearrange("(c p) t -> p c t", p=128),
              outv[:].rearrange("p (c t) -> p c t", c=NCHUNK), stv)
        p.dma("sync", o_k[:, i * T:(i + 1) * T].rearrange("(c p) t -> p c t", p=128),
              outk[:].rearrange("p (c t) -> p c t", c=NCHUNK), stk)
        p.dma("sync", o_bt[i * T:(i + 1) * T, :].rearrange("(s p) e -> p s e", p=128),
              bt_tok[:].rearrange("p (s e) -> p s e", e=16), stb)
        comp_done[i] = pe.n

    for sem in (stq, stk, stv, stb):
        p.wait("sync", sem, sem.n)
    p.finish(stb, stb.n)
    p.end_phase()
    p.pop()
    return p



def build_scan(name="scan", ci0=0, ci1=64, p=None, rename=None):
    import os
    STAGE = int(os.environ.get("SCAN_STAGE", "99"))
    standalone = p is None
    if standalone:
        p = Prog(name)
    p.push(rename)
    p.begin_phase("sc_")
    SEQ = 8192
    NH = 8
    HD = 64
    C = 128
    qs = p.din("qs", (NH * HD, SEQ), BF16)
    ks = p.din("ks", (NH * HD, SEQ), BF16)
    vs = p.din("vs", (NH * HD, SEQ), BF16)
    bt = p.din("bt", (SEQ, NH), F32)
    z_in = p.din("z_in", (NH * HD, HD), F32)
    o_out = p.dout("o_fm", (NH * HD, SEQ), BF16)
    z_out = p.dout("z_fin", (NH * HD, HD), F32)

    tri = np.tril(np.ones((128, 128), np.float32), -1).T
    trii = np.triu(np.ones((128, 128), np.float32))
    c_mu = p.const("c_maskU", tri)
    c_mi = p.const("c_maskLi", trii)
    c_idb = p.const("c_idb", np.eye(128, dtype=BF16NP))

    maskU = p.sb("maskU", (128, 128), F32)
    maskLi = p.sb("maskLi", (128, 128), F32)
    id_bf = p.sb("id_bf", (128, 128), BF16)
    kq = [p.sb(f"kq{j}", (64, NH * C), BF16) for j in range(2)]
    kk = [p.sb(f"kk{j}", (64, NH * C), BF16) for j in range(2)]
    kv = [p.sb(f"kv{j}", (64, NH * C), BF16) for j in range(2)]
    bcol = [p.sb(f"bcol{j}", (128, NH), F32) for j in range(2)]
    aT = p.sb("aT", (128, 7 * 128), BF16)
    aN = p.sb("aN", (128, 7 * 128), BF16)
    xw = [p.sb(f"xw{j}", (128, 128), BF16) for j in range(2)]
    kr_t = p.sb("kr_t", (128, 64), BF16)
    w_t = p.sb("w_t", (128, 64), BF16)
    u_t = p.sb("u_t", (128, 64), BF16)
    gt = p.sb("gt", (128, 128), BF16)
    qt_all = p.sb("qt_all", (64, NH * C), BF16)
    rt_all = p.sb("rt_all", (64, NH * HD), F32)
    pm_all = p.sb("pm_all", (64, NH * HD), F32)
    zc = [p.sb(f"z{j}", (64, NH * HD), F32) for j in range(2)]
    z_bf = p.sb("z_bf", (64, HD), BF16)
    ztmp = p.sb("ztmp", (64, HD), F32)
    o_intra = p.sb("o_intra", (128, NH * HD), F32)
    o_tok = p.sb("o_tok", (128, 128), BF16)
    o_stash = p.sb("o_stash", (128, 4 * 128), BF16)

    psA = p.ps("psA", (128, 512))
    psB = p.ps("psB", (128, 512))
    psC = p.ps("psC", (128, 512))
    psT = p.ps("psT", (128, 1024), BF16)
    psz = [p.ps(f"psz{j}", (128, 512)) for j in range(2)]

    ld = p.sem("ld")
    lin = [p.sem("lin0"), p.sem("lin1")]
    pe = p.sem("pe")
    dve = p.stream_sem("vector")
    act = p.stream_sem("scalar")
    sto = p.sem("sto")
    szf = p.sem("szf")

    p.dma("sync", maskU[:], c_mu, ld)
    p.dma("sync", maskLi[:], c_mi, ld)
    p.dma("sync", id_bf[:], c_idb, ld)
    p.dma("sync", zc[0][:].rearrange("d (h e) -> d h e", h=NH),
          z_in.rearrange("(h d) e -> d h e", d=64), ld)
    v_ld = ld.n
    p.wait("tensor", ld, v_ld)
    p.wait("vector", ld, v_ld)
    p.wait("scalar", ld, v_ld)

    NCH = ci1 - ci0
    comp_done = [0] * NCH

    for ic in range(NCH):
        i = ci0 + ic
        bb = ic % 2
        if ic >= 2:
            p.wait("sync", pe, comp_done[ic - 2])
        vq = p.dma("sync", kq[bb][:].rearrange("d (h t) -> d h t", h=NH),
                   qs[:, i * C:(i + 1) * C].rearrange("(h d) t -> d h t", d=64),
                   lin[bb], gate=False)
        vk = p.dma("sync", kk[bb][:].rearrange("d (h t) -> d h t", h=NH),
                   ks[:, i * C:(i + 1) * C].rearrange("(h d) t -> d h t", d=64),
                   lin[bb], gate=False)
        vv = p.dma("sync", kv[bb][:].rearrange("d (h t) -> d h t", h=NH),
                   vs[:, i * C:(i + 1) * C].rearrange("(h d) t -> d h t", d=64),
                   lin[bb], gate=False)
        vb = p.dma("sync", bcol[bb][:], bt[i * C:(i + 1) * C, :], lin[bb], gate=False)
        v_in = vb
        p.wait("tensor", lin[bb], v_in)
        p.wait("vector", lin[bb], v_in)

        zcur = zc[(ic + (ci0 % 2)) % 2]
        znxt = zc[(ic + 1 + (ci0 % 2)) % 2]

        for h in range(NH):
            if STAGE < 0:
                break
            q_t = kq[bb][:, h * C:(h + 1) * C]
            k_t = kk[bb][:, h * C:(h + 1) * C]
            v_t = kv[bb][:, h * C:(h + 1) * C]
            b_c = bcol[bb][:, h:h + 1]

            p.op("tensor", lambda e, k_t=k_t: e.matmul(
                psA[:, 0:128], k_t, k_t, start=True, stop=True), sem=pe)
            v_smat = pe.n
            if STAGE < 1:
                p.wait("tensor", dve, dve.n)
                p.wait("tensor", act, act.n)
                continue
            p.wait("vector", pe, v_smat)
            v_atT = p.op("vector", lambda e, b_c=b_c: e.scalar_tensor_tensor(
                aT[:, 0:128], maskU[:], b_c, psA[:, 0:128], ALU.mult, ALU.mult),
                sem=dve)
            if STAGE < 2:
                p.wait("tensor", dve, dve.n)
                p.wait("tensor", act, act.n)
                continue
            p.wait("tensor", dve, v_atT)
            p.op("tensor", lambda e: e.transpose(psT[:, 0:128], aT[:, 0:128], id_bf[:]),
                 sem=pe)
            v_trA = pe.n
            p.wait("scalar", pe, v_trA)
            v_a0 = p.op("scalar", lambda e: e.copy(aN[:, 0:128], psT[:, 0:128]), sem=act)
            if STAGE < 3:
                p.wait("tensor", dve, dve.n)
                p.wait("tensor", act, act.n)
                continue
            prev_evac = ("scalar", v_a0)
            NSQL = int(os.environ.get("SCAN_SQL", "6"))
            for l in range(NSQL):
                p.wait("tensor", act if prev_evac[0] == "scalar" else dve, prev_evac[1])
                p.op("tensor", lambda e, l=l: e.matmul(
                    psB[:, 0:128], aT[:, l * 128:(l + 1) * 128],
                    aN[:, l * 128:(l + 1) * 128], start=True, stop=True))
                p.op("tensor", lambda e, l=l: e.matmul(
                    psC[:, 0:128], aN[:, l * 128:(l + 1) * 128],
                    aT[:, l * 128:(l + 1) * 128], start=True, stop=True), sem=pe)
                v_sq = pe.n
                p.wait("vector", pe, v_sq)
                v_e1 = p.op("vector", lambda e, l=l: e.tensor_copy(
                    aN[:, (l + 1) * 128:(l + 2) * 128], psB[:, 0:128]), sem=dve)
                p.wait("scalar", pe, v_sq)
                v_e2 = p.op("scalar", lambda e, l=l: e.copy(
                    aT[:, (l + 1) * 128:(l + 2) * 128], psC[:, 0:128]), sem=act)
                prev_evac = ("scalar", v_e2)
                p.wait("tensor", dve, v_e1)
            p.wait("tensor", act, prev_evac[1])
            if NSQL < 6:
                p.wait("tensor", dve, dve.n)
                continue

            if STAGE < 4:
                p.wait("tensor", dve, dve.n)
                p.wait("tensor", act, act.n)
                continue
            p.op("tensor", lambda e, k_t=k_t: e.transpose(
                psT[:, 0:64], k_t, id_bf[0:64, 0:64]), sem=pe)
            v_tk = pe.n
            p.wait("vector", pe, v_tk)
            v_kr = p.op("vector", lambda e: e.tensor_copy(kr_t[:, 0:64], psT[:, 0:64]),
                        sem=dve)
            v_x0a = p.op("vector", lambda e: e.tensor_copy(xw[0][:, 0:64], psT[:, 0:64]),
                         sem=dve)
            p.wait("tensor", dve, v_x0a)
            p.op("tensor", lambda e, v_t=v_t: e.transpose(
                psT[:, 0:64], v_t, id_bf[0:64, 0:64]), sem=pe)
            v_tv = pe.n
            p.wait("vector", pe, v_tv)
            v_x0b = p.op("vector", lambda e: e.tensor_copy(xw[0][:, 64:128], psT[:, 0:64]),
                         sem=dve)
            p.wait("tensor", dve, v_x0b)

            if STAGE < 5:
                p.wait("tensor", dve, dve.n)
                p.wait("tensor", act, act.n)
                continue
            xi = 0
            for ai, sign in ((6, 1), (5, 1), (4, 1), (3, 1), (2, 1), (1, 1), (0, -1)):
                p.op("tensor", lambda e, ai=ai, xi=xi: e.matmul(
                    psB[:, 0:128], aT[:, ai * 128:(ai + 1) * 128], xw[xi][:],
                    start=True, stop=True), sem=pe)
                v_ap = pe.n
                p.wait("vector", pe, v_ap)
                if sign > 0:
                    v_x = p.op("vector", lambda e, xi=xi: e.tensor_add(
                        xw[1 - xi][:], xw[xi][:], psB[:, 0:128]), sem=dve)
                else:
                    v_x = p.op("vector", lambda e, xi=xi: e.tensor_sub(
                        xw[1 - xi][:], xw[xi][:], psB[:, 0:128]), sem=dve)
                p.wait("tensor", dve, v_x)
                xi = 1 - xi

            if STAGE < 6:
                p.wait("tensor", dve, dve.n)
                p.wait("tensor", act, act.n)
                continue
            v_w = p.op("vector", lambda e, xi=xi, b_c=b_c: e.tensor_scalar_mul(
                w_t[:, 0:64], xw[xi][:, 0:64], b_c), sem=dve)
            v_u = p.op("vector", lambda e, xi=xi, b_c=b_c: e.tensor_scalar_mul(
                u_t[:, 0:64], xw[xi][:, 64:128], b_c), sem=dve)
            p.wait("tensor", dve, v_u)

            if STAGE < 7:
                p.wait("tensor", dve, dve.n)
                p.wait("tensor", act, act.n)
                continue
            p.op("tensor", lambda e, k_t=k_t, q_t=q_t: e.matmul(
                psA[:, 0:128], k_t, q_t, start=True, stop=True), sem=pe)
            v_gm = pe.n
            p.wait("vector", pe, v_gm)
            v_gt = p.op("vector", lambda e: e.tensor_mul(
                gt[:], psA[:, 0:128], maskLi[:]), sem=dve)
            p.wait("tensor", dve, v_gt)

            if STAGE < 8:
                p.wait("tensor", dve, dve.n)
                p.wait("tensor", act, act.n)
                continue
            p.op("tensor", lambda e, h=h: e.matmul(
                psA[:, 0:64], gt[:], u_t[:, 0:64],
                start=True, stop=True), sem=pe)
            v_oi = pe.n
            p.wait("scalar", pe, v_oi)
            v_oie = p.op("scalar", lambda e, h=h: e.copy(
                o_intra[:, h * 64:(h + 1) * 64], psA[:, 0:64]), sem=act)
            p.wait("tensor", act, v_oie)
            p.op("tensor", lambda e: e.matmul(
                psB[0:64, 0:128], w_t[:, 0:64], gt[:], start=True, stop=True), sem=pe)
            v_qm = pe.n
            p.wait("vector", pe, v_qm)
            v_qt = p.op("vector", lambda e, h=h, q_t=q_t: e.tensor_sub(
                qt_all[:, h * C:(h + 1) * C], q_t, psB[0:64, 0:128]), sem=dve)
            p.op("tensor", lambda e: e.matmul(
                psA[0:64, 0:64], w_t[:, 0:64], kr_t[:, 0:64], start=True, stop=True),
                sem=pe)
            v_rt = pe.n
            p.wait("tensor", dve, v_qt)
            p.op("tensor", lambda e: e.matmul(
                psB[0:64, 0:64], kr_t[:, 0:64], u_t[:, 0:64], start=True, stop=True),
                sem=pe)
            v_pm = pe.n
            p.wait("scalar", pe, v_rt)
            v_rte = p.op("scalar", lambda e, h=h: e.copy(
                rt_all[:, h * 64:(h + 1) * 64], psA[0:64, 0:64]), sem=act)
            p.wait("scalar", pe, v_pm)
            v_pme = p.op("scalar", lambda e, h=h: e.copy(
                pm_all[:, h * 64:(h + 1) * 64], psB[0:64, 0:64]), sem=act)
            p.wait("tensor", act, v_pme)
            p.wait("tensor", dve, v_qt)

        if STAGE < 9:
            comp_done[ic] = pe.n
            continue
        if ic > 0:
            p.wait("vector", sto, sto.n)
        corr_free = [None, None]
        for h in range(NH):
            zslc = zcur[:, h * 64:(h + 1) * 64]
            bank = psz[h % 2]
            cbank = psA if h % 2 == 0 else psB
            v_zb = p.op("vector", lambda e, zslc=zslc: e.tensor_copy(
                z_bf[:, 0:64], zslc), sem=dve)
            p.wait("tensor", dve, v_zb)
            if corr_free[h % 2] is not None:
                p.wait("tensor", dve, corr_free[h % 2])
            p.op("tensor", lambda e, h=h, cbank=cbank: e.matmul(
                cbank[:, 0:64], qt_all[:, h * C:(h + 1) * C],
                z_bf[:, 0:64], start=True, stop=True), sem=pe)
            v_corr = pe.n
            p.op("tensor", lambda e, h=h, zslc=zslc, bank=bank: e.matmul(
                bank[0:64, 0:64], rt_all[:, h * 64:(h + 1) * 64], zslc,
                start=True, stop=True), sem=pe)
            v_zm = pe.n
            p.wait("vector", pe, v_corr)
            v_oh = p.op("vector", lambda e, h=h, cbank=cbank: e.tensor_add(
                o_tok[:, (h % 2) * 64:(h % 2) * 64 + 64],
                cbank[:, 0:64], o_intra[:, h * 64:(h + 1) * 64]), sem=dve)
            corr_free[h % 2] = v_oh
            if h % 2 == 1:
                p.wait("tensor", dve, v_oh)
                p.op("tensor", lambda e: e.transpose(
                    psT[:, 0:128], o_tok[:], id_bf[:]), sem=pe)
                v_otr = pe.n
                p.wait("vector", pe, v_otr)
                v_stash = p.op("vector", lambda e, h=h: e.tensor_copy(
                    o_stash[:, (h // 2) * 128:(h // 2 + 1) * 128],
                    psT[:, 0:128]), sem=dve)
                p.wait("tensor", dve, v_stash)
            p.wait("vector", pe, v_zm)
            p.op("vector", lambda e, zslc=zslc, bank=bank: e.scalar_tensor_tensor(
                ztmp[:, 0:64], zslc, 1.0, bank[0:64, 0:64], ALU.mult, ALU.subtract),
                sem=dve)
            v_zn = p.op("vector", lambda e, h=h, znxt=znxt: e.tensor_add(
                znxt[:, h * 64:(h + 1) * 64], ztmp[:, 0:64],
                pm_all[:, h * 64:(h + 1) * 64]), sem=dve)
            p.wait("tensor", dve, v_zn)
        v_oev = dve.n
        p.wait("sync", dve, v_oev)
        p.dma("sync", o_out[:, i * C:(i + 1) * C].rearrange("(hp d) t -> d hp t", d=128),
              o_stash[:].rearrange("d (hp t) -> d hp t", hp=4), sto)
        p.wait("tensor", dve, v_oev)
        comp_done[ic] = pe.n

    zfin = zc[(NCH + (ci0 % 2)) % 2]
    p.wait("sync", dve, dve.n)
    p.dma("sync", z_out.rearrange("(h d) e -> d h e", d=64),
          zfin[:].rearrange("d (h e) -> d h e", h=NH), szf)
    p.wait("sync", sto, sto.n)
    p.finish(szf, szf.n)
    p.end_phase()
    p.pop()
    return p



def build_attnout(p=None, rename=None):
    standalone = p is None
    if standalone:
        p = Prog('attnout')
    p.push(rename)
    p.begin_phase('ao_')
    T = 256
    NTILE = NT // T
    o_in = p.din("o_fm", (D, NT), BF16)
    xn_fm = p.din("xn_fm", (D, NT), F32)
    wo = p.din("wo", (D, D), F32)
    w1 = p.din("w1", (D, 4096), F32)
    w2 = p.din("w2", (4096, D), F32)
    bo_c = p.din("bo_c", (D, 1), F32)
    ln2_c = p.din("ln2_c", (D, 1), F32)
    b1_c = p.din("b1_c", (4096, 1), F32)
    b2_c = p.din("b2_c", (D, 1), F32)
    o_acc1 = p.dout("acc1_fm", (D, NT), F32)

    wo_sb = p.sb("wo_sb", (128, 8 * D), BF16)
    w1_sb = p.sb("w1_sb", (128, 8 * 4096), BF16)
    w2_sb = p.sb("w2_sb", (128, 32 * D), BF16)
    bcols = p.sb("bcols", (128, 8 + 8 + 8 + 32), F32)
    ones_f = p.sb("ones_f", (128, 1), F32)
    onesr_f = p.sb("onesr_f", (1, 128), F32)
    cvals = p.sb("cvals", (128, 1), F32)
    ob = p.sb("ob", (128, NCHUNK * T), BF16)
    xnb = p.sb("xnb", (128, NCHUNK * T), F32)
    y1 = p.sb("y1", (128, NCHUNK * T), F32)
    xh2 = p.sb("xh2", (128, NCHUNK * T), BF16)
    hb = p.sb("hb", (128, 32 * T), BF16)
    sq = p.sb("sq", (128, NCHUNK * T), F32)
    stat = p.sb("stat", (1, 2 * T), F32)

    ps_m = [p.ps("ps_m0", (128, 512)), p.ps("ps_m1", (128, 512))]
    ps_r = p.ps("ps_r", (1, 512))
    ps_bc = p.ps("ps_bc", (128, 512))

    ld = p.sem("ld")
    lw = p.sem("lw")
    li = p.sem("li")
    pe = p.sem("pe")
    dve = p.stream_sem("vector")
    act = p.stream_sem("scalar")
    gp = p.stream_sem("gpsimd")
    st = p.sem("st")

    for dc in range(NCHUNK):
        p.dma("gpsimd", wo_sb[:, dc * D:(dc + 1) * D],
              wo[dc * 128:(dc + 1) * 128, :], lw, gate=False)
        p.dma("gpsimd", w1_sb[:, dc * 4096:(dc + 1) * 4096],
              w1[dc * 128:(dc + 1) * 128, :], lw, gate=False)
    for fc in range(32):
        p.dma("gpsimd", w2_sb[:, fc * D:(fc + 1) * D],
              w2[fc * 128:(fc + 1) * 128, :], lw, gate=False)
    v_lw = lw.n
    p.dma("sync", bcols[:, 0:8], bo_c.rearrange("(c p) one -> p (c one)", p=128), ld, slow=True)
    p.dma("sync", bcols[:, 8:16], ln2_c.rearrange("(c p) one -> p (c one)", p=128), ld, slow=True)
    p.dma("sync", bcols[:, 16:24], b2_c.rearrange("(c p) one -> p (c one)", p=128), ld, slow=True)
    p.dma("sync", bcols[:, 24:56], b1_c.rearrange("(c p) one -> p (c one)", p=128), ld, slow=True)
    v_ld = ld.n
    p.op("gpsimd", lambda e: e.memset(ones_f[:], 1.0), sem=gp)
    p.op("gpsimd", lambda e: e.memset(onesr_f[:], 1.0), sem=gp)
    p.op("gpsimd", lambda e: e.memset(cvals[:, 0:1], 1e-6), sem=gp)
    v_gp = gp.n

    p.wait("vector", ld, v_ld)
    p.wait("scalar", ld, v_ld)
    p.wait("tensor", lw, v_lw)
    p.wait("tensor", gp, v_gp)
    p.wait("scalar", gp, v_gp)

    psum_free = [None, None]
    comp_done = [0] * NTILE

    def waitfree(nb):
        if psum_free[nb] is not None:
            eng, v = psum_free[nb]
            p.wait("tensor", {"dve": dve, "act": act}[eng], v)

    for i in range(NTILE):
        if i >= 1:
            p.wait("sync", dve, comp_done[i - 1])
            p.wait("vector", st, st.n)
        v_li = p.dma("sync", ob[:].rearrange("p (c t) -> p c t", c=NCHUNK),
                     o_in[:, i * T:(i + 1) * T].rearrange("(c p) t -> p c t", p=128), li)
        v_li = p.dma("sync", xnb[:].rearrange("p (c t) -> p c t", c=NCHUNK),
                     xn_fm[:, i * T:(i + 1) * T].rearrange("(c p) t -> p c t", p=128), li)
        p.wait("tensor", li, v_li)
        p.wait("vector", li, v_li)

        for mo in range(NCHUNK):
            nb = mo % 2
            bank = ps_m[nb]
            waitfree(nb)
            for dc in range(NCHUNK):
                p.op("tensor", lambda e, mo=mo, dc=dc, bank=bank: e.matmul(
                    bank[:, 0:T], wo_sb[:, (dc * 8 + mo) * 128:(dc * 8 + mo + 1) * 128],
                    ob[:, dc * T:(dc + 1) * T],
                    start=(dc == 0), stop=(dc == NCHUNK - 1)), sem=pe)
            v_mm = pe.n
            p.wait("vector", pe, v_mm)
            v_ev = p.op("vector", lambda e, mo=mo, bank=bank: e.scalar_tensor_tensor(
                y1[:, mo * T:(mo + 1) * T], xnb[:, mo * T:(mo + 1) * T],
                bcols[:, mo:mo + 1], bank[:, 0:T], ALU.add, ALU.add), sem=dve)
            psum_free[nb] = ("dve", v_ev)

        p.wait("scalar", dve, dve.n)
        for c in range(NCHUNK):
            p.op("scalar", lambda e, c=c: e.activation(
                sq[:, c * T:(c + 1) * T], y1[:, c * T:(c + 1) * T], AF.Square), sem=act)
            v_sq = act.n
            p.wait("tensor", act, v_sq)
            p.op("tensor", lambda e, c=c: e.matmul(
                ps_r[:, 0:T], ones_f[:], sq[:, c * T:(c + 1) * T],
                start=(c == 0), stop=(c == NCHUNK - 1)), sem=pe)
        v_st = pe.n
        p.wait("scalar", pe, v_st)
        p.op("scalar", lambda e: e.activation(stat[:, 0:T], ps_r[:, 0:T], AF.Ln,
                                              bias=cvals[0:1, 0:1], scale=1.0 / D), sem=act)
        v_r2 = p.op("scalar", lambda e: e.activation(stat[:, 0:T], stat[:, 0:T],
                                                     AF.Exp, scale=-0.5), sem=act)
        p.wait("tensor", act, v_r2)
        p.op("tensor", lambda e: e.matmul(ps_bc[:, 0:T], onesr_f[:], stat[:, 0:T],
                                          start=True, stop=True), sem=pe)
        v_bc = pe.n
        p.wait("vector", pe, v_bc)
        for c in range(NCHUNK):
            p.op("vector", lambda e, c=c: e.scalar_tensor_tensor(
                xh2[:, c * T:(c + 1) * T], y1[:, c * T:(c + 1) * T],
                bcols[:, 8 + c:8 + c + 1], ps_bc[:, 0:T], ALU.mult, ALU.mult), sem=dve)
        v_xh2 = dve.n
        p.wait("tensor", dve, v_xh2)

        for fc in range(32):
            nb = fc % 2
            bank = ps_m[nb]
            waitfree(nb)
            for dc in range(NCHUNK):
                p.op("tensor", lambda e, fc=fc, dc=dc, bank=bank: e.matmul(
                    bank[:, 0:T], w1_sb[:, dc * 4096 + fc * 128:dc * 4096 + (fc + 1) * 128],
                    xh2[:, dc * T:(dc + 1) * T],
                    start=(dc == 0), stop=(dc == NCHUNK - 1)), sem=pe)
            v_mm = pe.n
            p.wait("scalar", pe, v_mm)
            v_g = p.op("scalar", lambda e, fc=fc, bank=bank: e.activation(
                hb[:, fc * T:(fc + 1) * T], bank[:, 0:T], AF.Gelu_apprx_tanh,
                bias=bcols[:, 24 + fc:24 + fc + 1]), sem=act)
            psum_free[nb] = ("act", v_g)

        p.wait("tensor", act, act.n)
        for mo in range(NCHUNK):
            nb = mo % 2
            bank = ps_m[nb]
            waitfree(nb)
            for fc in range(32):
                p.op("tensor", lambda e, mo=mo, fc=fc, bank=bank: e.matmul(
                    bank[:, 0:T], w2_sb[:, fc * D + mo * 128:fc * D + (mo + 1) * 128],
                    hb[:, fc * T:(fc + 1) * T],
                    start=(fc == 0), stop=(fc == 31)), sem=pe)
            v_mm = pe.n
            p.wait("vector", pe, v_mm)
            v_a = p.op("vector", lambda e, mo=mo, bank=bank: e.scalar_tensor_tensor(
                y1[:, mo * T:(mo + 1) * T], y1[:, mo * T:(mo + 1) * T],
                bcols[:, 16 + mo:16 + mo + 1], bank[:, 0:T], ALU.add, ALU.add), sem=dve)
            psum_free[nb] = ("dve", v_a)

        v_done = dve.n
        p.wait("sync", dve, v_done)
        p.dma("sync", o_acc1[:, i * T:(i + 1) * T].rearrange("(c p) t -> p c t", p=128),
              y1[:].rearrange("p (c t) -> p c t", c=NCHUNK), st)
        comp_done[i] = v_done

    p.finish(st, st.n)
    p.end_phase()
    p.pop()
    return p



def build_x1init(p=None, rename=None):
    standalone = p is None
    if standalone:
        p = Prog('x1init')
    p.push(rename)
    p.begin_phase('xi_')
    T = 512
    NTILE = NT // T
    x_fm = p.din("x_fm", (D, NT), F32)
    acc1 = p.din("acc1_fm", (D, NT), F32)
    xn_fm = p.din("xn_fm", (D, NT), F32)
    o_acc = p.dout("x1acc", (D, NT), F32)
    a = p.sb("a", (128, NCHUNK * T), F32)
    b = p.sb("b", (128, NCHUNK * T), F32)
    c = p.sb("c", (128, NCHUNK * T), F32)
    li = p.sem("li")
    dve = p.stream_sem("vector")
    st = p.sem("st")
    for i in range(NTILE):
        if i > 0:
            p.wait("sync", dve, dve.n)
        for t, src in ((a, x_fm), (b, acc1), (c, xn_fm)):
            p.dma("sync", t[:].rearrange("p (c t) -> p c t", c=NCHUNK),
                  src[:, i * T:(i + 1) * T].rearrange("(c p) t -> p c t", p=128), li,
                  gate=False)
        v_li = p.wait("vector", li, li.n)
        if i > 0:
            p.wait("vector", st, st.n)
        p.op("vector", lambda e: e.tensor_add(a[:], a[:], b[:]), sem=dve)
        p.op("vector", lambda e: e.tensor_add(a[:], a[:], c[:]), sem=dve)
        p.wait("sync", dve, dve.n)
        p.dma("sync", o_acc[:, i * T:(i + 1) * T].rearrange("(c p) t -> p c t", p=128),
              a[:].rearrange("p (c t) -> p c t", c=NCHUNK), st)
    p.finish(st, st.n)
    p.end_phase()
    p.pop()
    return p



def build_moe(name, e0, e1, layer, p=None, rename=None):
    standalone = p is None
    if standalone:
        p = Prog(name)
    p.push(rename)
    p.begin_phase(name + "_")
    T = 512
    NTILE = NT // T
    F = 4096
    xhat = p.din("xhat_bf", (D, NT), BF16)
    acc_in = p.din("acc_in", (D, NT), F32)
    wm = p.din("wm_fm", (8, NT), F32)
    w1d = p.din("W1T", (8, D, F), F32)
    w2d = p.din("W2T", (8, F, D), F32)
    b1d = p.din("b1e", (8, F), F32)
    b2d = p.din("b2e", (8, D), F32)
    if layer == 0:
        gad = p.din("gammae", (8, D), F32)
        bed = p.din("betae", (8, D), F32)
    o_acc = p.dout("acc_out", (D, NT), F32)

    w1_sb = p.sb("w1_sb", (128, 8 * F), BF16)
    w2_sb = p.sb("w2_sb", (128, 32 * D), BF16)
    xh = p.sb("xh", (128, NCHUNK * T), BF16)
    accb = p.sb("accb", (128, NCHUNK * T), F32)
    hb = p.sb("hb", (128, 32 * T), BF16)
    w0row = p.sb("w0row", (1, NT), BF16)
    onesr_b = p.sb("onesr_b", (1, 128), BF16)
    if layer == 0:
        b1row = p.sb("b1row", (1, F), BF16)
        b2row = p.sb("b2row", (1, D), BF16)
        gcols = p.sb("gcols", (128, 8), F32)
        becols = p.sb("becols", (128, 8), BF16)
        onesr_f = p.sb("onesr_f", (1, 128), F32)
    else:
        b1cols = p.sb("b1cols", (128, 32), F32)
        b2cols = p.sb("b2cols", (128, 8), F32)
        w_bc = p.sb("w_bc", (128, T), F32)
        tmp2 = p.sb("tmp2", (128, T), F32)

    ps_w = p.ps("ps_w", (128, 512))
    ps_m = [p.ps("ps_m0", (128, 512)), p.ps("ps_m1", (128, 512))]
    if layer == 0:
        ps_b1 = p.ps("ps_b1", (1, 512))

    lw1 = p.sem("lw1")
    lw2 = p.sem("lw2")
    lr = p.sem("lr")
    lrs = p.sem("lrs")
    la = p.sem("la")
    lx = p.sem("lx")
    pe = p.sem("pe")
    dve = p.stream_sem("vector")
    act = p.stream_sem("scalar")
    gp = p.stream_sem("gpsimd")
    sa = p.sem("sa")

    p.op("gpsimd", lambda e_: e_.memset(onesr_b[:], 1.0), sem=gp)
    if layer == 0:
        p.op("gpsimd", lambda e_: e_.memset(onesr_f[:], 1.0), sem=gp)
    v_gp = gp.n
    p.wait("tensor", gp, v_gp)

    AFUN = AF.Relu if layer == 0 else AF.Gelu_apprx_tanh
    NE = e1 - e0
    v_lw1 = [0] * (NE + 1)
    v_lw2 = [0] * (NE + 1)
    v_lrs = [0] * (NE + 1)

    def load_w1(ei):
        ex = e0 + ei
        for dc in range(NCHUNK):
            p.dma("gpsimd", w1_sb[:, dc * F:(dc + 1) * F],
                  w1d[ex, dc * 128:(dc + 1) * 128, :], lw1, gate=False)
        v_lw1[ei] = lw1.n

    def load_w2(ei):
        ex = e0 + ei
        for fc in range(32):
            p.dma("gpsimd", w2_sb[:, fc * D:(fc + 1) * D],
                  w2d[ex, fc * 128:(fc + 1) * 128, :], lw2, gate=False)
        v_lw2[ei] = lw2.n

    def load_rows(ei):
        e = e0 + ei
        p.dma("gpsimd", w0row[:], wm[e:e + 1, :], lr, gate=False)
        if layer == 0:
            p.dma("gpsimd", b1row[:], b1d[e:e + 1, :], lr, gate=False)
            p.dma("gpsimd", b2row[:], b2d[e:e + 1, :], lr, gate=False)
            p.dma("gpsimd", gcols[:], gad[e:e + 1, :].rearrange(
                "one (c pp) -> pp (c one)", pp=128), lrs, slow=True, gate=False)
            p.dma("gpsimd", becols[:], bed[e:e + 1, :].rearrange(
                "one (c pp) -> pp (c one)", pp=128), lr, slow=True, gate=False)
        else:
            p.dma("gpsimd", b1cols[:], b1d[e:e + 1, :].rearrange(
                "one (fc pp) -> pp (fc one)", pp=128), lr, slow=True, gate=False)
            p.dma("gpsimd", b2cols[:], b2d[e:e + 1, :].rearrange(
                "one (mo pp) -> pp (mo one)", pp=128), lr, slow=True, gate=False)
        v_lrs[ei] = lr.n

    load_rows(0)
    load_w1(0)
    load_w2(0)

    lx_val = [0]
    v_la = [0]
    sa_val = [0]
    wb_guard = [0]

    for ei, e in enumerate(range(e0, e1)):
        v_lr = v_lrs[ei]
        p.wait("tensor", lw1, v_lw1[ei])
        p.wait("tensor", lr, v_lr)
        if layer == 0:
            p.wait("vector", lw1, v_lw1[ei])
            p.wait("vector", lr, v_lr)
            p.wait("vector", lrs, lrs.n)
            p.wait("tensor", lrs, lrs.n)
        else:
            p.wait("scalar", lr, v_lr)
            p.wait("vector", lr, v_lr)

        if layer == 0:
            for fg in range(8):
                p.wait("tensor", dve, dve.n)
                for dc in range(NCHUNK):
                    p.op("tensor", lambda e_, fg=fg, dc=dc: e_.matmul(
                        ps_b1[:, 0:512], becols[:, dc:dc + 1],
                        w1_sb[:, dc * F + fg * 512:dc * F + (fg + 1) * 512],
                        start=(dc == 0), stop=(dc == NCHUNK - 1)),
                        sem=pe if dc == NCHUNK - 1 else None)
                v_mv = pe.n
                p.wait("vector", pe, v_mv)
                p.op("vector", lambda e_, fg=fg: e_.tensor_add(
                    b1row[:, fg * 512:(fg + 1) * 512], b1row[:, fg * 512:(fg + 1) * 512],
                    ps_b1[:, 0:512]), sem=dve)
            for dc in range(NCHUNK):
                p.op("vector", lambda e_, dc=dc: e_.tensor_scalar_mul(
                    w1_sb[:, dc * F:(dc + 1) * F], w1_sb[:, dc * F:(dc + 1) * F],
                    gcols[:, dc:dc + 1]), sem=dve)
            v_fold = dve.n
            p.wait("tensor", dve, v_fold)

        for i in range(NTILE):
            glob = ei * NTILE + i
            if glob == 0:
                lx_val[0] = p.dma(
                    "sync", xh[:].rearrange("p (c t) -> p c t", c=NCHUNK),
                    xhat[:, 0:T].rearrange("(c p) t -> p c t", p=128), lx, gate=False)
            if glob > 0:
                p.wait("sync", sa, sa_val[0])
            src = acc_in if ei == 0 else o_acc
            v_la[0] = p.dma("sync", accb[:].rearrange("p (c t) -> p c t", c=NCHUNK),
                            src[:, i * T:(i + 1) * T].rearrange("(c p) t -> p c t", p=128),
                            la, gate=False)

            p.wait("tensor", lx, lx_val[0])
            p.wait("tensor", dve, wb_guard[0])
            p.op("tensor", lambda e_, i=i: e_.matmul(
                ps_w[:, 0:T], onesr_b[:], w0row[:, i * T:(i + 1) * T],
                start=True, stop=True), sem=pe)
            v_w = pe.n
            p.wait("vector", pe, v_w)
            if layer == 0:
                for c in range(NCHUNK):
                    p.op("vector", lambda e_, c=c: e_.tensor_mul(
                        xh[:, c * T:(c + 1) * T], xh[:, c * T:(c + 1) * T],
                        ps_w[:, 0:T]), sem=dve)
                v_xe = dve.n
                p.wait("tensor", dve, v_xe)
            else:
                v_xe = p.op("vector", lambda e_: e_.tensor_copy(
                    w_bc[:], ps_w[:, 0:T]), sem=dve)
            wb_guard[0] = v_xe

            psum_free = [None, None]
            for fc in range(32):
                nb = fc % 2
                bank = ps_m[nb]
                if psum_free[nb] is not None:
                    p.wait("tensor", act, psum_free[nb])
                for dc in range(NCHUNK):
                    lastmm = (dc == NCHUNK - 1) and layer != 0
                    p.op("tensor", lambda e_, fc=fc, dc=dc, bank=bank: e_.matmul(
                        bank[:, 0:T], w1_sb[:, dc * F + fc * 128:dc * F + (fc + 1) * 128],
                        xh[:, dc * T:(dc + 1) * T],
                        start=(dc == 0), stop=(dc == NCHUNK - 1 and layer != 0)),
                        sem=pe if lastmm else None)
                if layer == 0:
                    p.op("tensor", lambda e_, fc=fc, i=i, bank=bank: e_.matmul(
                        bank[:, 0:T], b1row[:, fc * 128:(fc + 1) * 128],
                        w0row[:, i * T:(i + 1) * T], start=False, stop=True), sem=pe)
                v_mm = pe.n
                p.wait("scalar", pe, v_mm)
                if layer == 0:
                    v_h = p.op("scalar", lambda e_, fc=fc, bank=bank: e_.activation(
                        hb[:, fc * T:(fc + 1) * T], bank[:, 0:T], AFUN), sem=act)
                else:
                    v_h = p.op("scalar", lambda e_, fc=fc, bank=bank: e_.activation(
                        hb[:, fc * T:(fc + 1) * T], bank[:, 0:T], AFUN,
                        bias=b1cols[:, fc:fc + 1]), sem=act)
                psum_free[nb] = v_h
            comp_w1 = pe.n
            p.wait("tensor", act, act.n)

            if glob + 1 < NE * NTILE:
                i2 = (i + 1) % NTILE
                p.wait("sync", pe, comp_w1)
                lx_val[0] = p.dma(
                    "sync", xh[:].rearrange("p (c t) -> p c t", c=NCHUNK),
                    xhat[:, i2 * T:(i2 + 1) * T].rearrange("(c p) t -> p c t", p=128),
                    lx, gate=False)
            if i == NTILE - 1 and ei + 1 < NE:
                p.wait("gpsimd", pe, comp_w1)
                load_w1(ei + 1)

            if i == 0:
                p.wait("tensor", lw2, v_lw2[ei])
            psum_free = [None, None]
            for mo in range(NCHUNK):
                nb = mo % 2
                bank = ps_m[nb]
                if psum_free[nb] is not None:
                    p.wait("tensor", dve, psum_free[nb])
                for fc in range(32):
                    lastmm = (fc == 31) and layer != 0
                    p.op("tensor", lambda e_, mo=mo, fc=fc, bank=bank: e_.matmul(
                        bank[:, 0:T], w2_sb[:, fc * D + mo * 128:fc * D + (mo + 1) * 128],
                        hb[:, fc * T:(fc + 1) * T],
                        start=(fc == 0), stop=(fc == 31 and layer != 0)),
                        sem=pe if lastmm else None)
                if layer == 0:
                    p.op("tensor", lambda e_, mo=mo, i=i, bank=bank: e_.matmul(
                        bank[:, 0:T], b2row[:, mo * 128:(mo + 1) * 128],
                        w0row[:, i * T:(i + 1) * T], start=False, stop=True), sem=pe)
                v_mm = pe.n
                p.wait("vector", pe, v_mm)
                p.wait("vector", la, v_la[0])
                if layer == 0:
                    v_a = p.op("vector", lambda e_, mo=mo, bank=bank: e_.tensor_add(
                        accb[:, mo * T:(mo + 1) * T], accb[:, mo * T:(mo + 1) * T],
                        bank[:, 0:T]), sem=dve)
                else:
                    p.op("vector", lambda e_, mo=mo, bank=bank: e_.scalar_tensor_tensor(
                        tmp2[:], bank[:, 0:T], b2cols[:, mo:mo + 1], w_bc[:],
                        ALU.add, ALU.mult), sem=dve)
                    v_a = p.op("vector", lambda e_, mo=mo: e_.tensor_add(
                        accb[:, mo * T:(mo + 1) * T], accb[:, mo * T:(mo + 1) * T],
                        tmp2[:]), sem=dve)
                psum_free[nb] = v_a
            v_done = dve.n
            p.wait("sync", dve, v_done)
            p.dma("sync", o_acc[:, i * T:(i + 1) * T].rearrange("(c p) t -> p c t", p=128),
                  accb[:].rearrange("p (c t) -> p c t", c=NCHUNK), sa, gate=False)
            sa_val[0] = sa.n

        if ei + 1 < NE:
            p.wait("gpsimd", pe, pe.n)
            load_rows(ei + 1)
            load_w2(ei + 1)

    p.finish(sa, sa.n)
    p.end_phase()
    p.pop()
    return p



def build_ph1(p=None, rename=None):
    standalone = p is None
    if standalone:
        p = Prog('ph1')
    p.push(rename)
    p.begin_phase('p1_')
    T = 512
    NTILE = NT // T
    x1 = p.din("x1_fm", (D, NT), F32)
    wg = p.din("Wg", (D, 8), F32)
    o_xn1 = p.dout("xn1_bf", (D, NT), BF16)
    o_g1 = p.dout("g1_row", (1, NT), F32)
    o_g2 = p.dout("g2_row", (1, NT), F32)
    o_pos = p.dout("pos_incl", (1, 2 * NT), F32)
    o_cnt = p.dout("counts", (8, 1), F32)
    o_ohE = p.dout("ohE_out", (8, 2 * NT), F32)
    o_gE = p.dout("gE_out", (1, 2 * NT), F32)

    idf = p.const("c_idf", np.eye(128, dtype=np.float32))

    xs = [p.sb("x0", (128, NCHUNK * T), F32)] * 2
    xn = p.sb("xn", (128, NCHUNK * T), F32)
    sq = p.sb("sq", (128, 2 * T), F32)
    out_bf = p.sb("out_bf", (128, NCHUNK * T), BF16)
    ones_f = p.sb("ones_f", (128, 1), F32)
    onesr_f = p.sb("onesr_f", (1, 128), F32)
    cvals = p.sb("cvals", (128, 1), F32)
    wg_sb = p.sb("wg_sb", (128, NCHUNK * 8), F32)
    id_f = p.sb("id_f", (128, 128), F32)
    stat = p.sb("stat", (1, 4 * T), F32)
    sc_sb = p.sb("sc_sb", (8, T), F32)
    sc_tok = p.sb("sc_tok", (128, 8), F32)
    mx8 = p.sb("mx8", (128, 8), F32)
    oh_tok = p.sb("oh_tok", (128, 16), F32)
    gg = p.sb("gg", (128, 4), F32)
    g1_sb = p.sb("g1_sb", (1, NT), F32)
    g2_sb = p.sb("g2_sb", (1, NT), F32)
    ohEc = p.sb("ohEc", (8, 2 * NT), F32)
    ohC = p.sb("ohC", (8, 2 * NT), F32)
    gEr = p.sb("gEr", (1, 2 * NT), F32)
    cnt_sb = p.sb("cnt_sb", (8, 1), F32)
    pos_sb = p.sb("pos_sb", (1, 512), F32)

    ps_mu = p.ps("ps_mu", (1, 512))
    ps_sq = p.ps("ps_sq", (1, 512))
    ps_b1 = p.ps("ps_b1", (128, 512))
    ps_b2 = p.ps("ps_b2", (128, 512))
    ps_sc = p.ps("ps_sc", (8, 512))
    ps_tr = p.ps("ps_tr", (128, 512))

    ld = p.sem("ld")
    lx = [p.sem("lx0"), p.sem("lx1")]
    pe = p.sem("pe")
    dve = p.stream_sem("vector")
    act = p.stream_sem("scalar")
    gp = p.stream_sem("gpsimd")
    st = p.sem("st")
    sxn = p.sem("sxn")

    p.dma("sync", wg_sb[:].rearrange("p (c e) -> p c e", e=8),
          wg.rearrange("(c p) e -> p c e", p=128), ld)
    p.dma("sync", id_f[:], idf, ld)
    v_ld = ld.n
    p.op("gpsimd", lambda e: e.memset(ones_f[:], 1.0), sem=gp)
    p.op("gpsimd", lambda e: e.memset(onesr_f[:], 1.0), sem=gp)
    p.op("gpsimd", lambda e: e.memset(cvals[:, 0:1], 1e-5), sem=gp)
    v_gp = gp.n
    p.wait("tensor", gp, v_gp)
    p.wait("tensor", ld, v_ld)
    p.wait("scalar", gp, v_gp)
    p.wait("vector", ld, v_ld)

    MU, RSTD, MR, TMP = range(4)

    def srow(k):
        return stat[:, k * T:(k + 1) * T]

    comp_done = [0] * NTILE
    sq_guard = [0] * NTILE
    ps_tr_free = [0]

    for i in range(NTILE):
        xb = xs[0]
        if i >= 1:
            p.wait("sync", dve, comp_done[i - 1])
        v_lx = p.dma("sync", xb[:].rearrange("p (c t) -> p c t", c=NCHUNK),
                     x1[:, i * T:(i + 1) * T].rearrange("(c p) t -> p c t", p=128),
                     lx[0])
        p.wait("scalar", lx[0], v_lx)
        p.wait("tensor", lx[0], v_lx)
        sq_cons = [0, 0]
        for c in range(NCHUNK):
            if c >= 2:
                p.wait("scalar", pe, sq_cons[c % 2])
            p.op("scalar", lambda e, c=c, xb=xb: e.activation(
                sq[:, (c % 2) * T:(c % 2 + 1) * T], xb[:, c * T:(c + 1) * T],
                AF.Square), sem=act)
            v_sq = act.n
            p.op("tensor", lambda e, c=c, xb=xb: e.matmul(
                ps_mu[:, 0:T], ones_f[:], xb[:, c * T:(c + 1) * T],
                start=(c == 0), stop=(c == NCHUNK - 1)))
            p.wait("tensor", act, v_sq)
            p.op("tensor", lambda e, c=c: e.matmul(
                ps_sq[:, 0:T], ones_f[:], sq[:, (c % 2) * T:(c % 2 + 1) * T],
                start=(c == 0), stop=(c == NCHUNK - 1)), sem=pe)
            sq_cons[c % 2] = pe.n
        v_stats = pe.n
        sq_guard[i] = v_stats
        p.wait("vector", pe, v_stats)
        p.op("vector", lambda e: e.tensor_scalar_mul(srow(MU), ps_mu[:, 0:T], 1.0 / D), sem=dve)
        p.op("vector", lambda e: e.tensor_scalar_mul(srow(TMP), ps_sq[:, 0:T], 1.0 / D), sem=dve)
        p.op("vector", lambda e: e.tensor_mul(srow(RSTD), srow(MU), srow(MU)), sem=dve)
        v_var = p.op("vector", lambda e: e.tensor_sub(srow(TMP), srow(TMP), srow(RSTD)),
                     sem=dve)
        p.wait("scalar", dve, v_var)
        p.op("scalar", lambda e: e.activation(srow(RSTD), srow(TMP), AF.Ln,
                                              bias=cvals[0:1, 0:1]), sem=act)
        v_rstd = p.op("scalar", lambda e: e.activation(srow(RSTD), srow(RSTD), AF.Exp,
                                                       scale=-0.5), sem=act)
        p.wait("vector", act, v_rstd)
        v_mr = p.op("vector", lambda e: e.tensor_mul(srow(MR), srow(MU), srow(RSTD)), sem=dve)
        p.wait("tensor", dve, v_mr)
        p.op("tensor", lambda e: e.matmul(ps_b1[:, 0:T], onesr_f[:], srow(RSTD),
                                          start=True, stop=True))
        p.op("tensor", lambda e: e.matmul(ps_b2[:, 0:T], onesr_f[:], srow(MR),
                                          start=True, stop=True), sem=pe)
        v_bc = pe.n
        p.wait("vector", pe, v_bc)
        if i > 0:
            p.wait("vector", sxn, sxn.n)
        for c in range(NCHUNK):
            p.op("vector", lambda e, c=c, xb=xb: e.tensor_mul(
                xn[:, c * T:(c + 1) * T], xb[:, c * T:(c + 1) * T], ps_b1[:, 0:T]), sem=dve)
            p.op("vector", lambda e, c=c: e.tensor_sub(
                xn[:, c * T:(c + 1) * T], xn[:, c * T:(c + 1) * T], ps_b2[:, 0:T]), sem=dve)
            p.op("vector", lambda e, c=c: e.tensor_copy(
                out_bf[:, c * T:(c + 1) * T], xn[:, c * T:(c + 1) * T]), sem=dve)
        v_xn = dve.n
        p.wait("sync", dve, v_xn)
        p.dma("sync", o_xn1[:, i * T:(i + 1) * T].rearrange("(c p) t -> p c t", p=128),
              out_bf[:].rearrange("p (c t) -> p c t", c=NCHUNK), sxn)

        p.wait("tensor", dve, v_xn)
        for c in range(NCHUNK):
            p.op("tensor", lambda e, c=c: e.matmul(
                ps_sc[:, 0:T], wg_sb[:, c * 8:(c + 1) * 8], xn[:, c * T:(c + 1) * T],
                start=(c == 0), stop=(c == NCHUNK - 1)), sem=pe)
        v_sc = pe.n
        p.wait("vector", pe, v_sc)
        v_scsb = p.op("vector", lambda e: e.tensor_copy(sc_sb[:], ps_sc[:, 0:T]), sem=dve)

        for s in range(4):
            p.wait("tensor", dve, max(v_scsb, ps_tr_free[0]))
            p.op("tensor", lambda e, s=s: e.transpose(
                ps_tr[:, 0:8], sc_sb[:, s * 128:(s + 1) * 128], id_f[0:8, 0:8]), sem=pe)
            v_tr = pe.n
            p.wait("vector", pe, v_tr)
            v_tok = p.op("vector", lambda e: e.tensor_copy(sc_tok[:], ps_tr[:, 0:8]), sem=dve)
            v_mx = p.op("vector", lambda e: e.max(mx8[:], sc_tok[:]), sem=dve)
            p.wait("vector", dve, v_mx)
            p.op("vector", lambda e: e.tensor_single_scalar(
                oh_tok[:, 0:8], sc_tok[:], mx8[:, 0:1], ALU.is_equal), sem=dve)
            v_oh = p.op("vector", lambda e: e.tensor_single_scalar(
                oh_tok[:, 8:16], sc_tok[:], mx8[:, 1:2], ALU.is_equal), sem=dve)
            p.op("vector", lambda e: e.tensor_sub(gg[:, 2:3], mx8[:, 0:1], mx8[:, 1:2]),
                 sem=dve)
            v_d = p.op("vector", lambda e: e.tensor_sub(gg[:, 3:4], mx8[:, 1:2], mx8[:, 0:1]),
                       sem=dve)
            p.wait("scalar", dve, v_d)
            p.op("scalar", lambda e: e.activation(gg[:, 0:1], gg[:, 2:3], AF.Sigmoid), sem=act)
            v_gs = p.op("scalar", lambda e: e.activation(gg[:, 1:2], gg[:, 3:4], AF.Sigmoid),
                        sem=act)
            p.wait("tensor", dve, v_oh)
            p.wait("tensor", act, v_gs)
            col0 = i * T + s * 128
            p.op("tensor", lambda e: e.transpose(ps_tr[0:8, 0:128], oh_tok[:, 0:8],
                                                 id_f[:]), sem=pe)
            v_tr1 = pe.n
            p.wait("vector", pe, v_tr1)
            v_c1 = p.op("vector", lambda e, col0=col0: e.tensor_copy(
                ohEc[0:8, 2 * col0:2 * col0 + 256].rearrange(
                    "p (t k) -> p t k", k=2)[:, :, 0], ps_tr[0:8, 0:128]), sem=dve)
            p.wait("tensor", dve, v_c1)
            p.op("tensor", lambda e: e.transpose(ps_tr[0:8, 0:128], oh_tok[:, 8:16],
                                                 id_f[:]), sem=pe)
            v_tr2 = pe.n
            p.wait("vector", pe, v_tr2)
            v_c2 = p.op("vector", lambda e, col0=col0: e.tensor_copy(
                ohEc[0:8, 2 * col0:2 * col0 + 256].rearrange(
                    "p (t k) -> p t k", k=2)[:, :, 1], ps_tr[0:8, 0:128]), sem=dve)
            p.wait("tensor", dve, v_c2)
            p.op("tensor", lambda e: e.transpose(ps_tr[0:1, 0:128], gg[:, 0:1], id_f[:]),
                 sem=pe)
            v_trg = pe.n
            p.wait("vector", pe, v_trg)
            v_g1 = p.op("vector", lambda e, col0=col0: e.tensor_copy(
                g1_sb[:, col0:col0 + 128], ps_tr[0:1, 0:128]), sem=dve)
            p.wait("tensor", dve, v_g1)
            p.op("tensor", lambda e: e.transpose(ps_tr[0:1, 0:128], gg[:, 1:2], id_f[:]),
                 sem=pe)
            v_trg2 = pe.n
            p.wait("vector", pe, v_trg2)
            v_last = p.op("vector", lambda e, col0=col0: e.tensor_copy(
                g2_sb[:, col0:col0 + 128], ps_tr[0:1, 0:128]), sem=dve)
            ps_tr_free[0] = v_last
        comp_done[i] = dve.n

    v_all = dve.n
    p.wait("vector", dve, v_all)
    p.op("vector", lambda e: e.tensor_copy(
        gEr[:].rearrange("p (t k) -> p t k", k=2)[:, :, 0], g1_sb[:]), sem=dve)
    p.op("vector", lambda e: e.tensor_copy(
        gEr[:].rearrange("p (t k) -> p t k", k=2)[:, :, 1], g2_sb[:]), sem=dve)
    v_scan = p.op("vector", lambda e: e.tensor_tensor_scan(
        ohC[:], ohEc[:], ohEc[:], 0.0, ALU.add, ALU.bypass), sem=dve)
    p.op("vector", lambda e: e.tensor_copy(cnt_sb[:], ohC[:, 2 * NT - 1:2 * NT]),
         sem=dve)
    v_prod = p.op("vector", lambda e: e.tensor_mul(ohC[:], ohEc[:], ohC[:]), sem=dve)
    p.wait("tensor", dve, v_prod)
    for g in range(2 * NT // 512):
        p.wait("tensor", st, st.n)
        p.op("tensor", lambda e, g=g: e.matmul(
            ps_mu[:, 0:512], ones_f[0:8, :], ohC[:, g * 512:(g + 1) * 512],
            start=True, stop=True), sem=pe)
        v_pm = pe.n
        p.wait("vector", pe, v_pm)
        v_pc = p.op("vector", lambda e, g=g: e.tensor_copy(
            pos_sb[:], ps_mu[:, 0:512]), sem=dve)
        p.wait("sync", dve, v_pc)
        p.dma("sync", o_pos[:, g * 512:(g + 1) * 512], pos_sb[:], st, gate=True)
    v_pos = dve.n
    p.wait("sync", dve, v_pos)
    p.dma("sync", o_g1, g1_sb[:], st, gate=False)
    p.dma("sync", o_g2, g2_sb[:], st, gate=False)
    p.dma("sync", o_cnt, cnt_sb[:], st, gate=True)
    p.dma("sync", o_ohE, ohEc[:], st, gate=True)
    p.dma("sync", o_gE, gEr[:], st, gate=True)
    p.wait("sync", sxn, sxn.n)
    p.finish(st, st.n)
    p.end_phase()
    p.pop()
    return p



CAP1 = 8602


def build_wm1(p=None, rename=None):
    standalone = p is None
    if standalone:
        p = Prog('wm1')
    p.push(rename)
    p.begin_phase('wm_')
    counts_all = p.din("counts_all", (8, 8), F32)
    selcol = p.din("selcol", (8, 1), F32)
    pos = p.din("pos_incl", (1, 2 * NT), F32)
    ohE_in = p.din("ohE_in", (8, 2 * NT), F32)
    gE_in = p.din("gE_in", (1, 2 * NT), F32)
    o_wm = p.dout("wm_fm", (8, NT), F32)

    tril8 = p.const("c_tril8", np.tril(np.ones((8, 8), np.float32), -1).T)

    ca_sb = p.sb("ca_sb", (8, 8), F32)
    sel_sb = p.sb("sel_sb", (8, 1), F32)
    tri_sb = p.sb("tri_sb", (8, 8), F32)
    pre_sb = p.sb("pre_sb", (8, 8), F32)
    offc_sb = p.sb("offc_sb", (8, 1), F32)
    ohE = p.sb("ohE", (8, 2 * NT), F32)
    pos_sb = p.sb("pos_sb", (1, 512), F32)
    offe_sb = p.sb("offe_sb", (1, 512), F32)
    wE_sb = p.sb("wE", (1, 2 * NT), F32)
    wtmp = p.sb("wtmp", (8, 512), F32)
    g_sb = p.sb("g_sb", (1, 2 * NT), F32)
    onesr = p.sb("onesr", (1, 8), F32)
    wm_sb = p.sb("wm_sb", (8, NT), F32)

    ps_a = p.ps("ps_a", (128, 512))
    ps_b = p.ps("ps_b", (128, 512))

    ld = p.sem("ld")
    lp = p.sem("lp")
    pe = p.sem("pe")
    dve = p.stream_sem("vector")
    gp = p.stream_sem("gpsimd")
    st = p.sem("st")

    p.dma("sync", ca_sb[:], counts_all, ld, gate=False)
    p.dma("sync", sel_sb[:], selcol, ld, gate=False)
    p.dma("sync", tri_sb[:], tril8, ld, gate=False)
    p.dma("sync", ohE[:], ohE_in, ld, gate=False)
    p.dma("sync", g_sb[:], gE_in, ld, gate=False)
    v_ld = ld.n
    p.op("gpsimd", lambda e: e.memset(onesr[:], 1.0), sem=gp)
    v_gp = gp.n
    p.wait("tensor", ld, v_ld)
    p.wait("tensor", gp, v_gp)
    p.wait("vector", ld, v_ld)

    p.op("tensor", lambda e: e.matmul(ps_a[0:8, 0:8], tri_sb[:], ca_sb[:],
                                      start=True, stop=True), sem=pe)
    v_pre = pe.n
    p.wait("vector", pe, v_pre)
    v_prc = p.op("vector", lambda e: e.tensor_copy(pre_sb[:], ps_a[0:8, 0:8]), sem=dve)
    p.wait("tensor", dve, v_prc)
    p.op("tensor", lambda e: e.matmul(ps_b[0:8, 0:1], pre_sb[:], sel_sb[:],
                                      start=True, stop=True), sem=pe)
    v_off = pe.n
    p.wait("vector", pe, v_off)
    v_ofc = p.op("vector", lambda e: e.tensor_copy(offc_sb[:], ps_b[0:8, 0:1]), sem=dve)


    pos_cons = [0]
    for g in range(2 * NT // 512):
        p.wait("tensor", dve, dve.n)
        p.op("tensor", lambda e, g=g: e.matmul(
            ps_a[0:1, 0:512], offc_sb[:], ohE[:, g * 512:(g + 1) * 512],
            start=True, stop=True), sem=pe)
        v_oe = pe.n
        p.wait("sync", dve, pos_cons[0])
        v_lp = p.dma("sync", pos_sb[:], pos[:, g * 512:(g + 1) * 512], lp, gate=True)
        p.wait("vector", pe, v_oe)
        p.wait("vector", lp, v_lp)
        v_pa = p.op("vector", lambda e: e.tensor_add(
            offe_sb[:], pos_sb[:], ps_a[0:1, 0:512]), sem=dve)
        pos_cons[0] = v_pa
        p.op("vector", lambda e: e.tensor_single_scalar(
            offe_sb[:], offe_sb[:], float(CAP1), ALU.is_le), sem=dve)
        v_w = p.op("vector", lambda e, g=g: e.tensor_mul(
            wE_sb[:, g * 512:(g + 1) * 512], g_sb[:, g * 512:(g + 1) * 512],
            offe_sb[:]), sem=dve)

    wEv = wE_sb[:].rearrange("p (t k) -> p t k", k=2)
    for g in range(NT // 512):
        sl = slice(g * 512, (g + 1) * 512)
        p.wait("tensor", dve, dve.n)
        p.op("tensor", lambda e, g=g, wEv=wEv: e.matmul(
            ps_a[0:8, 0:512], onesr[:], wEv[:, g * 512:(g + 1) * 512, 0],
            start=True, stop=True), sem=pe)
        v_b1 = pe.n
        p.op("tensor", lambda e, g=g, wEv=wEv: e.matmul(
            ps_b[0:8, 0:512], onesr[:], wEv[:, g * 512:(g + 1) * 512, 1],
            start=True, stop=True), sem=pe)
        v_b2 = pe.n
        oh1v = ohE[0:8, 2 * g * 512:2 * (g + 1) * 512].rearrange(
            "p (t k) -> p t k", k=2)[:, :, 0]
        oh2v = ohE[0:8, 2 * g * 512:2 * (g + 1) * 512].rearrange(
            "p (t k) -> p t k", k=2)[:, :, 1]
        p.wait("vector", pe, v_b1)
        p.op("vector", lambda e, sl=sl, oh1v=oh1v: e.tensor_mul(
            wm_sb[:, sl], oh1v, ps_a[0:8, 0:512]), sem=dve)
        p.wait("vector", pe, v_b2)
        p.op("vector", lambda e, sl=sl, oh2v=oh2v: e.tensor_mul(
            pos_sb[0:8, 0:512] if False else wtmp[:, 0:512], oh2v, ps_b[0:8, 0:512]),
            sem=dve)
        p.op("vector", lambda e, sl=sl: e.tensor_add(
            wm_sb[:, sl], wm_sb[:, sl], wtmp[:, 0:512]), sem=dve)
    v_wm = dve.n
    p.wait("sync", dve, v_wm)
    p.dma("sync", o_wm, wm_sb[:], st, gate=True)
    p.finish(st, st.n)
    p.end_phase()
    p.pop()
    return p



def _attn1_slots(qi):
    return [8, 9] + list(range(qi, qi + 10))


def _attn1_masked(qi):
    out = []
    for s, ck in enumerate(_attn1_slots(qi)):
        if s < 2 or ck <= 9 or ck in (qi, qi + 1, qi + 9):
            out.append(s)
    return out


def build_attn1(p=None, rename=None):
    import os
    ASTAGE = int(os.environ.get("ATT_STAGE", "99"))
    standalone = p is None
    if standalone:
        p = Prog('attn1')
    p.push(rename)
    p.begin_phase('a1_')
    KC = 5248
    NKT = 41
    NKV = 4
    xn1 = p.din("xn1_bf", (D, NT), BF16)
    x1 = p.din("x1_fm", (D, NT), F32)
    xprev = p.din("xprev_fm", (D, 1028), F32)
    Wq = p.din("Wq", (D, D), F32)
    Wk = p.din("Wk", (D, 512), F32)
    Wv = p.din("Wv", (D, 512), F32)
    Wo = p.din("Wo", (D, D), F32)
    hd3T = p.din("hd3T", (3, 8), F32)
    hscc = p.din("hscc", (1, 8), F32)
    cq2 = p.din("cosq2", (128, NT), F32)
    sq2 = p.din("sinq2", (128, NT), F32)
    ck2 = p.din("cosk2", (128, KC), F32)
    sk2 = p.din("sink2", (128, KC), F32)
    addm = p.din("addm", (32 * 12, 128 * 128), F32)
    q_scr = p.dint("q_scr", (D, NT), BF16)
    o_scr = p.dint("o_scr", (D, NT), BF16)
    o_x2a = p.dout("x2a_fm", (D, NT), F32)

    wsh = p.sb("wsh", (128, 8 * D), BF16)
    wk_sb = wsh
    wv_sb = None
    hd3_sb = p.sb("hd3_sb", (3, 8), BF16)
    hsc_sb = p.sb("hsc_sb", (1, 8), F32)
    kh = [p.sb(f"kh{j}", (128, KC), BF16) for j in range(NKV)]
    vt = [p.sb(f"vt{j}", (128, NKT * 132), BF16) for j in range(NKV)]
    xpn = p.sb("xpn", (128, NCHUNK * 1152), BF16)
    xlb = p.sb("xlb", (128, NCHUNK * 512), BF16)
    qh = p.sb("qh", (128, NT), BF16)
    ohd = p.sb("ohd", (128, NT), BF16)
    rc = p.sb("rc", (128, 512), BF16)
    rs = p.sb("rs", (128, 512), BF16)
    rt1 = p.sb("rt1", (128, 512), BF16)
    rt2 = p.sb("rt2", (128, 512), BF16)
    adm = [p.sb("adm0", (128, 12 * 128), F32)] * 2
    kpr = p.sb("kpr", (1, 1536), BF16)
    qpr = p.sb("qpr", (1, 128), BF16)
    mcol = p.sb("mcol", (128, 8), F32)
    ptok = p.sb("ptok", (128, 1536), BF16)
    stat = p.sb("stat", (1, 4 * 512), F32)
    ones_f = p.sb("ones_f", (128, 1), F32)
    onesr_f = p.sb("onesr_f", (1, 128), F32)
    cvals = p.sb("cvals", (128, 1), F32)
    id_bf = p.sb("id_bf", (128, 128), BF16)
    xa = p.sb("xa", (128, NCHUNK * 512), F32)
    xt2 = p.sb("xt2", (128, NCHUNK * 512), F32)
    ob2 = p.sb("ob2", (128, NCHUNK * 512), BF16)

    psS = [p.ps(f"psS{j}", (128, 512)) for j in range(4)]
    psB16 = [p.ps(f"psB16{j}", (128, 1024), BF16) for j in range(2)]
    psAV = p.ps("psAV", (128, 512))

    idb = p.const("c_idb", np.eye(128, dtype=BF16NP))

    ld = p.sem("ld")
    lw = p.sem("lw")
    lq = p.sem("lq")
    lm = [p.sem("lm0"), p.sem("lm1")]
    lx = p.sem("lx")
    lrt = p.sem("lrt")
    pe = p.sem("pe")
    dve = p.stream_sem("vector")
    act = p.stream_sem("scalar")
    gp = p.stream_sem("gpsimd")
    st = p.sem("st")
    so = p.sem("so")

    for dc in range(NCHUNK):
        p.dma("gpsimd", wsh[:, dc * 512:(dc + 1) * 512], Wk[dc * 128:(dc + 1) * 128, :],
              lw, gate=False)
        p.dma("gpsimd", wsh[:, 4096 + dc * 512:4096 + (dc + 1) * 512],
              Wv[dc * 128:(dc + 1) * 128, :], lw, gate=False)
    p.dma("gpsimd", hd3_sb[:], hd3T, lw, gate=False)
    v_lw = lw.n
    p.dma("sync", hsc_sb[:], hscc, ld, gate=False)
    p.dma("sync", id_bf[:], idb, ld, gate=False)
    v_ld = ld.n
    p.op("gpsimd", lambda e: e.memset(ones_f[:], 1.0), sem=gp)
    p.op("gpsimd", lambda e: e.memset(onesr_f[:], 1.0), sem=gp)
    p.op("gpsimd", lambda e: e.memset(cvals[:, 0:1], 1e-5), sem=gp)
    p.op("gpsimd", lambda e: e.memset(
        xpn[:].rearrange("p (c t) -> p c t", c=NCHUNK)[:, :, 1028:1152], 0.0), sem=gp)
    for j in range(NKV):
        p.op("gpsimd", lambda e, j=j: e.memset(
            vt[j][:].rearrange("p (k c) -> p k c", c=132)[:, :, 128:132], 0.0), sem=gp)
        p.op("gpsimd", lambda e, j=j: e.memset(
            vt[j][:].rearrange("p (k c) -> p k c", c=132)[:, :, 128:129], 1.0), sem=gp)
    v_gp = gp.n
    for eng in ("tensor", "vector", "scalar"):
        p.wait(eng, lw, v_lw)
        p.wait(eng, ld, v_ld)
        p.wait(eng, gp, v_gp)

    MU, RSTD, MR, TMP = range(4)
    LNG = [(0, 512), (512, 1024), (1024, 1028)]
    vt_cons = [0, 0]
    kp_cons = [0]
    exp_cons = [0]
    adm_cons = [0, 0]
    pt_cons = [0, 0]
    pt_eng = ["v", "v"]

    def srow(k, w):
        return stat[:, k * 512:k * 512 + w]

    for (g0, g1) in LNG:
        gw = g1 - g0
        p.wait("sync", dve, dve.n)
        v_lp = p.dma("sync", xa[:, 0:NCHUNK * gw].rearrange("p (c t) -> p c t", c=NCHUNK),
                     xprev[:, g0:g1].rearrange("(c p) t -> p c t", p=128), lx, gate=True)
        p.wait("tensor", lx, v_lp)
        p.wait("scalar", lx, v_lp)
        p.wait("scalar", dve, dve.n)
        p.wait("vector", lx, v_lp)
        sq_cons2 = [0, 0]
        for c in range(NCHUNK):
            xc = xa[:, c * gw: (c + 1) * gw]
            sl2 = (c % 2) * 512
            if c >= 2:
                p.wait("scalar", pe, sq_cons2[c % 2])
            p.op("scalar", lambda e, xc=xc, gw=gw, sl2=sl2: e.activation(
                xt2[:, sl2:sl2 + gw], xc, AF.Square), sem=act)
            v_sq = act.n
            p.op("tensor", lambda e, xc=xc, c=c, gw=gw: e.matmul(
                psS[0][0:1, 0:gw], ones_f[:], xc, start=(c == 0), stop=(c == NCHUNK - 1)))
            p.wait("tensor", act, v_sq)
            p.op("tensor", lambda e, c=c, gw=gw, sl2=sl2: e.matmul(
                psS[1][0:1, 0:gw], ones_f[:], xt2[:, sl2:sl2 + gw],
                start=(c == 0), stop=(c == NCHUNK - 1)), sem=pe)
            sq_cons2[c % 2] = pe.n
        v_stats = pe.n
        p.wait("vector", pe, v_stats)
        p.op("vector", lambda e, gw=gw: e.tensor_scalar_mul(
            srow(MU, gw), psS[0][0:1, 0:gw], 1.0 / D), sem=dve)
        p.op("vector", lambda e, gw=gw: e.tensor_scalar_mul(
            srow(TMP, gw), psS[1][0:1, 0:gw], 1.0 / D), sem=dve)
        p.op("vector", lambda e, gw=gw: e.tensor_mul(
            srow(RSTD, gw), srow(MU, gw), srow(MU, gw)), sem=dve)
        v_var = p.op("vector", lambda e, gw=gw: e.tensor_sub(
            srow(TMP, gw), srow(TMP, gw), srow(RSTD, gw)), sem=dve)
        p.wait("scalar", dve, v_var)
        p.op("scalar", lambda e, gw=gw: e.activation(
            srow(RSTD, gw), srow(TMP, gw), AF.Ln, bias=cvals[0:1, 0:1]), sem=act)
        v_rstd = p.op("scalar", lambda e, gw=gw: e.activation(
            srow(RSTD, gw), srow(RSTD, gw), AF.Exp, scale=-0.5), sem=act)
        p.wait("vector", act, v_rstd)
        v_mr = p.op("vector", lambda e, gw=gw: e.tensor_mul(
            srow(MR, gw), srow(MU, gw), srow(RSTD, gw)), sem=dve)
        p.wait("tensor", dve, v_mr)
        p.op("tensor", lambda e, gw=gw: e.matmul(
            psS[2][:, 0:gw], onesr_f[:], srow(RSTD, gw), start=True, stop=True))
        p.op("tensor", lambda e, gw=gw: e.matmul(
            psS[3][:, 0:gw], onesr_f[:], srow(MR, gw), start=True, stop=True), sem=pe)
        v_bc = pe.n
        p.wait("vector", pe, v_bc)
        for c in range(NCHUNK):
            xc = xa[:, c * gw: (c + 1) * gw]
            p.op("vector", lambda e, xc=xc, gw=gw: e.tensor_mul(
                xt2[:, 0:gw], xc, psS[2][:, 0:gw]), sem=dve)
            p.op("vector", lambda e, c=c, g0=g0, g1=g1, gw=gw: e.tensor_sub(
                xpn[:, c * 1152 + g0: c * 1152 + g1], xt2[:, 0:gw], psS[3][:, 0:gw]),
                sem=dve)
    v_xpn = dve.n
    p.wait("tensor", dve, v_xpn)

    groups = [("p", 0, 512, 0), ("p", 512, 512, 512), ("p", 1024, 128, 1024)]
    for gl in range(8):
        groups.append(("l", gl * 512, 512, 1152 + gl * 512))
    for (src, soff, gw, ooff) in groups:
        p.wait("gpsimd", dve, dve.n)
        vrt = p.dma("gpsimd", rc[:, 0:gw], ck2[:, ooff:ooff + gw], lrt, gate=False)
        vrt = p.dma("gpsimd", rs[:, 0:gw], sk2[:, ooff:ooff + gw], lrt, gate=False)
        if src == "l":
            p.wait("sync", pe, pe.n)
            vxl = p.dma("sync", xlb[:].rearrange("p (c t) -> p c t", c=NCHUNK),
                        xn1[:, soff:soff + gw].rearrange("(c p) t -> p c t", p=128),
                        lx, gate=True)
            p.wait("tensor", lx, vxl)
        p.wait("vector", lrt, lrt.n)
        for m in range(NKV):
            for dc in range(NCHUNK):
                rhs = (xpn[:, dc * 1152 + soff: dc * 1152 + soff + gw] if src == "p"
                       else xlb[:, dc * 512:dc * 512 + gw])
                p.op("tensor", lambda e, m=m, dc=dc, rhs=rhs, gw=gw: e.matmul(
                    psS[0][:, 0:gw], wsh[:, dc * 512 + m * 128:dc * 512 + (m + 1) * 128],
                    rhs, start=(dc == 0), stop=(dc == NCHUNK - 1)))
                p.op("tensor", lambda e, m=m, dc=dc, rhs=rhs, gw=gw: e.matmul(
                    psS[1][:, 0:gw],
                    wsh[:, 4096 + dc * 512 + m * 128:4096 + dc * 512 + (m + 1) * 128],
                    rhs, start=(dc == 0), stop=(dc == NCHUNK - 1)), sem=pe)
            v_mm = pe.n
            p.wait("vector", pe, v_mm)
            p.op("vector", lambda e, gw=gw: e.tensor_copy(rt1[:, 0:gw], psS[0][:, 0:gw]),
                 sem=dve)
            p.op("vector", lambda e, gw=gw: e.tensor_scalar_mul(
                rt2[0:64, 0:gw], rt1[64:128, 0:gw], -1.0), sem=dve)
            p.op("vector", lambda e, gw=gw: e.tensor_copy(
                rt2[64:128, 0:gw], rt1[0:64, 0:gw]), sem=dve)
            p.op("vector", lambda e, gw=gw: e.tensor_mul(
                rt1[:, 0:gw], rt1[:, 0:gw], rc[:, 0:gw]), sem=dve)
            p.op("vector", lambda e, gw=gw: e.tensor_mul(
                rt2[:, 0:gw], rt2[:, 0:gw], rs[:, 0:gw]), sem=dve)
            v_k = p.op("vector", lambda e, m=m, ooff=ooff, gw=gw: e.tensor_add(
                kh[m][:, ooff:ooff + gw], rt1[:, 0:gw], rt2[:, 0:gw]), sem=dve)
            p.wait("scalar", pe, v_mm)
            p.op("scalar", lambda e, gw=gw: e.activation(
                ob2[:, 0:gw], psS[1][:, 0:gw], AF.Copy), sem=act)
            v_vev = act.n
            p.wait("tensor", act, v_vev)
            p.wait("tensor", dve, v_k)
            for sc in range(gw // 128):
                ck = (ooff + sc * 128) // 128
                pb = psB16[sc % 2]
                if sc >= 2:
                    p.wait("tensor", dve, vt_cons[sc % 2])
                p.op("tensor", lambda e, sc=sc: e.transpose(
                    psB16[sc % 2][:, 0:128],
                    ob2[:, sc * 128:(sc + 1) * 128], id_bf[:]), sem=pe)
                v_tr = pe.n
                p.wait("vector", pe, v_tr)
                vt_cons[sc % 2] = p.op("vector", lambda e, m=m, ck=ck, sc=sc: e.tensor_copy(
                    vt[m][:, ck * 132:ck * 132 + 128], psB16[sc % 2][:, 0:128]), sem=dve)
            p.wait("tensor", dve, dve.n)
    v_kv = dve.n

    if ASTAGE < 1:
        p.wait("sync", dve, dve.n)
        p.finish(st, max(st.n, 16))
        return p
    p.wait("gpsimd", pe, pe.n)
    for dc in range(NCHUNK):
        p.dma("gpsimd", wsh[:, dc * D:(dc + 1) * D], Wq[dc * 128:(dc + 1) * 128, :],
              lw, gate=False)
    v_lw2 = lw.n
    p.wait("tensor", lw, v_lw2)
    for gq in range(8):
        ooff = gq * 512
        p.wait("gpsimd", dve, dve.n)
        p.dma("gpsimd", rc[:, 0:512], cq2[:, ooff:ooff + 512], lrt, gate=False)
        p.dma("gpsimd", rs[:, 0:512], sq2[:, ooff:ooff + 512], lrt, gate=False)
        p.wait("sync", pe, pe.n)
        vxl = p.dma("sync", xlb[:].rearrange("p (c t) -> p c t", c=NCHUNK),
                    xn1[:, ooff:ooff + 512].rearrange("(c p) t -> p c t", p=128),
                    lx, gate=True)
        p.wait("tensor", lx, vxl)
        p.wait("vector", lrt, lrt.n)
        for m in range(8):
            p.wait("tensor", dve, dve.n)
            for dc in range(NCHUNK):
                p.op("tensor", lambda e, m=m, dc=dc: e.matmul(
                    psS[0][:, 0:512], wsh[:, dc * D + m * 128:dc * D + (m + 1) * 128],
                    xlb[:, dc * 512:(dc + 1) * 512],
                    start=(dc == 0), stop=(dc == NCHUNK - 1)), sem=pe)
            v_mm = pe.n
            p.wait("vector", pe, v_mm)
            p.op("vector", lambda e: e.tensor_copy(rt1[:, 0:512], psS[0][:, 0:512]), sem=dve)
            p.op("vector", lambda e: e.tensor_scalar_mul(
                rt2[0:64, 0:512], rt1[64:128, 0:512], -1.0), sem=dve)
            p.op("vector", lambda e: e.tensor_copy(
                rt2[64:128, 0:512], rt1[0:64, 0:512]), sem=dve)
            p.op("vector", lambda e: e.tensor_mul(rt1[:, 0:512], rt1[:, 0:512],
                                                  rc[:, 0:512]), sem=dve)
            p.op("vector", lambda e: e.tensor_mul(rt2[:, 0:512], rt2[:, 0:512],
                                                  rs[:, 0:512]), sem=dve)
            v_q = p.op("vector", lambda e: e.tensor_add(rt1[:, 0:512], rt1[:, 0:512],
                                                        rt2[:, 0:512]), sem=dve)
            p.wait("sync", dve, v_q)
            p.dma("sync", q_scr[m * 128:(m + 1) * 128, ooff:ooff + 512], rt1[:, 0:512],
                  st, gate=True)
            p.wait("vector", st, st.n)
    p.wait("tensor", dve, dve.n)

    p.wait("gpsimd", pe, pe.n)
    for dc in range(NCHUNK):
        p.dma("gpsimd", wsh[:, dc * D:(dc + 1) * D], Wo[dc * 128:(dc + 1) * 128, :],
              lw, gate=False)
    v_lw3 = lw.n

    SC = 1.0 / math.sqrt(128.0)
    for h in range(8 if ASTAGE >= 2 else 0):
        kvh = h // 2
        vq = p.dma("sync", qh[:], q_scr[h * 128:(h + 1) * 128, :], lq, gate=True)
        p.wait("tensor", lq, vq)
        for qi in range(32 if ASTAGE >= 3 else 1):
            mb = 0
            p.wait("sync", dve, adm_cons[0])
            vm = 0
            for ms in _attn1_masked(qi):
                vm = p.dma("sync", adm[0][:, ms * 128:(ms + 1) * 128],
                           addm[qi * 12 + ms:qi * 12 + ms + 1, :].rearrange(
                               "one (q k) -> q (one k)", q=128), lm[0], gate=True)
            slots = _attn1_slots(qi)
            regs = [(1024, 256, 0), (qi * 128, 512, 256), (qi * 128 + 512, 512, 768),
                    (qi * 128 + 1024, 256, 1280)]
            for (koff, kw, coff) in regs:
                p.wait("tensor", dve, kp_cons[0])
                p.op("tensor", lambda e, koff=koff, kw=kw, kvh=kvh, h=h: e.matmul(
                    psAV[0:1, 0:kw], hd3_sb[:, h:h + 1], kh[kvh][0:3, koff:koff + kw],
                    start=True, stop=True), sem=pe)
                v_kp = pe.n
                p.wait("vector", pe, v_kp)
                kp_cons[0] = p.op("vector", lambda e, coff=coff, kw=kw, h=h:
                                  e.tensor_scalar(
                    kpr[:, coff:coff + kw], psAV[0:1, 0:kw],
                    hsc_sb[:, h:h + 1], 1.0 / SC,
                    ALU.mult, ALU.mult), sem=dve)
            p.wait("tensor", dve, kp_cons[0])
            p.op("tensor", lambda e, qi=qi, h=h: e.matmul(
                psAV[0:1, 0:128], hd3_sb[:, h:h + 1], qh[0:3, qi * 128:(qi + 1) * 128],
                start=True, stop=True), sem=pe)
            v_qp = pe.n
            p.wait("vector", pe, v_qp)
            v_qpr = p.op("vector", lambda e: e.tensor_copy(qpr[:], psAV[0:1, 0:128]),
                         sem=dve)
            p.wait("tensor", dve, v_qpr)

            q_t = qh[:, qi * 128:(qi + 1) * 128]
            p.wait("tensor", act, exp_cons[0])
            for ri, (koff, kw, coff) in enumerate(regs):
                p.op("tensor", lambda e, ri=ri, koff=koff, kw=kw, kvh=kvh, q_t=q_t:
                     e.matmul(psS[ri][:, 0:kw], q_t, kh[kvh][:, koff:koff + kw],
                              start=True, stop=False))
                p.op("tensor", lambda e, ri=ri, kw=kw, coff=coff: e.matmul(
                    psS[ri][:, 0:kw], qpr[:], kpr[:, coff:coff + kw],
                    start=False, stop=True), sem=pe)
            v_s = pe.n
            p.wait("vector", pe, v_s)
            p.wait("vector", lm[0], vm)
            for s in _attn1_masked(qi):
                coff = 256 + (s - 2) * 128 if s >= 2 else s * 128
                ri, off = (0, coff) if coff < 256 else \
                          (1, coff - 256) if coff < 768 else \
                          (2, coff - 768) if coff < 1280 else (3, coff - 1280)
                p.op("vector", lambda e, ri=ri, off=off, s=s: e.tensor_tensor(
                    psS[ri][:, off:off + 128], psS[ri][:, off:off + 128],
                    adm[0][:, s * 128:(s + 1) * 128], ALU.add), sem=dve)
            for ri, (koff, kw, coff) in enumerate(regs):
                p.op("vector", lambda e, ri=ri, kw=kw: e.reduce_max(
                    mcol[:, ri:ri + 1], psS[ri][:, 0:kw], axis=AX.X), sem=dve)
            p.op("vector", lambda e: e.tensor_max(mcol[:, 4:5], mcol[:, 0:1],
                                                  mcol[:, 1:2]), sem=dve)
            p.op("vector", lambda e: e.tensor_max(mcol[:, 5:6], mcol[:, 2:3],
                                                  mcol[:, 3:4]), sem=dve)
            p.op("vector", lambda e: e.tensor_max(mcol[:, 6:7], mcol[:, 4:5],
                                                  mcol[:, 5:6]), sem=dve)
            v_m = p.op("vector", lambda e: e.tensor_scalar_mul(
                mcol[:, 7:8], mcol[:, 6:7], -1.0), sem=dve)
            adm_cons[mb] = v_m
            p.wait("scalar", dve, v_m)
            for ri, (koff, kw, coff) in enumerate(regs):
                p.op("scalar", lambda e, ri=ri, kw=kw, coff=coff: e.activation(
                    ptok[:, coff:coff + kw], psS[ri][:, 0:kw], AF.Exp,
                    bias=mcol[:, 7:8]), sem=act)
            v_p = act.n
            exp_cons[0] = v_p
            p.wait("tensor", act, v_p)
            for sc2 in range(12):
                pb2 = sc2 % 2
                if sc2 >= 2:
                    p.wait("tensor", dve if pt_eng[pb2] == "v" else act, pt_cons[pb2])
                p.op("tensor", lambda e, sc2=sc2, pb2=pb2: e.transpose(
                    psB16[pb2][:, 0:128], ptok[:, sc2 * 128:(sc2 + 1) * 128], id_bf[:]),
                    sem=pe)
                v_tr = pe.n
                if sc2 % 2 == 0:
                    p.wait("vector", pe, v_tr)
                    pt_cons[pb2] = p.op("vector", lambda e, sc2=sc2, pb2=pb2: e.tensor_copy(
                        ptok[:, sc2 * 128:(sc2 + 1) * 128], psB16[pb2][:, 0:128]), sem=dve)
                    pt_eng[pb2] = "v"
                else:
                    p.wait("scalar", pe, v_tr)
                    pt_cons[pb2] = p.op("scalar", lambda e, sc2=sc2, pb2=pb2: e.copy(
                        ptok[:, sc2 * 128:(sc2 + 1) * 128], psB16[pb2][:, 0:128]), sem=act)
                    pt_eng[pb2] = "s"
            p.wait("tensor", dve, dve.n)
            p.wait("tensor", act, act.n)
            p.wait("tensor", dve, dve.n)
            slots_ck = [8, 9] + list(range(qi, qi + 10))
            for i2, ck in enumerate(slots_ck):
                p.op("tensor", lambda e, i2=i2, ck=ck, kvh=kvh: e.matmul(
                    psAV[:, 0:132], ptok[:, i2 * 128:(i2 + 1) * 128],
                    vt[kvh][:, ck * 132:(ck + 1) * 132],
                    start=(i2 == 0), stop=(i2 == 11)), sem=pe)
            v_av = pe.n
            p.wait("vector", pe, v_av)
            p.op("vector", lambda e: e.reciprocal(mcol[:, 0:1], psAV[:, 128:129]), sem=dve)
            v_o = p.op("vector", lambda e: e.tensor_scalar_mul(
                ptok[:, 0:128], psAV[:, 0:128], mcol[:, 0:1]), sem=dve)
            p.wait("tensor", dve, v_o)
            p.op("tensor", lambda e: e.transpose(psB16[0][:, 0:128], ptok[:, 0:128],
                                                 id_bf[:]), sem=pe)
            v_tro = pe.n
            p.wait("vector", pe, v_tro)
            v_oc = p.op("vector", lambda e, qi=qi: e.tensor_copy(
                ohd[:, qi * 128:(qi + 1) * 128], psB16[0][:, 0:128]), sem=dve)
            pt_cons[0] = v_oc
            pt_eng[0] = "v"
            kp_cons[0] = v_oc
        p.wait("sync", dve, dve.n)
        p.dma("sync", o_scr[h * 128:(h + 1) * 128, :], ohd[:], so, gate=True)
        p.wait("vector", so, so.n)

    p.wait("tensor", lw, v_lw3)
    p.wait("sync", so, so.n)
    for i in range(NCHUNK):
        T2 = 512
        p.wait("sync", dve, dve.n)
        vxo = p.dma("sync", ob2[:].rearrange("p (c t) -> p c t", c=NCHUNK),
                    o_scr[:, i * T2:(i + 1) * T2].rearrange("(c p) t -> p c t", p=128),
                    lq, gate=True)
        vx1 = p.dma("sync", xa[:].rearrange("p (c t) -> p c t", c=NCHUNK),
                    x1[:, i * T2:(i + 1) * T2].rearrange("(c p) t -> p c t", p=128),
                    lx, gate=True)
        p.wait("tensor", lq, vxo)
        psum_free2 = [0, 0]
        for mo in range(NCHUNK):
            nb = mo % 2
            if psum_free2[nb]:
                p.wait("tensor", dve, psum_free2[nb])
            for dc in range(NCHUNK):
                p.op("tensor", lambda e, mo=mo, dc=dc, nb=nb: e.matmul(
                    psS[nb][:, 0:T2], wsh[:, (dc * 8 + mo) * 128:(dc * 8 + mo + 1) * 128],
                    ob2[:, dc * T2:(dc + 1) * T2],
                    start=(dc == 0), stop=(dc == NCHUNK - 1)), sem=pe)
            v_mm = pe.n
            p.wait("vector", pe, v_mm)
            p.wait("vector", lx, vx1)
            if i > 0 or mo > 1:
                p.wait("vector", st, st.n)
            v_e = p.op("vector", lambda e, mo=mo, nb=nb: e.tensor_add(
                xt2[:, mo * T2:(mo + 1) * T2], xa[:, mo * T2:(mo + 1) * T2],
                psS[nb][:, 0:T2]), sem=dve)
            psum_free2[nb] = v_e
        v_done = dve.n
        p.wait("sync", dve, v_done)
        p.dma("sync", o_x2a[:, i * T2:(i + 1) * T2].rearrange("(c p) t -> p c t", p=128),
              xt2[:].rearrange("p (c t) -> p c t", c=NCHUNK), st, gate=True)

    p.finish(st, st.n)
    p.end_phase()
    p.pop()
    return p



def _attn1_host_consts(core):
    h = core % 2
    p0 = h * NT
    kpos = np.full(5248, -1e9)
    if h == 1:
        kpos[0:1024] = np.arange(3072, 4096)
        kpos[1024:1028] = np.arange(4)
    kpos[1152:5248] = p0 + np.arange(NT)
    qpos = p0 + np.arange(NT).astype(np.float64)
    inv = 1.0 / (10000.0 ** (np.arange(0, 128, 2, dtype=np.float32) / 128))

    def tables(pos):
        pc = pos.astype(np.float32)[None, :] * inv[:, None]
        return np.cos(pc).astype(np.float32), np.sin(pc).astype(np.float32)

    SCALE = np.float32(1.0 / np.sqrt(128.0))
    cq, sq = tables(qpos)
    kposc = np.where(kpos < -1e8, 0, kpos)
    ck_, sk_ = tables(kposc)
    out = {
        "cosq2": np.ascontiguousarray(np.concatenate([cq, cq], 0) * SCALE),
        "sinq2": np.ascontiguousarray(np.concatenate([sq, sq], 0) * SCALE),
        "cosk2": np.ascontiguousarray(np.concatenate([ck_, ck_], 0)),
        "sink2": np.ascontiguousarray(np.concatenate([sk_, sk_], 0)),
    }
    addm = np.zeros((32 * 12, 128 * 128), np.float32)
    for qi in range(32):
        for s, ckk in enumerate(_attn1_slots(qi)):
            pk = kpos[ckk * 128:(ckk + 1) * 128]
            pq = qpos[qi * 128:(qi + 1) * 128]
            causal = pk[None, :] <= pq[:, None]
            sink = (pk[None, :] >= 0) & (pk[None, :] < 4)
            window = pk[None, :] >= pq[:, None] - 1023
            valid = pk[None, :] > -1e8
            if s < 2:
                allowed = causal & sink & valid
            else:
                allowed = causal & window & (~sink) & valid
            addm[qi * 12 + s] = np.where(allowed, 0.0, -1e30).reshape(-1)
            if s not in _attn1_masked(qi) and not allowed.all():
                raise AssertionError(f"mask coverage bug qi={qi} s={s}")
    out["addm"] = addm
    return out



def _col(a):
    return np.ascontiguousarray(np.asarray(a, np.float32).reshape(-1, 1))



def _attn1_host_consts(core):
    h = core % 2
    p0 = h * NT
    kpos = np.full(5248, -1e9)
    if h == 1:
        kpos[0:1024] = np.arange(3072, 4096)
        kpos[1024:1028] = np.arange(4)
    kpos[1152:5248] = p0 + np.arange(NT)
    qpos = p0 + np.arange(NT).astype(np.float64)
    inv = 1.0 / (10000.0 ** (np.arange(0, 128, 2, dtype=np.float32) / 128))

    def tables(pos):
        pc = pos.astype(np.float32)[None, :] * inv[:, None]
        return np.cos(pc).astype(np.float32), np.sin(pc).astype(np.float32)

    SCALE = np.float32(1.0 / np.sqrt(128.0))
    cq, sq = tables(qpos)
    kposc = np.where(kpos < -1e8, 0, kpos)
    ck_, sk_ = tables(kposc)
    out = {
        "cosq2": np.ascontiguousarray(np.concatenate([cq, cq], 0) * SCALE),
        "sinq2": np.ascontiguousarray(np.concatenate([sq, sq], 0) * SCALE),
        "cosk2": np.ascontiguousarray(np.concatenate([ck_, ck_], 0)),
        "sink2": np.ascontiguousarray(np.concatenate([sk_, sk_], 0)),
    }
    addm = np.zeros((32 * 12, 128 * 128), np.float32)
    for qi in range(32):
        for s, ckk in enumerate(_attn1_slots(qi)):
            pk = kpos[ckk * 128:(ckk + 1) * 128]
            pq = qpos[qi * 128:(qi + 1) * 128]
            causal = pk[None, :] <= pq[:, None]
            sink = (pk[None, :] >= 0) & (pk[None, :] < 4)
            window = pk[None, :] >= pq[:, None] - 1023
            valid = pk[None, :] > -1e8
            if s < 2:
                allowed = causal & sink & valid
            else:
                allowed = causal & window & (~sink) & valid
            addm[qi * 12 + s] = np.where(allowed, 0.0, -1e30).reshape(-1)
            if s not in _attn1_masked(qi) and not allowed.all():
                raise AssertionError(f"mask coverage bug qi={qi} s={s}")
    out["addm"] = addm
    return out



def _col(a):
    return np.ascontiguousarray(np.asarray(a, np.float32).reshape(-1, 1))


def kernel(**inputs):
    import functools

    def f(k):
        return np.ascontiguousarray(np.asarray(inputs[k], np.float32))

    x = f("L0_attn_x")
    x_fm = [np.ascontiguousarray(x[c // 2, (c % 2) * NT:(c % 2 + 1) * NT, :].T)
            for c in range(NCORE)]

    cenT = np.ascontiguousarray(f("L0_ffn_centroids").T)
    ln1c = _col(inputs["L0_attn_ln1_scale"])
    r0 = run_prog(build_ph0, [{"x_fm": x_fm[c], "cenT": cenT, "ln1c": ln1c}
                              for c in range(NCORE)])

    qkv_common = {
        "wq": f("L0_attn_wq"), "wk": f("L0_attn_wk"), "wv": f("L0_attn_wv"),
        "wbeta": f("L0_attn_wbeta"), "bq_c": _col(inputs["L0_attn_bq"]),
        "bk_c": _col(inputs["L0_attn_bk"]), "bv_c": _col(inputs["L0_attn_bv"]),
        "bbeta_c": _col(inputs["L0_attn_bbeta"]),
    }
    r1 = run_prog(build_qkv, [{"xh1_bf": r0[c]["xh1_bf"], **qkv_common}
                              for c in range(NCORE)])

    def scat(key, sc):
        b, j = sc // 2, sc % 2
        return np.ascontiguousarray(np.concatenate(
            [r1[2 * b][key][512 * j:512 * (j + 1), :],
             r1[2 * b + 1][key][512 * j:512 * (j + 1), :]], axis=1))

    qs = [scat("q_f", sc) for sc in range(NCORE)]
    ks = [scat("kn_f", sc) for sc in range(NCORE)]
    vs = [scat("v_f", sc) for sc in range(NCORE)]
    bt = [np.ascontiguousarray(np.concatenate(
        [r1[2 * (sc // 2)]["beta_tok"][:, 8 * (sc % 2):8 * (sc % 2) + 8],
         r1[2 * (sc // 2) + 1]["beta_tok"][:, 8 * (sc % 2):8 * (sc % 2) + 8]], axis=0))
        for sc in range(NCORE)]
    z = [np.zeros((512, 64), np.float32)] * NCORE
    o_parts = []
    for ci in range(0, 64, 16):
        bld = functools.partial(build_scan, f"scan{ci}", ci, ci + 16)
        bld.__name__ = f"scan{ci}"
        rs = run_prog(bld, [{"qs": qs[sc], "ks": ks[sc], "vs": vs[sc],
                             "bt": bt[sc], "z_in": z[sc]} for sc in range(NCORE)])
        z = [rs[sc]["z_fin"] for sc in range(NCORE)]
        o_parts.append([rs[sc]["o_fm"] for sc in range(NCORE)])
    o_scan = [sum(np.asarray(o_parts[k][sc], np.float32) for k in range(4))
              for sc in range(NCORE)]
    o_fm = [np.ascontiguousarray(np.concatenate(
        [o_scan[2 * (c // 2)][:, (c % 2) * NT:(c % 2 + 1) * NT],
         o_scan[2 * (c // 2) + 1][:, (c % 2) * NT:(c % 2 + 1) * NT]],
        axis=0).astype(BF16NP)) for c in range(NCORE)]

    ao_common = {
        "wo": f("L0_attn_wo"), "w1": f("L0_attn_w1"), "w2": f("L0_attn_w2"),
        "bo_c": _col(inputs["L0_attn_bo"]), "ln2_c": _col(inputs["L0_attn_ln2_scale"]),
        "b1_c": _col(inputs["L0_attn_b1"]), "b2_c": _col(inputs["L0_attn_b2"]),
    }
    r2 = run_prog(build_attnout, [{"o_fm": o_fm[c], "xn_fm": r0[c]["xn_fm"],
                                   **ao_common} for c in range(NCORE)])

    r3 = run_prog(build_x1init, [{"x_fm": x_fm[c], "acc1_fm": r2[c]["acc1_fm"],
                                  "xn_fm": r0[c]["xn_fm"]} for c in range(NCORE)])
    moe0_common = {
        "W1T": np.ascontiguousarray(f("L0_ffn_W1").transpose(0, 2, 1)),
        "W2T": np.ascontiguousarray(f("L0_ffn_W2").transpose(0, 2, 1)),
        "b1e": f("L0_ffn_b1"), "b2e": f("L0_ffn_b2"),
        "gammae": f("L0_ffn_gamma"), "betae": f("L0_ffn_beta"),
    }
    bld = functools.partial(build_moe, "moe0a", 0, 4, 0)
    bld.__name__ = "moe0a"
    r4 = run_prog(bld, [{"xhat_bf": r0[c]["xhat_bf"], "acc_in": r3[c]["x1acc"],
                         "wm_fm": r0[c]["w0_fm"], **moe0_common} for c in range(NCORE)])
    bld = functools.partial(build_moe, "moe0b", 4, 8, 0)
    bld.__name__ = "moe0b"
    r5 = run_prog(bld, [{"xhat_bf": r0[c]["xhat_bf"], "acc_in": r4[c]["acc_out"],
                         "wm_fm": r0[c]["w0_fm"], **moe0_common} for c in range(NCORE)])
    x1_fm = [r5[c]["acc_out"] for c in range(NCORE)]

    r6 = run_prog(build_ph1, [{"x1_fm": x1_fm[c], "Wg": f("L1_ffn_Wg")}
                              for c in range(NCORE)])
    counts_all = np.ascontiguousarray(
        np.stack([r6[c]["counts"][:, 0] for c in range(NCORE)], axis=0))
    wm_maps = []
    for c in range(NCORE):
        ohE = np.empty((8, 2 * NT), np.float32)
        ohE[:, 0::2] = r6[c]["oh1_fm"]
        ohE[:, 1::2] = r6[c]["oh2_fm"]
        gE = np.empty((1, 2 * NT), np.float32)
        gE[:, 0::2] = r6[c]["g1_row"]
        gE[:, 1::2] = r6[c]["g2_row"]
        sel = np.zeros((8, 1), np.float32)
        sel[c, 0] = 1.0
        wm_maps.append({"counts_all": counts_all, "selcol": sel,
                        "pos_incl": r6[c]["pos_incl"],
                        "oh1_fm": r6[c]["oh1_fm"], "oh2_fm": r6[c]["oh2_fm"],
                        "ohE_in": np.ascontiguousarray(ohE),
                        "gE_in": np.ascontiguousarray(gE)})
    r7 = run_prog(build_wm1, wm_maps)

    at_maps = []
    for c in range(NCORE):
        h = c % 2
        if h == 1:
            part = x1_fm[c - 1]
            xprev = np.ascontiguousarray(
                np.concatenate([part[:, 3072:4096], part[:, 0:4]], axis=1))
        else:
            xprev = np.zeros((D, 1028), np.float32)
        at_maps.append({
            "xn1_bf": r6[c]["xn1_bf"], "x1_fm": x1_fm[c], "xprev_fm": xprev,
            "Wq": f("L1_attn_Wq"), "Wk": f("L1_attn_Wk"), "Wv": f("L1_attn_Wv"),
            "Wo": f("L1_attn_Wo"),
            "hd3T": np.ascontiguousarray(f("L1_attn_head_dirs").T),
            "hscc": np.ascontiguousarray(f("L1_attn_head_scales").reshape(1, 8)),
            **_attn1_host_consts(c),
        })
    r8 = run_prog(build_attn1, at_maps)

    moe1_common = {
        "W1T": f("L1_ffn_W1"), "W2T": f("L1_ffn_W2"),
        "b1e": f("L1_ffn_b1"), "b2e": f("L1_ffn_b2"),
    }
    bld = functools.partial(build_moe, "moe1a", 0, 4, 1)
    bld.__name__ = "moe1a"
    r9 = run_prog(bld, [{"xhat_bf": r6[c]["xn1_bf"], "acc_in": r8[c]["x2a_fm"],
                         "wm_fm": r7[c]["wm_fm"], **moe1_common} for c in range(NCORE)])
    bld = functools.partial(build_moe, "moe1b", 4, 8, 1)
    bld.__name__ = "moe1b"
    r10 = run_prog(bld, [{"xhat_bf": r6[c]["xn1_bf"], "acc_in": r9[c]["acc_out"],
                          "wm_fm": r7[c]["wm_fm"], **moe1_common} for c in range(NCORE)])

    out = np.zeros((4, 8192, D), np.float32)
    for c in range(NCORE):
        out[c // 2, (c % 2) * NT:(c % 2 + 1) * NT, :] = r10[c]["acc_out"].T
    return out



def build_scan2(p=None, rename=None):
    standalone = p is None
    if standalone:
        p = Prog("scan2")
    p.push(rename)
    p.begin_phase("sc_")
    NH = 16
    C = 128
    NCH = 32
    qs = p.din("q_f", (D, NT), BF16)
    ks = p.din("kn_f", (D, NT), BF16)
    vs = p.din("v_f", (D, NT), BF16)
    bt = p.din("beta_tok", (NT, NH), F32)
    oddm = p.din("oddmask", (64, 1), F32)
    o_out = p.dout("o_fm", (D, NT), BF16)
    qt_scr = p.dint("qt_scr", (64, NCH * NH * C), BF16)
    rt_scr = p.dint("rt_scr", (64, NCH * NH * 64), F32)
    zfin = p.dint("zfin", (NH * 64, 64), F32)
    zpair = p.dint("zpair", (2 * NH * 64, 64), F32)

    tri = np.tril(np.ones((128, 128), np.float32), -1).T
    trii = np.triu(np.ones((128, 128), np.float32))
    c_mu = p.const("c_maskU", tri)
    c_mi = p.const("c_maskLi", trii)
    c_idb = p.const("c_idb", np.eye(128, dtype=BF16NP))

    maskU = p.sb("maskU", (128, 128), F32)
    maskLi = p.sb("maskLi", (128, 128), F32)
    id_bf = p.sb("id_bf", (128, 128), BF16)
    kq = [p.sb(f"kq{j}", (64, NH * C), BF16) for j in range(2)]
    kk = [p.sb(f"kk{j}", (64, NH * C), BF16) for j in range(2)]
    kv = [p.sb(f"kv{j}", (64, NH * C), BF16) for j in range(2)]
    bcol = [p.sb(f"bcol{j}", (128, NH), F32) for j in range(2)]
    aT = p.sb("aT", (128, 7 * 128), BF16)
    aN = p.sb("aN", (128, 7 * 128), BF16)
    xw = [p.sb(f"xw{j}", (128, 128), BF16) for j in range(2)]
    kr_t = p.sb("kr_t", (128, 64), BF16)
    w_t = p.sb("w_t", (128, 64), BF16)
    u_t = p.sb("u_t", (128, 64), BF16)
    gt = p.sb("gt", (128, 128), BF16)
    qt_all = p.sb("qt_all", (64, NH * C), BF16)
    rt_all = p.sb("rt_all", (64, NH * 64), F32)
    pm_all = p.sb("pm_all", (64, NH * 64), F32)
    zc = [p.sb(f"z{j}", (64, NH * 64), F32) for j in range(2)]
    z_bf = p.sb("z_bf", (64, 64), BF16)
    ztmp = p.sb("ztmp", (64, 64), F32)
    o_intra = p.sb("o_intra", (128, NH * 64), F32)
    o_tok = p.sb("o_tok", (128, 128), BF16)
    o_stash = p.sb("o_stash", (128, 8 * 128), BF16)
    dmask = p.sb("dmask", (64, 1), F32)

    psA = p.ps("psA", (128, 512))
    psB = p.ps("psB", (128, 512))
    psC = p.ps("psC", (128, 512))
    psT = p.ps("psT", (128, 1024), BF16)
    psz = [p.ps(f"psz{j}", (128, 512)) for j in range(2)]

    ld = p.sem("ld")
    lin = [p.sem("lin0"), p.sem("lin1")]
    pe = p.sem("pe")
    dve = p.stream_sem("vector")
    act = p.stream_sem("scalar")
    gp = p.stream_sem("gpsimd")
    sto = p.sem("sto")
    sstash = p.sem("sstash")
    cc = p.sem("cc")

    p.dma("sync", maskU[:], c_mu, ld, gate=False)
    p.dma("sync", maskLi[:], c_mi, ld, gate=False)
    p.dma("sync", id_bf[:], c_idb, ld, gate=False)
    p.dma("sync", dmask[:], oddm, ld, gate=False)
    v_ld = ld.n
    p.op("gpsimd", lambda e: e.memset(zc[0][:], 0.0), sem=gp)
    v_gp = gp.n
    p.wait("tensor", ld, v_ld)
    p.wait("vector", ld, v_ld)
    p.wait("vector", gp, v_gp)
    p.wait("scalar", ld, v_ld)

    comp_done = [0] * NCH

    for i in range(NCH):
        bb = i % 2
        if i >= 2:
            p.wait("sync", pe, comp_done[i - 2])
        p.dma("sync", kq[bb][:].rearrange("d (h t) -> d h t", h=NH),
              qs[:, i * C:(i + 1) * C].rearrange("(h d) t -> d h t", d=64),
              lin[bb], gate=False)
        p.dma("sync", kk[bb][:].rearrange("d (h t) -> d h t", h=NH),
              ks[:, i * C:(i + 1) * C].rearrange("(h d) t -> d h t", d=64),
              lin[bb], gate=False)
        p.dma("sync", kv[bb][:].rearrange("d (h t) -> d h t", h=NH),
              vs[:, i * C:(i + 1) * C].rearrange("(h d) t -> d h t", d=64),
              lin[bb], gate=False)
        vb = p.dma("sync", bcol[bb][:], bt[i * C:(i + 1) * C, :], lin[bb], gate=False)
        v_in = vb
        p.wait("tensor", lin[bb], v_in)
        p.wait("vector", lin[bb], v_in)
        p.wait("vector", sstash, sstash.n)
        p.wait("scalar", sstash, sstash.n)

        zcur = zc[i % 2]
        znxt = zc[(i + 1) % 2]

        for h in range(NH):
            q_t = kq[bb][:, h * C:(h + 1) * C]
            k_t = kk[bb][:, h * C:(h + 1) * C]
            v_t = kv[bb][:, h * C:(h + 1) * C]
            b_c = bcol[bb][:, h:h + 1]
            p.op("tensor", lambda e, k_t=k_t: e.matmul(
                psA[:, 0:128], k_t, k_t, start=True, stop=True), sem=pe)
            v_smat = pe.n
            p.wait("vector", pe, v_smat)
            v_atT = p.op("vector", lambda e, b_c=b_c: e.scalar_tensor_tensor(
                aT[:, 0:128], maskU[:], b_c, psA[:, 0:128], ALU.mult, ALU.mult),
                sem=dve)
            p.wait("tensor", dve, v_atT)
            p.op("tensor", lambda e: e.transpose(psT[:, 0:128], aT[:, 0:128], id_bf[:]),
                 sem=pe)
            v_trA = pe.n
            p.wait("scalar", pe, v_trA)
            v_a0 = p.op("scalar", lambda e: e.copy(aN[:, 0:128], psT[:, 0:128]), sem=act)
            prev_evac = ("scalar", v_a0)
            for l in range(6):
                p.wait("tensor", act if prev_evac[0] == "scalar" else dve, prev_evac[1])
                p.op("tensor", lambda e, l=l: e.matmul(
                    psB[:, 0:128], aT[:, l * 128:(l + 1) * 128],
                    aN[:, l * 128:(l + 1) * 128], start=True, stop=True))
                p.op("tensor", lambda e, l=l: e.matmul(
                    psC[:, 0:128], aN[:, l * 128:(l + 1) * 128],
                    aT[:, l * 128:(l + 1) * 128], start=True, stop=True), sem=pe)
                v_sq = pe.n
                p.wait("vector", pe, v_sq)
                v_e1 = p.op("vector", lambda e, l=l: e.tensor_copy(
                    aN[:, (l + 1) * 128:(l + 2) * 128], psB[:, 0:128]), sem=dve)
                p.wait("scalar", pe, v_sq)
                v_e2 = p.op("scalar", lambda e, l=l: e.copy(
                    aT[:, (l + 1) * 128:(l + 2) * 128], psC[:, 0:128]), sem=act)
                prev_evac = ("scalar", v_e2)
                p.wait("tensor", dve, v_e1)
            p.wait("tensor", act, prev_evac[1])
            p.op("tensor", lambda e, k_t=k_t: e.transpose(
                psT[:, 0:64], k_t, id_bf[0:64, 0:64]), sem=pe)
            v_tk = pe.n
            p.wait("vector", pe, v_tk)
            v_kr = p.op("vector", lambda e: e.tensor_copy(kr_t[:, 0:64], psT[:, 0:64]),
                        sem=dve)
            v_x0a = p.op("vector", lambda e: e.tensor_copy(xw[0][:, 0:64], psT[:, 0:64]),
                         sem=dve)
            p.wait("tensor", dve, v_x0a)
            p.op("tensor", lambda e, v_t=v_t: e.transpose(
                psT[:, 0:64], v_t, id_bf[0:64, 0:64]), sem=pe)
            v_tv = pe.n
            p.wait("vector", pe, v_tv)
            v_x0b = p.op("vector", lambda e: e.tensor_copy(xw[0][:, 64:128], psT[:, 0:64]),
                         sem=dve)
            p.wait("tensor", dve, v_x0b)
            xi = 0
            for ai, sign in ((6, 1), (5, 1), (4, 1), (3, 1), (2, 1), (1, 1), (0, -1)):
                p.op("tensor", lambda e, ai=ai, xi=xi: e.matmul(
                    psB[:, 0:128], aT[:, ai * 128:(ai + 1) * 128], xw[xi][:],
                    start=True, stop=True), sem=pe)
                v_ap = pe.n
                p.wait("vector", pe, v_ap)
                if sign > 0:
                    v_x = p.op("vector", lambda e, xi=xi: e.tensor_add(
                        xw[1 - xi][:], xw[xi][:], psB[:, 0:128]), sem=dve)
                else:
                    v_x = p.op("vector", lambda e, xi=xi: e.tensor_sub(
                        xw[1 - xi][:], xw[xi][:], psB[:, 0:128]), sem=dve)
                p.wait("tensor", dve, v_x)
                xi = 1 - xi
            v_w = p.op("vector", lambda e, xi=xi, b_c=b_c: e.tensor_scalar_mul(
                w_t[:, 0:64], xw[xi][:, 0:64], b_c), sem=dve)
            v_u = p.op("vector", lambda e, xi=xi, b_c=b_c: e.tensor_scalar_mul(
                u_t[:, 0:64], xw[xi][:, 64:128], b_c), sem=dve)
            p.wait("tensor", dve, v_u)
            p.op("tensor", lambda e, k_t=k_t, q_t=q_t: e.matmul(
                psA[:, 0:128], k_t, q_t, start=True, stop=True), sem=pe)
            v_gm = pe.n
            p.wait("vector", pe, v_gm)
            v_gt = p.op("vector", lambda e: e.tensor_mul(
                gt[:], psA[:, 0:128], maskLi[:]), sem=dve)
            p.wait("tensor", dve, v_gt)
            p.op("tensor", lambda e, h=h: e.matmul(
                psA[:, 0:64], gt[:], u_t[:, 0:64], start=True, stop=True), sem=pe)
            v_oi = pe.n
            p.wait("scalar", pe, v_oi)
            v_oie = p.op("scalar", lambda e, h=h: e.copy(
                o_intra[:, h * 64:(h + 1) * 64], psA[:, 0:64]), sem=act)
            p.wait("tensor", act, v_oie)
            p.op("tensor", lambda e: e.matmul(
                psB[0:64, 0:128], w_t[:, 0:64], gt[:], start=True, stop=True), sem=pe)
            v_qm = pe.n
            p.wait("vector", pe, v_qm)
            v_qt = p.op("vector", lambda e, h=h, q_t=q_t: e.tensor_sub(
                qt_all[:, h * C:(h + 1) * C], q_t, psB[0:64, 0:128]), sem=dve)
            p.op("tensor", lambda e: e.matmul(
                psA[0:64, 0:64], w_t[:, 0:64], kr_t[:, 0:64], start=True, stop=True),
                sem=pe)
            v_rt = pe.n
            p.wait("tensor", dve, v_qt)
            p.op("tensor", lambda e: e.matmul(
                psB[0:64, 0:64], kr_t[:, 0:64], u_t[:, 0:64], start=True, stop=True),
                sem=pe)
            v_pm = pe.n
            p.wait("scalar", pe, v_rt)
            v_rte = p.op("scalar", lambda e, h=h: e.copy(
                rt_all[:, h * 64:(h + 1) * 64], psA[0:64, 0:64]), sem=act)
            p.wait("scalar", pe, v_pm)
            v_pme = p.op("scalar", lambda e, h=h: e.copy(
                pm_all[:, h * 64:(h + 1) * 64], psB[0:64, 0:64]), sem=act)
            p.wait("tensor", act, v_pme)

        p.wait("sync", dve, dve.n)
        p.wait("sync", act, act.n)
        p.dma("sync", qt_scr[:, i * NH * C:(i + 1) * NH * C], qt_all[:], sstash,
              gate=True)
        p.dma("sync", rt_scr[:, i * NH * 64:(i + 1) * NH * 64], rt_all[:], sstash,
              gate=False)

        if i > 0:
            p.wait("vector", sto, sto.n)
        corr_free = [None, None]
        for h in range(NH):
            zslc = zcur[:, h * 64:(h + 1) * 64]
            bank = psz[h % 2]
            cbank = psA if h % 2 == 0 else psB
            v_zb = p.op("vector", lambda e, zslc=zslc: e.tensor_copy(
                z_bf[:, 0:64], zslc), sem=dve)
            p.wait("tensor", dve, v_zb)
            if corr_free[h % 2] is not None:
                p.wait("tensor", dve, corr_free[h % 2])
            p.op("tensor", lambda e, h=h, cbank=cbank: e.matmul(
                cbank[:, 0:64], qt_all[:, h * C:(h + 1) * C],
                z_bf[:, 0:64], start=True, stop=True), sem=pe)
            v_corr = pe.n
            p.op("tensor", lambda e, h=h, zslc=zslc, bank=bank: e.matmul(
                bank[0:64, 0:64], rt_all[:, h * 64:(h + 1) * 64], zslc,
                start=True, stop=True), sem=pe)
            v_zm = pe.n
            p.wait("vector", pe, v_corr)
            v_oh = p.op("vector", lambda e, h=h, cbank=cbank: e.tensor_add(
                o_tok[:, (h % 2) * 64:(h % 2) * 64 + 64],
                cbank[:, 0:64], o_intra[:, h * 64:(h + 1) * 64]), sem=dve)
            corr_free[h % 2] = v_oh
            if h % 2 == 1:
                p.wait("tensor", dve, v_oh)
                p.op("tensor", lambda e: e.transpose(
                    psT[:, 0:128], o_tok[:], id_bf[:]), sem=pe)
                v_otr = pe.n
                p.wait("vector", pe, v_otr)
                v_stash = p.op("vector", lambda e, h=h: e.tensor_copy(
                    o_stash[:, (h // 2) * 128:(h // 2 + 1) * 128],
                    psT[:, 0:128]), sem=dve)
                p.wait("tensor", dve, v_stash)
            p.wait("vector", pe, v_zm)
            p.op("vector", lambda e, zslc=zslc, bank=bank: e.scalar_tensor_tensor(
                ztmp[:, 0:64], zslc, 1.0, bank[0:64, 0:64], ALU.mult, ALU.subtract),
                sem=dve)
            v_zn = p.op("vector", lambda e, h=h, znxt=znxt: e.tensor_add(
                znxt[:, h * 64:(h + 1) * 64], ztmp[:, 0:64],
                pm_all[:, h * 64:(h + 1) * 64]), sem=dve)
            p.wait("tensor", dve, v_zn)
        v_oev = dve.n
        p.wait("sync", dve, v_oev)
        p.dma("sync", o_out[:, i * C:(i + 1) * C].rearrange("(hp d) t -> d hp t", d=128),
              o_stash[:].rearrange("d (hp t) -> d hp t", hp=8), sto)
        p.wait("tensor", dve, v_oev)
        comp_done[i] = pe.n

    zlast = zc[NCH % 2]
    p.wait("sync", dve, dve.n)
    p.dma("sync", zfin.rearrange("(h d) e -> d h e", d=64),
          zlast[:].rearrange("d (h e) -> d h e", h=NH), sstash, gate=True)
    p.wait("gpsimd", sstash, sstash.n)
    v_cc = cc.inc(1)
    p.on("gpsimd", lambda e: e.collective_compute(
        "AllGather", ALU.bypass,
        [[0, 1], [2, 3], [4, 5], [6, 7]],
        [zfin.opt()], [zpair.opt()]).then_inc(cc.h, 1))

    dcur = p.sb("dcur", (64, NH * 64), F32)
    dnxt = p.sb("dnxt", (64, NH * 64), F32)
    d_bf = p.sb("d_bf", (64, NH * 64), BF16)
    qt_l = p.sb("qt_l", (64, NH * C), BF16)
    rt_l = p.sb("rt_l", (64, NH * 64), F32)
    o_l = p.sb("o_l", (128, 8 * 128), BF16)
    lc = p.sem("lc")
    lo = p.sem("lo")
    p.wait("sync", cc, v_cc)
    v_zp = p.dma("sync", dcur[:].rearrange("d (h e) -> d h e", h=NH),
                 zpair[0:NH * 64, :].rearrange("(h d) e -> d h e", d=64), lc, gate=True)
    p.wait("vector", lc, v_zp)
    p.wait("vector", ld, v_ld)
    v_d0 = p.op("vector", lambda e: e.tensor_scalar_mul(
        dcur[:], dcur[:], dmask[:]), sem=dve)
    dd = [dcur, dnxt]
    for i in range(NCH):
        dc_, dn_ = dd[i % 2], dd[(i + 1) % 2]
        p.wait("sync", pe, pe.n)
        vq = p.dma("sync", qt_l[:], qt_scr[:, i * NH * C:(i + 1) * NH * C], lc, gate=True)
        vr = p.dma("sync", rt_l[:], rt_scr[:, i * NH * 64:(i + 1) * NH * 64], lc,
                   gate=False)
        p.wait("sync", sto, sto.n)
        p.wait("sync", dve, dve.n)
        vo = p.dma("sync", o_l[:].rearrange("d (hp t) -> d hp t", hp=8),
                   o_out[:, i * C:(i + 1) * C].rearrange("(hp d) t -> d hp t", d=128),
                   lo, gate=True)
        p.wait("vector", dve, dve.n)
        v_dbf = p.op("vector", lambda e, dc_=dc_: e.tensor_copy(d_bf[:], dc_[:]), sem=dve)
        p.wait("tensor", lc, vr)
        p.wait("tensor", dve, v_dbf)
        p.wait("vector", lo, vo)
        for hp in range(8):
            for sub in range(2):
                h = 2 * hp + sub
                bank = psz[sub]
                p.op("tensor", lambda e, h=h, sub=sub: e.matmul(
                    psA[:, 0:64] if sub == 0 else psB[:, 0:64],
                    qt_l[:, h * C:(h + 1) * C],
                    d_bf[:, h * 64:(h + 1) * 64], start=True, stop=True), sem=pe)
                v_do = pe.n
                p.op("tensor", lambda e, h=h, bank=bank, dc_=dc_: e.matmul(
                    bank[0:64, 0:64], rt_l[:, h * 64:(h + 1) * 64],
                    dc_[:, h * 64:(h + 1) * 64], start=True, stop=True), sem=pe)
                v_dz = pe.n
                p.wait("vector", pe, v_do)
                p.op("vector", lambda e, sub=sub: e.tensor_copy(
                    o_tok[:, sub * 64:sub * 64 + 64],
                    psA[:, 0:64] if sub == 0 else psB[:, 0:64]), sem=dve)
                p.wait("vector", pe, v_dz)
                v_dn = p.op("vector", lambda e, h=h, bank=bank, dc_=dc_, dn_=dn_:
                            e.scalar_tensor_tensor(
                    dn_[:, h * 64:(h + 1) * 64], dc_[:, h * 64:(h + 1) * 64], 1.0,
                    bank[0:64, 0:64], ALU.mult, ALU.subtract), sem=dve)
                p.wait("tensor", dve, v_dn)
            p.op("tensor", lambda e: e.transpose(psT[:, 0:128], o_tok[:], id_bf[:]),
                 sem=pe)
            v_tr = pe.n
            p.wait("vector", pe, v_tr)
            v_oadd = p.op("vector", lambda e, hp=hp: e.tensor_add(
                o_l[:, hp * 128:(hp + 1) * 128], o_l[:, hp * 128:(hp + 1) * 128],
                psT[:, 0:128]), sem=dve)
            p.wait("tensor", dve, v_oadd)
        v_corrd = dve.n
        p.wait("sync", dve, v_corrd)
        p.dma("sync", o_out[:, i * C:(i + 1) * C].rearrange("(hp d) t -> d hp t", d=128),
              o_l[:].rearrange("d (hp t) -> d hp t", hp=8), sto, gate=True)
    p.wait("sync", sto, sto.n)
    p.finish(sto, sto.n)
    p.end_phase()
    p.pop()
    return p



def build_fused():
    p = Prog("fused", fused=True, final_outputs=("out_fm",))
    build_ph0(p=p)
    build_qkv(p=p)
    build_scan2(p=p)
    build_attnout(p=p)
    build_x1init(p=p)
    build_moe("moe0", 0, 8, 0, p=p,
              rename={"acc_in": "x1acc", "acc_out": "x1_fm", "wm_fm": "w0_fm"})
    build_ph1(p=p)

    p.push(None)
    p.begin_phase("cc2_")
    counts = p.registry["counts"]
    cnt_ag = p.dint("cnt_ag", (64, 1), F32)
    xbnd_src = p.dint("xbnd_src", (D, 1028), F32)
    xbnd_pair = p.dint("xbnd_pair", (2 * D, 1028), F32)
    x1 = p.registry["x1_fm"]
    bs = p.sem("bs")
    cc = p.sem("cc")
    p.dma("sync", xbnd_src[:, 0:1024], x1[:, 3072:4096], bs, gate=True)
    p.dma("sync", xbnd_src[:, 1024:1028], x1[:, 0:4], bs, gate=False)
    p.wait("gpsimd", bs, bs.n)
    v1 = cc.inc(1)
    p.on("gpsimd", lambda e: e.collective_compute(
        "AllGather", ALU.bypass, [[0, 1], [2, 3], [4, 5], [6, 7]],
        [xbnd_src.opt()], [xbnd_pair.opt()]).then_inc(cc.h, 1))
    v2 = cc.inc(1)
    p.on("gpsimd", lambda e: e.collective_compute(
        "AllGather", ALU.bypass, [[0, 1, 2, 3, 4, 5, 6, 7]],
        [counts.opt()], [cnt_ag.opt()]).then_inc(cc.h, 1))
    p.end_phase()
    p.pop()
    p.registry["counts_all"] = cnt_ag.rearrange("(c e) one -> c (e one)", e=8)
    p.registry["xprev_fm"] = xbnd_pair[0:D, :]

    build_wm1(p=p, rename={"ohE_in": "ohE_out", "gE_in": "gE_out"})
    build_attn1(p=p)
    build_moe("moe1", 0, 8, 1, p=p,
              rename={"xhat_bf": "xn1_bf", "acc_in": "x2a_fm", "acc_out": "out_fm",
                      "wm_fm": "wm_fm", "W1T": "W1L1", "W2T": "W2L1",
                      "b1e": "b1L1", "b2e": "b2L1"})
    return p


def kernel(**inputs):
    def f(k):
        return np.ascontiguousarray(np.asarray(inputs[k], np.float32))

    x = f("L0_attn_x")
    common = {
        "cenT": np.ascontiguousarray(f("L0_ffn_centroids").T),
        "ln1c": _col(inputs["L0_attn_ln1_scale"]),
        "wq": f("L0_attn_wq"), "wk": f("L0_attn_wk"), "wv": f("L0_attn_wv"),
        "wbeta": f("L0_attn_wbeta"), "bq_c": _col(inputs["L0_attn_bq"]),
        "bk_c": _col(inputs["L0_attn_bk"]), "bv_c": _col(inputs["L0_attn_bv"]),
        "bbeta_c": _col(inputs["L0_attn_bbeta"]),
        "wo": f("L0_attn_wo"), "w1": f("L0_attn_w1"), "w2": f("L0_attn_w2"),
        "bo_c": _col(inputs["L0_attn_bo"]), "ln2_c": _col(inputs["L0_attn_ln2_scale"]),
        "b1_c": _col(inputs["L0_attn_b1"]), "b2_c": _col(inputs["L0_attn_b2"]),
        "W1T": np.ascontiguousarray(f("L0_ffn_W1").transpose(0, 2, 1)),
        "W2T": np.ascontiguousarray(f("L0_ffn_W2").transpose(0, 2, 1)),
        "b1e": f("L0_ffn_b1"), "b2e": f("L0_ffn_b2"),
        "gammae": f("L0_ffn_gamma"), "betae": f("L0_ffn_beta"),
        "Wg": f("L1_ffn_Wg"),
        "Wq": f("L1_attn_Wq"), "Wk": f("L1_attn_Wk"), "Wv": f("L1_attn_Wv"),
        "Wo": f("L1_attn_Wo"),
        "hd3T": np.ascontiguousarray(f("L1_attn_head_dirs").T),
        "hscc": np.ascontiguousarray(f("L1_attn_head_scales").reshape(1, 8)),
        "W1L1": f("L1_ffn_W1"), "W2L1": f("L1_ffn_W2"),
        "b1L1": f("L1_ffn_b1"), "b2L1": f("L1_ffn_b2"),
    }
    in_maps = []
    for c in range(NCORE):
        b, h = c // 2, c % 2
        sel = np.zeros((8, 1), np.float32)
        sel[c, 0] = 1.0
        m = {
            "x_fm": np.ascontiguousarray(x[b, h * NT:(h + 1) * NT, :].T),
            "oddmask": np.full((64, 1), float(h), np.float32),
            "selcol": sel,
            **_attn1_host_consts(c),
            **common,
        }
        in_maps.append(m)
    res = run_prog(build_fused, in_maps)
    out = np.zeros((4, 8192, D), np.float32)
    for c in range(NCORE):
        out[c // 2, (c % 2) * NT:(c % 2 + 1) * NT, :] = res[c]["out_fm"].T
    return out
